# version 05
# baseline (speedup 1.0000x reference)
import contextlib
import numpy as np
import ml_dtypes
import concourse.bass as bass
import concourse.mybir as mybir
from concourse.bass_utils import run_bass_kernel_spmd

F32 = mybir.dt.float32
BF16 = mybir.dt.bfloat16
FP8 = mybir.dt.float8e4
I32 = mybir.dt.int32
U8 = mybir.dt.uint8
AF = mybir.ActivationFunctionType
ALU = mybir.AluOpType
AX = mybir.AxisListType

NCORES = 8
D = 2048
SEQ = 2048
TPC = 1024
NF = D // 128
NT = TPC // 128
EPS = 1e-5
FP8MAX = 448.0
INTER1 = 5504
NI1 = INTER1 // 128
FFI = 8192
NFI = FFI // 128
SM_SCALE0 = float(np.float32(1.0) / np.sqrt(np.float32(128.0)))
SM_SCALE1 = float(np.float32(1.0) / np.sqrt(np.float32(2048.0)))

ENGINES = ("sync", "scalar", "vector", "tensor", "gpsimd")
GROUP8 = [[0, 1, 2, 3, 4, 5, 6, 7]]
PAIRS = [[0, 1], [2, 3], [4, 5], [6, 7]]


class Phaser:

    def __init__(self, nc, ctx):
        self.nc = nc
        self.ctx = ctx
        self.block = ctx.enter_context(nc.Block())
        self.bar = ctx.enter_context(nc.semaphore("bar0"))
        self.cum = 0
        self.epoch = 0
        self.fences = []
        self.engine_epoch = {e: 0 for e in ENGINES}

    def _catchup(self, eng, ename):
        while self.engine_epoch[ename] < self.epoch:
            s, tot = self.fences[self.engine_epoch[ename]]
            eng.wait_ge(s, tot)
            self.engine_epoch[ename] += 1

    def phase(self, **bodies):
        for ename, (emit, n_decl) in bodies.items():
            def body(eng, emit=emit, n_decl=n_decl, ename=ename):
                self._catchup(eng, ename)
                if self.cum > 0:
                    eng.wait_ge(self.bar, self.cum)
                got = [0]

                def inc(instr, amt=1):
                    instr.then_inc(self.bar, amt)
                    got[0] += amt
                    return instr

                emit(eng, inc)
                assert got[0] == n_decl, (
                    f"engine {ename}: declared {n_decl} incs, got {got[0]}")
            getattr(self.block, ename)(body)
        self.cum += sum(n for (_, n) in bodies.values())
        if self.cum > 40000:
            self.fences.append((self.bar, self.cum))
            self.bar = self.ctx.enter_context(
                self.nc.semaphore(f"bar{len(self.fences)}"))
            self.cum = 0
            self.epoch += 1

    def finish(self):
        def body(eng):
            self._catchup(eng, "gpsimd")
            if self.cum > 0:
                eng.wait_ge(self.bar, self.cum)
        self.block.gpsimd(body)


def _dma_tiles(eng, inc, sb, dram, rows, cols, to_sbuf=True, row0=0, col0=0):
    for k in range(rows // 128):
        s_ap = sb[:, col0 + cols * k: col0 + cols * (k + 1)]
        d_ap = dram[row0 + 128 * k: row0 + 128 * (k + 1), :]
        if to_sbuf:
            inc(eng.dma_start(out=s_ap, in_=d_ap), 16)
        else:
            inc(eng.dma_start(out=d_ap, in_=s_ap), 16)


def coll_phase(ph, kind, op, groups, pairs_io):
    def body(eng, inc):
        for (i_ap, o_ap) in pairs_io:
            inc(eng.collective_compute(kind, op, replica_groups=groups,
                                       ins=[i_ap.opt()], outs=[o_ap.opt()]), 1)
    ph.phase(gpsimd=(body, len(pairs_io)))


def pipe(ph, n, stages):
    maxd = max(d for (_, d, _, _) in stages)
    for t in range(n + maxd):
        todo = {}
        for (ename, d, fn, k) in stages:
            i = t - d
            if 0 <= i < n:
                kv = k(i) if callable(k) else k
                if kv:
                    todo.setdefault(ename, []).append((fn, i, kv))
        bodies = {}
        for ename, items in todo.items():
            def emit(eng, inc, items=items):
                for (fn, i, kv) in items:
                    fn(i, eng, inc)
            bodies[ename] = (emit, sum(kv for (_, _, kv) in items))
        if bodies:
            ph.phase(**bodies)


def pipe_tm2fm(ph, src_dram, sXnF, sxt, sxb, ptr2, sidB):
    def L(k, eng, inc):
        inc(eng.dma_start(out=sxt[:, D * (k % 2):D * (k % 2) + D],
                          in_=src_dram[128 * k:128 * (k + 1), :]), 16)

    def C(k, eng, inc):
        inc(eng.activation(out=sxb[:, D * (k % 2):D * (k % 2) + D],
                           in_=sxt[:, D * (k % 2):D * (k % 2) + D], func=AF.Copy))

    def Tr(k, eng, inc):
        h = 2048 * (k % 2)
        for j in range(NF):
            inc(eng.transpose(ptr2[:, h + 128 * j:h + 128 * (j + 1)],
                              sxb[:, D * (k % 2) + 128 * j:D * (k % 2) + 128 * (j + 1)],
                              sidB[:]))

    def Ev(k, eng, inc):
        h = 2048 * (k % 2)
        for j in range(NF):
            inc(eng.tensor_copy(
                out=sXnF[:, TPC * j + 128 * k:TPC * j + 128 * (k + 1)],
                in_=ptr2[:, h + 128 * j:h + 128 * (j + 1)]))

    pipe(ph, NT, [("sync", 0, L, 16), ("scalar", 1, C, 1),
                  ("tensor", 2, Tr, NF), ("vector", 3, Ev, NF)])



def stage_ln(nc, ph, io, with_maxes, tag):
    x, xn, identF = io["x"], io["xn"], io["identF"]
    with contextlib.ExitStack() as ctx:
        T = lambda name, shape, dt: ctx.enter_context(
            nc.sbuf_tensor(f"{tag}_{name}", shape, dt))
        sX = T("sX", [128, NT * D], F32)
        sXn = T("sXn", [128, NT * D], F32)
        sjunk = T("sjunk", [128, D], F32)
        sid = T("sid", [128, 128], F32)
        sms = T("sms", [128, NT], F32)
        sqs = T("sqs", [128, NT], F32)
        smean = T("smean", [128, NT], F32)
        stmp = T("stmp", [128, NT], F32)
        svar = T("svar", [128, NT], F32)
        srstd = T("srstd", [128, NT], F32)
        sbias = T("sbias", [128, NT], F32)
        sxm = T("sxm", [128, NT], F32)
        swm = T("swm", [128, 16], F32)
        scols = T("scols", [128, 3], F32)
        sred = T("sred", [3, 128], F32)
        sout3 = T("sout3", [3, 1], F32)
        srow3 = T("srow3", [1, 3], F32)
        seps = T("seps", [128, 1], F32)
        if with_maxes:
            swA = T("swA", [128, FFI], F32)
        pt3 = ctx.enter_context(nc.psum_tensor(f"{tag}_pt3", [3, 128], F32))

        def p0(eng, inc):
            inc(eng.dma_start(out=sid[:], in_=identF), 16)
            _dma_tiles(eng, inc, sX, x, TPC, D)
        ph.phase(sync=(p0, 16 * (1 + NT)))

        def p1s(eng, inc):
            for k in range(NT):
                inc(eng.activation(out=sjunk[:], in_=sX[:, D * k:D * (k + 1)],
                                   func=AF.Square, accum_out=sqs[:, k:k + 1]))

        def p1v(eng, inc):
            eng.memset(seps[:], EPS)
            for k in range(NT):
                inc(eng.tensor_reduce(out=sms[:, k:k + 1],
                                      in_=sX[:, D * k:D * (k + 1)],
                                      axis=AX.X, op=ALU.add))
        ph.phase(scalar=(p1s, NT), vector=(p1v, NT))

        if with_maxes:
            wjobs = [(io["wfc_sh"], 0, FFI), (io["wfc_sh"], 128, FFI)] + \
                    [(io["wproj_sh"], 128 * j, D) for j in range(8)]
            for wi, (wt, r0, cw) in enumerate(wjobs):
                def wl(eng, inc, wt=wt, r0=r0, cw=cw):
                    inc(eng.dma_start(out=swA[:, 0:cw], in_=wt[r0:r0 + 128, :]), 16)
                ph.phase(sync=(wl, 16))

                def wr(eng, inc, wi=wi, cw=cw):
                    inc(eng.tensor_reduce(out=swm[:, wi:wi + 1], in_=swA[:, 0:cw],
                                          axis=AX.X, op=ALU.max,
                                          apply_absolute_value=True))
                ph.phase(vector=(wr, 1))

        def p2v(eng, inc):
            inc(eng.tensor_scalar(out=smean[:], in0=sms[:], scalar1=1.0 / D,
                                  scalar2=None, op0=ALU.mult))
        ph.phase(vector=(p2v, 1))

        def p2b(eng, inc):
            inc(eng.activation(out=stmp[:], in_=smean[:], func=AF.Square))
        ph.phase(scalar=(p2b, 1))

        def p2c(eng, inc):
            inc(eng.scalar_tensor_tensor(out=svar[:], in0=sqs[:], scalar=1.0 / D,
                                         in1=stmp[:], op0=ALU.mult,
                                         op1=ALU.subtract))
        ph.phase(vector=(p2c, 1))

        def p3s(eng, inc):
            inc(eng.activation(out=srstd[:], in_=svar[:], func=AF.Sqrt,
                               bias=seps[:, 0:1]))
        ph.phase(scalar=(p3s, 1))

        def p4a(eng, inc):
            inc(eng.reciprocal(out=srstd[:], in_=srstd[:]))
        ph.phase(vector=(p4a, 1))

        def p4b(eng, inc):
            inc(eng.scalar_tensor_tensor(out=sbias[:], in0=smean[:], scalar=-1.0,
                                         in1=srstd[:], op0=ALU.mult, op1=ALU.mult))
        ph.phase(vector=(p4b, 1))

        def p5(eng, inc):
            for k in range(NT):
                inc(eng.activation(out=sXn[:, D * k:D * (k + 1)],
                                   in_=sX[:, D * k:D * (k + 1)], func=AF.Identity,
                                   bias=sbias[:, k:k + 1], scale=srstd[:, k:k + 1]))
        ph.phase(scalar=(p5, NT))

        def p6(eng, inc):
            for k in range(NT):
                inc(eng.tensor_reduce(out=sxm[:, k:k + 1],
                                      in_=sXn[:, D * k:D * (k + 1)], axis=AX.X,
                                      op=ALU.max, apply_absolute_value=True))
        ph.phase(vector=(p6, NT))

        def p7(eng, inc):
            eng.tensor_reduce(out=scols[:, 0:1], in_=sxm[:, 0:NT], axis=AX.X,
                              op=ALU.max)
            if with_maxes:
                eng.tensor_reduce(out=scols[:, 1:2], in_=swm[:, 0:2], axis=AX.X,
                                  op=ALU.max)
                inc(eng.tensor_reduce(out=scols[:, 2:3], in_=swm[:, 2:10],
                                      axis=AX.X, op=ALU.max))
            else:
                inc(eng.memset(scols[:, 1:3], 0.0))
        ph.phase(vector=(p7, 1))

        def p8(eng, inc):
            inc(eng.transpose(pt3[:], scols[:], sid[:]))
        ph.phase(tensor=(p8, 1))

        def p9(eng, inc):
            inc(eng.tensor_copy(out=sred[:], in_=pt3[:]))
        ph.phase(vector=(p9, 1))

        def p10(eng, inc):
            inc(eng.tensor_reduce(out=sout3[:], in_=sred[:], axis=AX.X, op=ALU.max))
        ph.phase(vector=(p10, 1))

        if with_maxes:
            def p11(eng, inc):
                inc(eng.transpose(pt3[0:1, 0:3], sout3[:], sid[0:3, 0:3]))
            ph.phase(tensor=(p11, 1))

            def p12(eng, inc):
                inc(eng.tensor_copy(out=srow3[:], in_=pt3[0:1, 0:3]))
            ph.phase(vector=(p12, 1))

            def p13(eng, inc):
                inc(eng.dma_start(out=io["maxes"], in_=srow3[:]), 16)
            ph.phase(gpsimd=(p13, 16))

        def pst(eng, inc):
            _dma_tiles(eng, inc, sXn, xn, TPC, D, to_sbuf=False)
        ph.phase(gpsimd=(pst, 16 * NT))



def stage_prep0(nc, ph, io, tag="pp"):
    xn, gmax = io["xn"], io["gmax"]
    wq, wk, wv = io["wq"], io["wk"], io["wv"]
    cosT, sinT = io["cosT"], io["sinT"]
    wfc_sh, wproj_sh = io["wfc_sh"], io["wproj_sh"]
    identB = io["identB"]
    qf, kf, vf, x8f = io["qf"], io["kf"], io["vf"], io["x8f"]
    wfc8t_sh, wproj8_sh = io["wfc8t_sh"], io["wproj8_sh"]

    with contextlib.ExitStack() as ctx:
        T = lambda name, shape, dt: ctx.enter_context(
            nc.sbuf_tensor(f"{tag}_{name}", shape, dt))
        sXnF = T("sXnF", [128, NF * TPC], BF16)
        sxt = T("sxt", [128, 2 * D], F32)
        sxb = T("sxb", [128, 2 * D], BF16)
        str_ = T("str", [128, NF * 128], BF16)
        srow9 = T("srow9", [1, 16], F32)
        sone = T("sone", [1, 128], F32)
        scols9 = T("scols9", [128, 12], F32)
        qf8a = T("qf8a", [128, D], FP8)
        qf8b = T("qf8b", [128, D], FP8)
        qua = T("qua", [128, D], BF16)
        qub = T("qub", [128, D], BF16)
        qmk = T("qmk", [128, D], U8)
        qmk2 = T("qmk2", [128, D], U8)
        qout = T("qout", [128, D], BF16)
        sid = T("sid", [128, 128], BF16)
        swf = T("swf", [128, 4 * D], F32)
        swb = T("swb", [128, 4 * D], BF16)
        scs = T("scs", [128, 2 * TPC], F32)
        ssn = T("ssn", [128, 2 * TPC], F32)
        st1 = T("st1", [128, 512], F32)
        st2 = T("st2", [128, 512], F32)
        sQa = T("sQa", [128, 2 * 512], BF16)
        sQb = T("sQb", [128, 2 * 512], BF16)
        sgm = T("sgm", [1, 3], F32)
        p9psum = ctx.enter_context(nc.psum_tensor(f"{tag}_p9", [128, 9], F32))
        p9 = p9psum[:]

        def p0(eng, inc):
            inc(eng.dma_start(out=sid[:], in_=identB), 16)
            inc(eng.dma_start(out=sgm[:], in_=gmax), 16)
        ph.phase(sync=(p0, 32))

        def p2(eng, inc):
            eng.memset(sone[:], 1.0)
            for j in range(3):
                inc(eng.tensor_scalar(out=srow9[0:1, 12 + j:13 + j],
                                      in0=sgm[0:1, j:j + 1],
                                      scalar1=1.0 / FP8MAX, scalar2=1e-6,
                                      op0=ALU.mult, op1=ALU.max))
        ph.phase(vector=(p2, 3))

        def p2x(eng, inc):
            for j in range(3):
                sc = srow9[0:1, 12 + j:13 + j]
                inc(eng.tensor_scalar(out=srow9[0:1, 3 * j + 2:3 * j + 3], in0=sc,
                                      scalar1=240.0, scalar2=None, op0=ALU.mult))
        ph.phase(vector=(p2x, 3))

        def p3(eng, inc):
            for j in range(3):
                inc(eng.reciprocal(out=srow9[0:1, 3 * j:3 * j + 1],
                                   in_=srow9[0:1, 12 + j:13 + j]))
        ph.phase(vector=(p3, 3))

        def p4(eng, inc):
            for j in range(3):
                inc(eng.tensor_scalar(out=srow9[0:1, 3 * j + 1:3 * j + 2],
                                      in0=srow9[0:1, 3 * j:3 * j + 1],
                                      scalar1=0.5, scalar2=None, op0=ALU.mult))
        ph.phase(vector=(p4, 3))

        def p5(eng, inc):
            inc(eng.matmul(p9, sone[:], srow9[0:1, 0:9], start=True, stop=True))
        ph.phase(tensor=(p5, 1))

        def p6(eng, inc):
            inc(eng.tensor_copy(out=scols9[:, 0:9], in_=p9))
        ph.phase(vector=(p6, 1))

        def p6b(eng, inc):
            for j in range(3):
                inc(eng.tensor_scalar(out=scols9[:, 9 + j:10 + j],
                                      in0=scols9[:, 3 * j + 2:3 * j + 3],
                                      scalar1=-1.0, scalar2=None, op0=ALU.mult))
        ph.phase(vector=(p6b, 3))

        def qc_of(j):
            return {"inv": scols9[:, 3 * j:3 * j + 1],
                    "invh": scols9[:, 3 * j + 1:3 * j + 2],
                    "thr": scols9[:, 3 * j + 2:3 * j + 3],
                    "nthr": scols9[:, 9 + j:10 + j]}

        def quant_phases(src_ap, qi, W=D):
            def pact(eng, inc):
                qc = qc_of(qi)
                eng.activation(out=qf8a[:, 0:W], in_=src_ap, func=AF.Identity,
                               scale=qc["inv"])
                inc(eng.activation(out=qua[:, 0:W], in_=qf8a[:, 0:W], func=AF.Copy))
                eng.activation(out=qf8b[:, 0:W], in_=src_ap, func=AF.Identity,
                               scale=qc["invh"])
                inc(eng.activation(out=qub[:, 0:W], in_=qf8b[:, 0:W], func=AF.Copy,
                                   scale=2.0))
            ph.phase(scalar=(pact, 2))

            def pm1(eng, inc):
                qc = qc_of(qi)
                inc(eng.tensor_scalar(out=qmk[:, 0:W], in0=src_ap,
                                      scalar1=qc["thr"], scalar2=None, op0=ALU.is_ge))
                inc(eng.tensor_scalar(out=qmk2[:, 0:W], in0=src_ap,
                                      scalar1=qc["nthr"], scalar2=None, op0=ALU.is_le))
            ph.phase(vector=(pm1, 2))

            def pm2(eng, inc):
                inc(eng.tensor_tensor(out=qmk[:, 0:W], in0=qmk[:, 0:W],
                                      in1=qmk2[:, 0:W], op=ALU.logical_or))
            ph.phase(vector=(pm2, 1))

            def psel(eng, inc):
                inc(eng.select(out=qout[:, 0:W], mask=qmk[:, 0:W],
                               on_true=qub[:, 0:W], on_false=qua[:, 0:W]))
            ph.phase(vector=(psel, 1))

        with nc.psum_tensor(f"{tag}_ptr2", [128, 4096], BF16) as ptr2:
            pipe_tm2fm(ph, xn, sXnF, sxt, sxb, ptr2, sid)

            for k in range(NT):
                s = k % 2

                def pl(eng, inc, k=k, s=s):
                    inc(eng.dma_start(out=sxt[:, D * s:D * (s + 1)],
                                      in_=xn[128 * k:128 * (k + 1), :]), 16)
                ph.phase(sync=(pl, 16))
                quant_phases(sxt[:, D * s:D * (s + 1)], 0)

                def ptj(eng, inc, k=k):
                    h = 2048 * (k % 2)
                    for j in range(NF):
                        inc(eng.transpose(ptr2[:, h + 128 * j:h + 128 * (j + 1)],
                                          qout[:, 128 * j:128 * (j + 1)], sid[:]))
                ph.phase(tensor=(ptj, NF))

                def pej(eng, inc, k=k):
                    h = 2048 * (k % 2)
                    for j in range(NF):
                        inc(eng.tensor_copy(out=str_[:, 128 * j:128 * (j + 1)],
                                            in_=ptr2[:, h + 128 * j:h + 128 * (j + 1)]))
                ph.phase(vector=(pej, NF))

                def psj(eng, inc, k=k):
                    for j in range(NF):
                        inc(eng.dma_start(
                            out=x8f[128 * j:128 * (j + 1), 128 * k:128 * (k + 1)],
                            in_=str_[:, 128 * j:128 * (j + 1)]), 16)
                ph.phase(gpsimd=(psj, 16 * NF))

            for t in range(2):
                for cchunk in range(4):
                    def pl(eng, inc, t=t, cchunk=cchunk):
                        inc(eng.dma_start(out=sxt[:, 0:D],
                                          in_=wfc_sh[128 * t:128 * (t + 1),
                                                     D * cchunk:D * (cchunk + 1)]), 16)
                    ph.phase(sync=(pl, 16))
                    quant_phases(sxt[:, 0:D], 1)

                    def pst(eng, inc, t=t, cchunk=cchunk):
                        for mm in range(16):
                            m = 16 * cchunk + mm
                            r0 = 256 * m + 128 * t
                            inc(eng.dma_start(out=wfc8t_sh[r0:r0 + 128, :],
                                              in_=qout[:, 128 * mm:128 * (mm + 1)]), 16)
                    ph.phase(gpsimd=(pst, 16 * 16))

            for t in range(8):
                def pl(eng, inc, t=t):
                    inc(eng.dma_start(out=sxt[:, 0:D],
                                      in_=wproj_sh[128 * t:128 * (t + 1), :]), 16)
                ph.phase(sync=(pl, 16))
                quant_phases(sxt[:, 0:D], 2)

                def pst(eng, inc, t=t):
                    inc(eng.dma_start(out=wproj8_sh[128 * t:128 * (t + 1), :],
                                      in_=qout[:, 0:D]), 16)
                ph.phase(gpsimd=(pst, 16))

        pq = ctx.enter_context(nc.psum_tensor(f"{tag}_pq", [128, 2048], F32))
        jobs = [(wq, qf, True), (wk, kf, True), (wv, vf, False)]

        def L(i, eng, inc):
            u = i // 2
            p, w = u // 3, u % 3
            if i % 2 == 1:
                return
            wten = jobs[w][0]
            so = 2 * D * ((i // 2) % 2)
            if w == 0:
                inc(eng.dma_start(out=scs[:, TPC * (p % 2):TPC * (p % 2) + TPC],
                                  in_=cosT[128 * p:128 * (p + 1), :]), 16)
                inc(eng.dma_start(out=ssn[:, TPC * (p % 2):TPC * (p % 2) + TPC],
                                  in_=sinT[128 * p:128 * (p + 1), :]), 16)
            for half, m in enumerate((p, p + 8)):
                inc(eng.dma_start(
                    out=swf[:, so + D * half:so + D * (half + 1)],
                    in_=wten[m].rearrange("(j p) c -> p j c", p=128)), 16)

        def C(i, eng, inc):
            if i % 2 == 1:
                return
            so = 2 * D * ((i // 2) % 2)
            inc(eng.activation(out=swb[:, so:so + 2 * D],
                               in_=swf[:, so:so + 2 * D], func=AF.Copy))

        def M(i, eng, inc):
            u, h = i // 2, i % 2
            so = 2 * D * ((i // 2) % 2)
            sl = 1024 * (i % 2)
            for half in range(2):
                for j in range(NF):
                    mm = eng.matmul(
                        pq[:, sl + 512 * half:sl + 512 * (half + 1)],
                        swb[:, so + D * half + 128 * j:so + D * half + 128 * (j + 1)],
                        sXnF[:, TPC * j + 512 * h:TPC * j + 512 * (h + 1)],
                        start=(j == 0), stop=(j == NF - 1))
                inc(mm)

        def RP(i, eng, inc):
            u, h = i // 2, i % 2
            p, w = u // 3, u % 3
            rope = jobs[w][2]
            sl = 1024 * (i % 2)
            qo = 512 * (i % 2)
            pm = pq[:, sl:sl + 512]
            pm8 = pq[:, sl + 512:sl + 1024]
            if rope:
                cc = scs[:, TPC * (p % 2) + 512 * h:TPC * (p % 2) + 512 * (h + 1)]
                ss = ssn[:, TPC * (p % 2) + 512 * h:TPC * (p % 2) + 512 * (h + 1)]
                eng.tensor_tensor(out=st1[:, 0:512], in0=cc, in1=pm, op=ALU.mult)
                eng.tensor_tensor(out=st2[:, 0:512], in0=ss, in1=pm8, op=ALU.mult)
                inc(eng.tensor_tensor(out=sQa[:, qo:qo + 512], in0=st1[:, 0:512],
                                      in1=st2[:, 0:512], op=ALU.subtract))
                eng.tensor_tensor(out=st1[:, 0:512], in0=cc, in1=pm8, op=ALU.mult)
                eng.tensor_tensor(out=st2[:, 0:512], in0=ss, in1=pm, op=ALU.mult)
                inc(eng.tensor_tensor(out=sQb[:, qo:qo + 512], in0=st1[:, 0:512],
                                      in1=st2[:, 0:512], op=ALU.add))
            else:
                inc(eng.tensor_copy(out=sQa[:, qo:qo + 512], in_=pm))
                inc(eng.tensor_copy(out=sQb[:, qo:qo + 512], in_=pm8))

        def ST(i, eng, inc):
            u, h = i // 2, i % 2
            p, w = u // 3, u % 3
            outd = jobs[w][1]
            qo = 512 * (i % 2)
            inc(eng.dma_start(out=outd[128 * p:128 * (p + 1), 512 * h:512 * (h + 1)],
                              in_=sQa[:, qo:qo + 512]), 16)
            inc(eng.dma_start(out=outd[128 * (p + 8):128 * (p + 9), 512 * h:512 * (h + 1)],
                              in_=sQb[:, qo:qo + 512]), 16)

        pipe(ph, 48, [
            ("sync", 0, L, lambda i: 0 if i % 2 else (16 * 4 if (i // 2) % 3 == 0 else 16 * 2)),
            ("scalar", 1, C, lambda i: 0 if i % 2 else 1),
            ("tensor", 2, M, 2),
            ("vector", 3, RP, 2),
            ("gpsimd", 4, ST, 32),
        ])



def stage_attn0(nc, ph, io, tag="a0"):
    qg, kg, vg = io["qg"], io["kg"], io["vg"]
    gate_row, sel_row = io["gate_row"], io["sel_row"]
    identB, identF = io["identB"], io["identF"]
    combh0, combh1 = io["combh0"], io["combh1"]
    NH = 8

    with contextlib.ExitStack() as ctx:
        T = lambda name, shape, dt: ctx.enter_context(
            nc.sbuf_tensor(f"{tag}_{name}", shape, dt))
        sQH = T("sQH", [128, NH * SEQ], BF16)
        sKH = T("sKH", [128, NH * SEQ], BF16)
        sVH = T("sVH", [128, NH * SEQ], BF16)
        sCMB = T("sCMB", [128, NH * SEQ], BF16)
        sCAND = T("sCAND", [128, 6 * TPC], BF16)
        sidB = T("sidB", [128, 128], BF16)
        sidF = T("sidF", [128, 128], F32)
        sqsc = T("sqsc", [128, SEQ], BF16)
        se1 = T("se1", [128, SEQ], F32)
        se2 = T("se2", [128, SEQ], F32)
        sqf = T("sqf", [128, SEQ], BF16)
        skf = T("skf", [128, SEQ], BF16)
        skfT = T("skfT", [128, SEQ], BF16)
        svT = T("svT", [128, 16 * 129], BF16)
        snt = T("snt", [128, 1], F32)
        smem = T("smem", [128, 129], BF16)
        sneg = T("sneg", [128, 16], F32)
        snegrb = T("snegrb", [16, 128], BF16)
        snegrow = T("snegrow", [1, SEQ], BF16)
        sonerow = T("sonerow", [1, 128], BF16)
        sonerowF = T("sonerowF", [1, 128], F32)
        sPT = T("sPT", [128, 16 * 512], BF16)
        sgr = T("sgr", [1, 40], F32)
        sgc = T("sgc", [128, 16], F32)
        sg1c = T("sg1c", [128, 16], F32)
        ssel = T("ssel", [128, 2], F32)
        srA = T("srA", [128, 4], F32)
        srM = T("srM", [128, 4], F32)
        stA = T("stA", [128, 512], F32)
        stM = T("stM", [128, 512], F32)
        sCT = T("sCT", [128, 512], BF16)
        pSA = ctx.enter_context(nc.psum_tensor(f"{tag}_pSA", [128, 2048], F32))
        pSB = ctx.enter_context(nc.psum_tensor(f"{tag}_pSB", [128, 1536], F32))
        pCT = ctx.enter_context(nc.psum_tensor(f"{tag}_pCT", [128, 512], BF16))

        pSt = [pSA[:, 0:512], pSA[:, 512:1024]]
        pAV = pSB[:, 0:129]
        pMO = pSB[:, 512:641]

        def p0(eng, inc):
            inc(eng.dma_start(out=sidB[:], in_=identB), 16)
            inc(eng.dma_start(out=sidF[:], in_=identF), 16)
            inc(eng.dma_start(out=sgr[0:1, 0:16], in_=gate_row), 16)
            inc(eng.dma_start(out=sgr[0:1, 32:34], in_=sel_row), 16)
        ph.phase(sync=(p0, 64))

        def pg1(eng, inc):
            inc(eng.activation(out=sgr[0:1, 16:32], in_=sgr[0:1, 0:16],
                               func=AF.Sigmoid))
        ph.phase(scalar=(pg1, 1))

        def pg2(eng, inc):
            inc(eng.tensor_scalar(out=sgr[0:1, 0:16], in0=sgr[0:1, 16:32],
                                  scalar1=-1.0, scalar2=1.0, op0=ALU.mult,
                                  op1=ALU.add))
            eng.memset(sonerowF[:], 1.0)
            eng.memset(sonerow[:], 1.0)
            for j in range(16):
                eng.memset(svT[:, 129 * j + 128:129 * (j + 1)], 1.0)
            inc(eng.memset(ssel[:], 0.0))
        ph.phase(vector=(pg2, 2))

        def pg3(eng, inc):
            eng.matmul(pSA[0:128, 0:16], sonerowF[:], sgr[0:1, 16:32],
                       start=True, stop=True)
            eng.matmul(pSA[0:128, 16:32], sonerowF[:], sgr[0:1, 0:16],
                       start=True, stop=True)
            inc(eng.matmul(pSA[0:128, 32:34], sonerowF[:], sgr[0:1, 32:34],
                           start=True, stop=True))
        ph.phase(tensor=(pg3, 1))

        def pg4(eng, inc):
            eng.tensor_copy(out=sgc[:], in_=pSA[0:128, 0:16])
            eng.tensor_copy(out=sg1c[:], in_=pSA[0:128, 16:32])
            inc(eng.tensor_copy(out=ssel[:], in_=pSA[0:128, 32:34]))
        ph.phase(vector=(pg4, 1))

        for i in range(NH):
            def bl(eng, inc, i=i):
                for w, (gt, dst) in enumerate(((qg, sQH), (kg, sKH), (vg, sVH))):
                    for half in range(2):
                        r0 = 2048 * half + 128 * i
                        inc(eng.dma_start(
                            out=dst[:, SEQ * i + TPC * half:SEQ * i + TPC * (half + 1)],
                            in_=gt[r0:r0 + 128, :]), 16)
                        inc(eng.dma_start(
                            out=sCAND[:, TPC * (2 * w + half):TPC * (2 * w + half + 1)],
                            in_=gt[r0 + 1024:r0 + 1024 + 128, :]), 16)
            ph.phase(sync=(bl, 16 * 12))

            def b1(eng, inc, i=i):
                for w, (gt, dst) in enumerate(((qg, sQH), (kg, sKH), (vg, sVH))):
                    for half in range(2):
                        sl = dst[:, SEQ * i + TPC * half:SEQ * i + TPC * (half + 1)]
                        inc(eng.tensor_scalar(out=sl, in0=sl, scalar1=ssel[:, 0:1],
                                              scalar2=None, op0=ALU.mult))
            ph.phase(vector=(b1, 6))

            def b2(eng, inc, i=i):
                for w, (gt, dst) in enumerate(((qg, sQH), (kg, sKH), (vg, sVH))):
                    for half in range(2):
                        c = TPC * (2 * w + half)
                        sl = dst[:, SEQ * i + TPC * half:SEQ * i + TPC * (half + 1)]
                        inc(eng.scalar_tensor_tensor(
                            out=sl, in0=sCAND[:, c:c + TPC], scalar=ssel[:, 1:2],
                            in1=sl, op0=ALU.mult, op1=ALU.add))
            ph.phase(vector=(b2, 6))

        for i in range(NH):
            qi = sQH[:, SEQ * i:SEQ * (i + 1)]
            ki = sKH[:, SEQ * i:SEQ * (i + 1)]
            vi = sVH[:, SEQ * i:SEQ * (i + 1)]

            def a1(eng, inc, qi=qi):
                inc(eng.activation(out=sqsc[:], in_=qi, func=AF.Identity,
                                   scale=SM_SCALE0))
            ph.phase(scalar=(a1, 1))

            for (src_, dst_) in ((qi, sqf), (ki, skf)):
                def a2(eng, inc, src_=src_):
                    inc(eng.tensor_scalar(out=se1[:], in0=src_, scalar1=0.0,
                                          scalar2=None, op0=ALU.min))
                    inc(eng.tensor_scalar(out=se2[:], in0=src_, scalar1=0.0,
                                          scalar2=None, op0=ALU.max))
                ph.phase(vector=(a2, 2))

                def a3(eng, inc):
                    inc(eng.activation(out=se1[:], in_=se1[:], func=AF.Exp))
                ph.phase(scalar=(a3, 1))

                def a4(eng, inc, dst_=dst_):
                    inc(eng.tensor_add(out=dst_[:], in0=se1[:], in1=se2[:]))
                ph.phase(vector=(a4, 1))

            def a5(eng, inc):
                inc(eng.tensor_reduce(out=snt[:], in_=skf[:], axis=AX.X, op=ALU.add))
            ph.phase(vector=(a5, 1))

            def a6(eng, inc):
                inc(eng.tensor_copy(out=smem[:, 128:129], in_=snt[:]))
            ph.phase(vector=(a6, 1))

            for g4 in range(4):
                def b1(eng, inc, g4=g4):
                    for u in range(4):
                        j = 4 * g4 + u
                        inc(eng.transpose(pCT[:, 128 * u:128 * (u + 1)],
                                          skf[:, 128 * j:128 * (j + 1)], sidB[:]))
                ph.phase(tensor=(b1, 4))

                def b2(eng, inc, g4=g4):
                    for u in range(4):
                        j = 4 * g4 + u
                        inc(eng.tensor_copy(out=skfT[:, 128 * j:128 * (j + 1)],
                                            in_=pCT[:, 128 * u:128 * (u + 1)]))
                ph.phase(vector=(b2, 4))
            for g4 in range(4):
                def b3(eng, inc, g4=g4, vi=vi):
                    for u in range(4):
                        j = 4 * g4 + u
                        inc(eng.transpose(pCT[:, 128 * u:128 * (u + 1)],
                                          vi[:, 128 * j:128 * (j + 1)], sidB[:]))
                ph.phase(tensor=(b3, 4))

                def b4(eng, inc, g4=g4):
                    for u in range(4):
                        j = 4 * g4 + u
                        inc(eng.tensor_copy(out=svT[:, 129 * j:129 * j + 128],
                                            in_=pCT[:, 128 * u:128 * (u + 1)]))
                ph.phase(vector=(b4, 4))

            def c1(eng, inc):
                for j in range(16):
                    mm = eng.matmul(pSB[:, 1024:1152], skfT[:, 128 * j:128 * (j + 1)],
                                    svT[:, 129 * j:129 * j + 128],
                                    start=(j == 0), stop=(j == 15))
                inc(mm)
            ph.phase(tensor=(c1, 1))

            def c2(eng, inc):
                inc(eng.activation(out=smem[:, 0:128], in_=pSB[:, 1024:1152],
                                   func=AF.Copy))
            ph.phase(scalar=(c2, 1))

            for u in range(16):
                def d1(eng, inc, u=u, ki=ki):
                    for cc4 in range(4):
                        mm = eng.matmul(pSA[:, 512 * cc4:512 * (cc4 + 1)],
                                        sqsc[:, 128 * u:128 * (u + 1)],
                                        ki[:, 512 * cc4:512 * (cc4 + 1)],
                                        start=True, stop=True)
                    inc(mm)
                ph.phase(tensor=(d1, 1))

                def d2(eng, inc, u=u):
                    inc(eng.tensor_reduce(out=sneg[:, u:u + 1], in_=pSA[:],
                                          axis=AX.X, op=ALU.max, negate=True))
                ph.phase(vector=(d2, 1))

            def e1(eng, inc):
                inc(eng.transpose(pSA[0:16, 0:128], sneg[:], sidF[:]))
            ph.phase(tensor=(e1, 1))

            def e2(eng, inc):
                inc(eng.tensor_copy(out=snegrb[:], in_=pSA[0:16, 0:128]))
            ph.phase(vector=(e2, 1))

            def e3(eng, inc):
                for r in range(16):
                    inc(eng.dma_start(out=snegrow[0:1, 128 * r:128 * (r + 1)],
                                      in_=snegrb[r:r + 1, :]), 16)
            ph.phase(gpsimd=(e3, 256))

            for qb in range(4):
                def f1(kp, eng, inc, qb=qb, ki=ki):
                    base = 1024 * (kp % 2)
                    for w in range(2):
                        kb = 2 * kp + w
                        p = pSA[:, base + 512 * w:base + 512 * (w + 1)]
                        eng.matmul(p, ki[:, 128 * kb:128 * (kb + 1)],
                                   sqsc[:, 512 * qb:512 * (qb + 1)],
                                   start=True, stop=False)
                        mm = eng.matmul(p, sonerow[:],
                                        snegrow[0:1, 512 * qb:512 * (qb + 1)],
                                        start=False, stop=True)
                    inc(mm)

                def f2(kp, eng, inc):
                    base = 1024 * (kp % 2)
                    inc(eng.activation(out=sPT[:, 1024 * kp:1024 * (kp + 1)],
                                       in_=pSA[:, base:base + 1024], func=AF.Exp))

                pipe(ph, 8, [("tensor", 0, f1, 1), ("scalar", 1, f2, 1)])

                AVs = [pSB[:, 0:129], pSB[:, 512:641], pSB[:, 1024:1153],
                       pSA[:, 0:129]]
                MOs = [pSA[:, 512:641], pSA[:, 1024:1153], pSA[:, 1536:1665],
                       pSB[:, 129:258]]

                def f3(eng, inc, qb=qb):
                    for v4 in range(4):
                        for kb in range(16):
                            mm = eng.matmul(
                                AVs[v4],
                                sPT[:, 512 * kb + 128 * v4:512 * kb + 128 * (v4 + 1)],
                                svT[:, 129 * kb:129 * (kb + 1)],
                                start=(kb == 0), stop=(kb == 15))
                        inc(mm)
                        inc(eng.matmul(MOs[v4], sqf[:, 512 * qb + 128 * v4:
                                                512 * qb + 128 * (v4 + 1)],
                                       smem[:, 0:129], start=True, stop=True))
                ph.phase(tensor=(f3, 8))

                def f4(eng, inc):
                    for v4 in range(4):
                        inc(eng.reciprocal(out=srA[:, v4:v4 + 1],
                                           in_=AVs[v4][:, 128:129]))
                        inc(eng.reciprocal(out=srM[:, v4:v4 + 1],
                                           in_=MOs[v4][:, 128:129]))
                ph.phase(vector=(f4, 8))

                def f5(eng, inc, i=i):
                    for v4 in range(4):
                        inc(eng.tensor_tensor(out=srA[:, v4:v4 + 1],
                                              in0=srA[:, v4:v4 + 1],
                                              in1=sg1c[:, i:i + 1], op=ALU.mult))
                        inc(eng.tensor_tensor(out=srM[:, v4:v4 + 1],
                                              in0=srM[:, v4:v4 + 1],
                                              in1=sgc[:, i:i + 1], op=ALU.mult))
                ph.phase(vector=(f5, 8))

                def f6(eng, inc):
                    for v4 in range(4):
                        inc(eng.activation(out=stA[:, 128 * v4:128 * (v4 + 1)],
                                           in_=AVs[v4][:, 0:128],
                                           func=AF.Identity, scale=srA[:, v4:v4 + 1]))
                        inc(eng.activation(out=stM[:, 128 * v4:128 * (v4 + 1)],
                                           in_=MOs[v4][:, 0:128],
                                           func=AF.Identity, scale=srM[:, v4:v4 + 1]))
                ph.phase(scalar=(f6, 8))

                def f7(eng, inc):
                    inc(eng.tensor_add(out=sCT[:], in0=stA[:], in1=stM[:]))
                ph.phase(vector=(f7, 1))

                def f8(eng, inc):
                    for v4 in range(4):
                        inc(eng.transpose(pCT[:, 128 * v4:128 * (v4 + 1)],
                                          sCT[:, 128 * v4:128 * (v4 + 1)], sidB[:]))
                ph.phase(tensor=(f8, 4))

                def f9(eng, inc, i=i, qb=qb):
                    for v4 in range(4):
                        inc(eng.tensor_copy(
                            out=sCMB[:, SEQ * i + 512 * qb + 128 * v4:
                                     SEQ * i + 512 * qb + 128 * (v4 + 1)],
                            in_=pCT[:, 128 * v4:128 * (v4 + 1)]))
                ph.phase(vector=(f9, 4))

        def pst(eng, inc):
            for i in range(NH):
                inc(eng.dma_start(out=combh0[128 * i:128 * (i + 1), :],
                                  in_=sCMB[:, SEQ * i:SEQ * i + TPC]), 16)
                inc(eng.dma_start(out=combh1[128 * i:128 * (i + 1), :],
                                  in_=sCMB[:, SEQ * i + TPC:SEQ * (i + 1)]), 16)
        ph.phase(gpsimd=(pst, 16 * 16))



def stage_fc0(nc, ph, io, tag="fc"):
    x8f, wfc8g, gmax, identF = io["x8f"], io["wfc8g"], io["gmax"], io["identF"]
    hbuf, hmax = io["hbuf"], io["hmax"]

    with contextlib.ExitStack() as ctx:
        T = lambda name, shape, dt: ctx.enter_context(
            nc.sbuf_tensor(f"{tag}_{name}", shape, dt))
        sX8 = T("sX8", [128, NF * TPC], BF16)
        sw = T("sw", [128, 2 * D], BF16)
        sidF = T("sidF", [128, 128], F32)
        srel = T("srel", [128, TPC], F32)
        sh = T("sh", [128, 2 * TPC], F32)
        srow = T("srow", [1, 8], F32)
        sone = T("sone", [1, 128], F32)
        sdq = T("sdq", [128, 1], F32)
        shm = T("shm", [128, NFI], F32)
        shm1 = T("shm1", [128, 1], F32)
        shmr = T("shmr", [1, 128], F32)
        sout1 = T("sout1", [1, 1], F32)
        sgm = T("sgm", [1, 3], F32)
        pacc = ctx.enter_context(nc.psum_tensor(f"{tag}_pacc", [128, 2 * TPC], F32))
        pRep = ctx.enter_context(nc.psum_tensor(f"{tag}_pRep", [128, 1], F32))

        def p0(eng, inc):
            inc(eng.dma_start(out=sidF[:], in_=identF), 16)
            inc(eng.dma_start(out=sgm[:], in_=gmax), 16)
            _dma_tiles(eng, inc, sX8, x8f, D, TPC)
        ph.phase(sync=(p0, 16 * 18))

        def p2(eng, inc):
            eng.memset(sone[:], 1.0)
            inc(eng.tensor_scalar(out=srow[0:1, 2:3], in0=sgm[0:1, 0:1],
                                  scalar1=1.0 / FP8MAX, scalar2=1e-6,
                                  op0=ALU.mult, op1=ALU.max))
            inc(eng.tensor_scalar(out=srow[0:1, 3:4], in0=sgm[0:1, 1:2],
                                  scalar1=1.0 / FP8MAX, scalar2=1e-6,
                                  op0=ALU.mult, op1=ALU.max))
        ph.phase(vector=(p2, 2))

        def p3(eng, inc):
            inc(eng.tensor_tensor(out=srow[0:1, 4:5], in0=srow[0:1, 2:3],
                                  in1=srow[0:1, 3:4], op=ALU.mult))
        ph.phase(vector=(p3, 1))

        def p4(eng, inc):
            inc(eng.matmul(pRep[:], sone[:], srow[0:1, 4:5], start=True, stop=True))
        ph.phase(tensor=(p4, 1))

        def p5(eng, inc):
            inc(eng.tensor_copy(out=sdq[:], in_=pRep[:]))
        ph.phase(vector=(p5, 1))

        def L(m, eng, inc):
            for a in range(NF // 2):
                r0 = 16384 * a + 256 * m
                inc(eng.dma_start(
                    out=sw[:, D * (m % 2) + 256 * a:D * (m % 2) + 256 * (a + 1)],
                    in_=wfc8g[r0:r0 + 256, :].rearrange("(j p) c -> p j c", p=128)), 16)

        def M(m, eng, inc):
            for h in range(2):
                for j in range(NF):
                    mm = eng.matmul(pacc[:, TPC * (m % 2) + 512 * h:
                                         TPC * (m % 2) + 512 * (h + 1)],
                                    sw[:, D * (m % 2) + 128 * j:
                                       D * (m % 2) + 128 * (j + 1)],
                                    sX8[:, TPC * j + 512 * h:TPC * j + 512 * (h + 1)],
                                    start=(j == 0), stop=(j == NF - 1))
                inc(mm)

        def R(m, eng, inc):
            eng.activation(out=srel[:], in_=pacc[:, TPC * (m % 2):TPC * (m % 2 + 1)],
                           func=AF.Relu)
            inc(eng.activation(out=sh[:, TPC * (m % 2):TPC * (m % 2 + 1)],
                               in_=srel[:], func=AF.Square, scale=sdq[:, 0:1]), 2)

        def V(m, eng, inc):
            inc(eng.tensor_reduce(out=shm[:, m:m + 1],
                                  in_=sh[:, TPC * (m % 2):TPC * (m % 2 + 1)],
                                  axis=AX.X, op=ALU.max))

        def S(m, eng, inc):
            inc(eng.dma_start(out=hbuf[128 * m:128 * (m + 1), :],
                              in_=sh[:, TPC * (m % 2):TPC * (m % 2 + 1)]), 16)

        pipe(ph, NFI, [("sync", 0, L, 16 * 8), ("tensor", 1, M, 2),
                       ("scalar", 2, R, 2), ("vector", 3, V, 1),
                       ("gpsimd", 3, S, 16)])

        def pf1(eng, inc):
            inc(eng.tensor_reduce(out=shm1[:], in_=shm[:], axis=AX.X, op=ALU.max))
        ph.phase(vector=(pf1, 1))

        def pf2(eng, inc):
            inc(eng.transpose(pacc[0:1, 0:128], shm1[:], sidF[:]))
        ph.phase(tensor=(pf2, 1))

        def pf3(eng, inc):
            inc(eng.tensor_copy(out=shmr[:], in_=pacc[0:1, 0:128]))
        ph.phase(vector=(pf3, 1))

        def pf4(eng, inc):
            inc(eng.tensor_reduce(out=sout1[:], in_=shmr[:], axis=AX.X, op=ALU.max))
        ph.phase(vector=(pf4, 1))

        def pf5(eng, inc):
            inc(eng.dma_start(out=hmax, in_=sout1[:]), 16)
        ph.phase(gpsimd=(pf5, 16))



def stage_proj0(nc, ph, io, tag="pj"):
    x, combA, combB, woT = io["x"], io["combA"], io["combB"], io["woT"]
    hbuf, wproj8g = io["hbuf"], io["wproj8g"]
    gmax, ghmax, sel_row = io["gmax"], io["ghmax"], io["sel_row"]
    x1 = io["x1"]

    with contextlib.ExitStack() as ctx:
        T = lambda name, shape, dt: ctx.enter_context(
            nc.sbuf_tensor(f"{tag}_{name}", shape, dt))
        sCMB = T("sCMB", [128, NF * TPC], BF16)
        sCA = T("sCA", [128, 2 * TPC], BF16)
        sCB = T("sCB", [128, 3 * TPC], BF16)
        sblt = T("sblt", [128, 2 * TPC], BF16)
        sR1 = T("sR1", [128, NT * D], F32)
        swf = T("swf", [128, 2 * 512], F32)
        swb = T("swb", [128, 2 * 512], BF16)
        sxs = T("sxs", [128, 2 * 512], F32)
        srow = T("srow", [1, 16], F32)
        ssel = T("ssel", [128, 2], F32)
        sselr = T("sselr", [1, 2], F32)
        sone = T("sone", [1, 128], F32)
        sqc = T("sqc", [128, 8], F32)
        sht = T("sht", [128, 2 * 512], F32)
        sh8 = T("sh8", [128, NFI * 512], BF16)
        swp = T("swp", [128, 8 * 512], BF16)
        qf8a = T("qf8a", [128, 512], FP8)
        qf8b = T("qf8b", [128, 512], FP8)
        qua = T("qua", [128, 3 * 512], BF16)
        qub = T("qub", [128, 3 * 512], BF16)
        qmk = T("qmk", [128, 3 * 512], U8)
        qmk2 = T("qmk2", [128, 2 * 512], U8)
        sgm = T("sgm", [1, 3], F32)
        sghm = T("sghm", [1, 1], F32)
        pW = ctx.enter_context(nc.psum_tensor(f"{tag}_pW", [128, 8 * 512], F32))

        def p0(eng, inc):
            inc(eng.dma_start(out=sselr[:], in_=sel_row), 16)
            inc(eng.dma_start(out=sgm[:], in_=gmax), 16)
            inc(eng.dma_start(out=sghm[:], in_=ghmax), 16)
        ph.phase(sync=(p0, 48))

        def p1(eng, inc):
            eng.memset(sone[:], 1.0)
            inc(eng.tensor_scalar(out=srow[0:1, 2:3], in0=sghm[0:1, 0:1],
                                  scalar1=1.0 / FP8MAX, scalar2=1e-6,
                                  op0=ALU.mult, op1=ALU.max))
            inc(eng.tensor_scalar(out=srow[0:1, 3:4], in0=sgm[0:1, 2:3],
                                  scalar1=1.0 / FP8MAX, scalar2=1e-6,
                                  op0=ALU.mult, op1=ALU.max))
        ph.phase(vector=(p1, 2))

        def p3(eng, inc):
            inc(eng.tensor_tensor(out=srow[0:1, 4:5], in0=srow[0:1, 2:3],
                                  in1=srow[0:1, 3:4], op=ALU.mult))
            inc(eng.reciprocal(out=srow[0:1, 5:6], in_=srow[0:1, 2:3]))
        ph.phase(vector=(p3, 2))

        def p4(eng, inc):
            inc(eng.tensor_scalar(out=srow[0:1, 6:7], in0=srow[0:1, 5:6],
                                  scalar1=0.5, scalar2=None, op0=ALU.mult))
            inc(eng.tensor_scalar(out=srow[0:1, 7:8], in0=srow[0:1, 2:3],
                                  scalar1=240.0, scalar2=None, op0=ALU.mult))
        ph.phase(vector=(p4, 2))

        def p4b(eng, inc):
            inc(eng.tensor_scalar(out=srow[0:1, 8:9], in0=srow[0:1, 7:8],
                                  scalar1=-1.0, scalar2=None, op0=ALU.mult))
        ph.phase(vector=(p4b, 1))

        def p5(eng, inc):
            eng.matmul(pW[0:128, 0:5], sone[:], srow[0:1, 4:9], start=True, stop=True)
            inc(eng.matmul(pW[0:128, 8:10], sone[:], sselr[:], start=True, stop=True))
        ph.phase(tensor=(p5, 1))

        def p6(eng, inc):
            inc(eng.tensor_copy(out=sqc[:, 0:5], in_=pW[0:128, 0:5]))
            inc(eng.tensor_copy(out=ssel[:], in_=pW[0:128, 8:10]))
        ph.phase(vector=(p6, 2))
        DQ2, INVH, INVH2, THR, NTHR = (sqc[:, j:j + 1] for j in range(5))

        def CL(j, eng, inc):
            s = j % 2
            inc(eng.dma_start(out=sCA[:, TPC * s:TPC * (s + 1)],
                              in_=combA[128 * j:128 * (j + 1), :]), 16)
            inc(eng.dma_start(out=sCB[:, TPC * (j % 3):TPC * (j % 3) + TPC],
                              in_=combB[128 * j:128 * (j + 1), :]), 16)

        def C1(j, eng, inc):
            s = j % 2
            inc(eng.tensor_scalar(out=sblt[:, TPC * s:TPC * (s + 1)],
                                  in0=sCA[:, TPC * s:TPC * (s + 1)],
                                  scalar1=ssel[:, 0:1], scalar2=None, op0=ALU.mult))

        def C2(j, eng, inc):
            s = j % 2
            inc(eng.scalar_tensor_tensor(out=sCMB[:, TPC * j:TPC * (j + 1)],
                                         in0=sCB[:, TPC * (j % 3):TPC * (j % 3) + TPC],
                                         scalar=ssel[:, 1:2],
                                         in1=sblt[:, TPC * s:TPC * (s + 1)],
                                         op0=ALU.mult, op1=ALU.add))

        pipe(ph, NF, [("sync", 0, CL, 32), ("vector", 1, C1, 1),
                      ("vector", 2, C2, 1)])

        for cP in range(4):
            def WL(j, eng, inc, cP=cP):
                inc(eng.dma_start(out=swf[:, 512 * (j % 2):512 * (j % 2) + 512],
                                  in_=woT[128 * j:128 * (j + 1),
                                          512 * cP:512 * (cP + 1)]), 16)

            def WC(j, eng, inc):
                s = j % 2
                inc(eng.activation(out=swb[:, 512 * s:512 * (s + 1)],
                                   in_=swf[:, 512 * s:512 * (s + 1)], func=AF.Copy))

            def WM(j, eng, inc):
                s = j % 2
                for t in range(8):
                    mm = eng.matmul(pW[:, 512 * t:512 * (t + 1)],
                                    sCMB[:, TPC * j + 128 * t:TPC * j + 128 * (t + 1)],
                                    swb[:, 512 * s:512 * (s + 1)],
                                    start=(j == 0), stop=(j == NF - 1))
                inc(mm)

            pipe(ph, NF, [("sync", 0, WL, 16), ("scalar", 1, WC, 1),
                          ("tensor", 2, WM, 1)])

            def XL(t, eng, inc, cP=cP):
                inc(eng.dma_start(out=sxs[:, 512 * (t % 2):512 * (t % 2) + 512],
                                  in_=x[128 * t:128 * (t + 1),
                                        512 * cP:512 * (cP + 1)]), 16)

            def XE(t, eng, inc, cP=cP):
                inc(eng.tensor_add(out=sR1[:, D * t + 512 * cP:D * t + 512 * (cP + 1)],
                                   in0=sxs[:, 512 * (t % 2):512 * (t % 2) + 512],
                                   in1=pW[:, 512 * t:512 * (t + 1)]))

            pipe(ph, 8, [("sync", 0, XL, 16), ("vector", 1, XE, 1)])

        for half in range(2):
            hs0 = 512 * half

            NH2 = NFI // 2
            def HL(j, eng, inc, hs0=hs0):
                inc(eng.dma_start(out=sht[:, 512 * (j % 2):512 * (j % 2) + 512],
                                  in_=hbuf[128 * j:128 * (j + 1), hs0:hs0 + 512]), 16)

            def HA(j, eng, inc):
                src_ = sht[:, 512 * (j % 2):512 * (j % 2) + 512]
                r = 512 * (j % 3)
                eng.activation(out=qf8a[:], in_=src_, func=AF.Identity, scale=INVH)
                inc(eng.activation(out=qua[:, r:r + 512], in_=qf8a[:],
                                   func=AF.Copy), 1)

            def HB(j, eng, inc):
                src_ = sht[:, 512 * (j % 2):512 * (j % 2) + 512]
                r = 512 * (j % 3)
                eng.tensor_scalar(out=qf8b[:], in0=src_, scalar1=INVH2,
                                  scalar2=None, op0=ALU.mult)
                inc(eng.tensor_scalar(out=qub[:, r:r + 512], in0=qf8b[:],
                                      scalar1=2.0, scalar2=None, op0=ALU.mult), 1)

            def HM(j, eng, inc):
                src_ = sht[:, 512 * (j % 2):512 * (j % 2) + 512]
                r = 512 * (j % 3)
                inc(eng.tensor_scalar(out=qmk[:, r:r + 512], in0=src_, scalar1=THR,
                                      scalar2=None, op0=ALU.is_ge))
                inc(eng.tensor_scalar(out=qmk2[:, 512 * (j % 2):512 * (j % 2) + 512],
                                      in0=src_, scalar1=NTHR,
                                      scalar2=None, op0=ALU.is_le))
            def HO(j, eng, inc):
                r = 512 * (j % 3)
                inc(eng.tensor_tensor(out=qmk[:, r:r + 512], in0=qmk[:, r:r + 512],
                                      in1=qmk2[:, 512 * (j % 2):512 * (j % 2) + 512],
                                      op=ALU.logical_or))

            def HS(j, eng, inc):
                r = 512 * (j % 3)
                inc(eng.select(out=sh8[:, 512 * j:512 * (j + 1)],
                               mask=qmk[:, r:r + 512], on_true=qub[:, r:r + 512],
                               on_false=qua[:, r:r + 512]))

            pipe(ph, NFI, [("sync", 0, HL, 16), ("scalar", 1, HA, 1),
                           ("gpsimd", 1, HB, 1),
                           ("vector", 1, HM, 2), ("vector", 2, HO, 1),
                           ("vector", 3, HS, 1)])

            NG = 16
            for cP in range(4):
                def L2(g, eng, inc, cP=cP):
                    for q in range(4):
                        mj = 4 * g + q
                        inc(eng.dma_start(
                            out=swp[:, 2048 * (g % 2) + 512 * q:
                                     2048 * (g % 2) + 512 * (q + 1)],
                            in_=wproj8g[128 * mj:128 * (mj + 1),
                                        512 * cP:512 * (cP + 1)]), 16)

                def M2(g, eng, inc):
                    for q in range(4):
                        mj = 4 * g + q
                        for t in range(4):
                            mm = eng.matmul(pW[:, 512 * t:512 * (t + 1)],
                                            sh8[:, 512 * mj + 128 * t:
                                                 512 * mj + 128 * (t + 1)],
                                            swp[:, 2048 * (g % 2) + 512 * q:
                                                 2048 * (g % 2) + 512 * (q + 1)],
                                            start=(mj == 0), stop=(mj == NFI - 1))
                    inc(mm)

                pipe(ph, NG, [("sync", 0, L2, 64), ("tensor", 1, M2, 1)])

                def SC(t, eng, inc):
                    inc(eng.tensor_scalar(out=sxs[:, 512 * (t % 2):512 * (t % 2) + 512],
                                          in0=pW[:, 512 * t:512 * (t + 1)],
                                          scalar1=DQ2, scalar2=None, op0=ALU.mult))

                def AD(t, eng, inc, half=half, cP=cP):
                    tg = 4 * half + t
                    sl = sR1[:, D * tg + 512 * cP:D * tg + 512 * (cP + 1)]
                    inc(eng.tensor_add(out=sl, in0=sl,
                                       in1=sxs[:, 512 * (t % 2):512 * (t % 2) + 512]))

                pipe(ph, 4, [("vector", 0, SC, 1), ("vector", 1, AD, 1)])

        def pst(eng, inc):
            _dma_tiles(eng, inc, sR1, x1, TPC, D, to_sbuf=False)
        ph.phase(gpsimd=(pst, 16 * NT))



def stage_qkv1(nc, ph, io, tag="q1"):
    xn1, identB = io["xn1"], io["identB"]
    wq, wk, wv = io["wq"], io["wk"], io["wv"]
    q1, k1, v1 = io["q1"], io["k1"], io["v1"]

    with contextlib.ExitStack() as ctx:
        T = lambda name, shape, dt: ctx.enter_context(
            nc.sbuf_tensor(f"{tag}_{name}", shape, dt))
        sXnF = T("sXnF", [128, NF * TPC], BF16)
        sxt = T("sxt", [128, 2 * D], F32)
        sxb = T("sxb", [128, 2 * D], BF16)
        sidB = T("sidB", [128, 128], BF16)
        swf = T("swf", [128, 2 * D], F32)
        swb = T("swb", [128, 2 * D], BF16)
        sQa = T("sQa", [128, 2 * TPC], BF16)
        ptr2 = ctx.enter_context(nc.psum_tensor(f"{tag}_ptr2", [128, 4096], BF16))
        pa = ctx.enter_context(nc.psum_tensor(f"{tag}_pa", [128, 2 * TPC], F32))

        def p0(eng, inc):
            inc(eng.dma_start(out=sidB[:], in_=identB), 16)
        ph.phase(sync=(p0, 16))

        pipe_tm2fm(ph, xn1, sXnF, sxt, sxb, ptr2, sidB)

        jobs = [(wq, q1, SM_SCALE1), (wk, k1, None), (wv, v1, None)]

        def L(i, eng, inc):
            wten = jobs[i // NF][0]
            m = i % NF
            inc(eng.dma_start(
                out=swf[:, D * (i % 2):D * (i % 2) + D],
                in_=wten[m].rearrange("(j p) c -> p j c", p=128)), 16)

        def C(i, eng, inc):
            s = i % 2
            inc(eng.activation(out=swb[:, D * s:D * (s + 1)],
                               in_=swf[:, D * s:D * (s + 1)], func=AF.Copy))

        def M(i, eng, inc):
            s = i % 2
            for h in range(2):
                for j in range(NF):
                    mm = eng.matmul(
                        pa[:, TPC * s + 512 * h:TPC * s + 512 * (h + 1)],
                        swb[:, D * s + 128 * j:D * s + 128 * (j + 1)],
                        sXnF[:, TPC * j + 512 * h:TPC * j + 512 * (h + 1)],
                        start=(j == 0), stop=(j == NF - 1))
                inc(mm)

        def E(i, eng, inc):
            s = i % 2
            scale = jobs[i // NF][2]
            if scale is None:
                inc(eng.activation(out=sQa[:, TPC * s:TPC * (s + 1)],
                                   in_=pa[:, TPC * s:TPC * (s + 1)], func=AF.Copy))
            else:
                inc(eng.activation(out=sQa[:, TPC * s:TPC * (s + 1)],
                                   in_=pa[:, TPC * s:TPC * (s + 1)],
                                   func=AF.Identity, scale=scale))

        def S(i, eng, inc):
            s = i % 2
            outd = jobs[i // NF][1]
            m = i % NF
            inc(eng.dma_start(out=outd[128 * m:128 * (m + 1), :],
                              in_=sQa[:, TPC * s:TPC * (s + 1)]), 16)

        pipe(ph, 3 * NF, [("sync", 0, L, 16), ("scalar", 1, C, 1),
                          ("tensor", 2, M, 2), ("scalar", 3, E, 1),
                          ("gpsimd", 4, S, 16)])



def stage_v1t(nc, ph, io, tag="vt"):
    v1g, identB, v1t = io["v1g"], io["identB"], io["v1t"]
    with contextlib.ExitStack() as ctx:
        T = lambda name, shape, dt: ctx.enter_context(
            nc.sbuf_tensor(f"{tag}_{name}", shape, dt))
        sin_ = T("sin", [128, 2 * TPC], BF16)
        stile = T("stile", [128, 2 * TPC], BF16)
        sidB = T("sidB", [128, 128], BF16)
        pT = ctx.enter_context(nc.psum_tensor(f"{tag}_pT", [128, 2048], BF16))

        def p0(eng, inc):
            inc(eng.dma_start(out=sidB[:], in_=identB), 16)
        ph.phase(sync=(p0, 16))

        def L(i, eng, inc):
            r, et = i // 16, i % 16
            inc(eng.dma_start(out=sin_[:, TPC * (i % 2):TPC * (i % 2) + TPC],
                              in_=v1g[2048 * r + 128 * et:2048 * r + 128 * (et + 1), :]), 16)

        def Tr(i, eng, inc):
            for b in range(8):
                inc(eng.transpose(pT[:, 1024 * (i % 2) + 128 * b:
                                     1024 * (i % 2) + 128 * (b + 1)],
                                  sin_[:, TPC * (i % 2) + 128 * b:
                                       TPC * (i % 2) + 128 * (b + 1)], sidB[:]))

        def Ev(i, eng, inc):
            for b in range(8):
                inc(eng.tensor_copy(out=stile[:, TPC * (i % 2) + 128 * b:
                                             TPC * (i % 2) + 128 * (b + 1)],
                                    in_=pT[:, 1024 * (i % 2) + 128 * b:
                                          1024 * (i % 2) + 128 * (b + 1)]))

        def St(i, eng, inc):
            r, et = i // 16, i % 16
            for b in range(8):
                kt_row = 1024 * r + 128 * b
                inc(eng.dma_start(out=v1t[kt_row:kt_row + 128,
                                          128 * et:128 * (et + 1)],
                                  in_=stile[:, TPC * (i % 2) + 128 * b:
                                           TPC * (i % 2) + 128 * (b + 1)]), 16)

        pipe(ph, 32, [("sync", 0, L, 16), ("tensor", 1, Tr, 8),
                      ("vector", 2, Ev, 8), ("gpsimd", 3, St, 16 * 8)])



def stage_attn1(nc, ph, io, tag="a1"):
    q1, k1g, v1t = io["q1"], io["k1g"], io["v1t"]
    woT, bo_row, x1 = io["woT"], io["bo_row"], io["x1"]
    identB, identF = io["identB"], io["identF"]
    r2 = io["r2"]

    with contextlib.ExitStack() as ctx:
        T = lambda name, shape, dt: ctx.enter_context(
            nc.sbuf_tensor(f"{tag}_{name}", shape, dt))
        sQ1 = T("sQ1", [128, NF * TPC], BF16)
        sK1 = T("sK1", [128, NF * SEQ], BF16)
        sPT = T("sPT", [128, NF * TPC], BF16)
        sidB = T("sidB", [128, 128], BF16)
        sidF = T("sidF", [128, 128], F32)
        sneg = T("sneg", [128, 8], F32)
        snegrb = T("snegrb", [8, 128], BF16)
        snegrow = T("snegrow", [1, TPC], BF16)
        sonerow = T("sonerow", [1, 128], BF16)
        sonecol = T("sonecol", [128, 1], BF16)
        srd = T("srd", [128, 8], F32)
        svs = T("svs", [128, 2 * 512], BF16)
        sbo = T("sbo", [1, D], F32)
        sbob = T("sbob", [1, D], BF16)
        swf = T("swf", [128, 2 * 512], F32)
        swb = T("swb", [128, 2 * 512], BF16)
        sxs = T("sxs", [128, 2 * 512], F32)
        sout = T("sout", [128, 2 * 512], F32)
        pS = ctx.enter_context(nc.psum_tensor(f"{tag}_pS", [128, 2048], F32))
        pCT = ctx.enter_context(nc.psum_tensor(f"{tag}_pCT", [128, 4096], BF16))

        sAT = sK1
        sAF = sQ1

        def p0(eng, inc):
            inc(eng.dma_start(out=sidB[:], in_=identB), 16)
            inc(eng.dma_start(out=sidF[:], in_=identF), 16)
            inc(eng.dma_start(out=sbo[:], in_=bo_row), 16)
            for j in range(NF):
                inc(eng.dma_start(out=sK1[:, SEQ * j:SEQ * j + TPC],
                                  in_=k1g[128 * j:128 * (j + 1), :]), 16)
                inc(eng.dma_start(out=sK1[:, SEQ * j + TPC:SEQ * (j + 1)],
                                  in_=k1g[2048 + 128 * j:2048 + 128 * (j + 1), :]), 16)

        def p0b(eng, inc):
            _dma_tiles(eng, inc, sQ1, q1, D, TPC)
        ph.phase(sync=(p0, 16 * 35), gpsimd=(p0b, 16 * 16))

        def p1(eng, inc):
            eng.memset(sonerow[:], 1.0)
            inc(eng.memset(sonecol[:], 1.0))
            inc(eng.tensor_copy(out=sbob[:], in_=sbo[:]))
        ph.phase(vector=(p1, 2))

        for u in range(8):
            def d1(eng, inc, u=u):
                for kc in range(4):
                    for j in range(NF):
                        mm = eng.matmul(pS[:, 512 * kc:512 * (kc + 1)],
                                        sQ1[:, TPC * j + 128 * u:TPC * j + 128 * (u + 1)],
                                        sK1[:, SEQ * j + 512 * kc:SEQ * j + 512 * (kc + 1)],
                                        start=(j == 0), stop=(j == NF - 1))
                inc(mm)
            ph.phase(tensor=(d1, 1))

            def d2(eng, inc, u=u):
                inc(eng.tensor_reduce(out=sneg[:, u:u + 1], in_=pS[:], axis=AX.X,
                                      op=ALU.max, negate=True))
            ph.phase(vector=(d2, 1))

        def e1(eng, inc):
            inc(eng.transpose(pS[0:8, 0:128], sneg[:], sidF[:]))
        ph.phase(tensor=(e1, 1))

        def e2(eng, inc):
            inc(eng.tensor_copy(out=snegrb[:], in_=pS[0:8, 0:128]))
        ph.phase(vector=(e2, 1))

        def e3(eng, inc):
            for r in range(8):
                inc(eng.dma_start(out=snegrow[0:1, 128 * r:128 * (r + 1)],
                                  in_=snegrb[r:r + 1, :]), 16)
        ph.phase(gpsimd=(e3, 128))

        def sf1(i, eng, inc):
            kb, qh2 = i // 2, i % 2
            p = pS[:, 512 * (i % 2):512 * (i % 2) + 512]
            for j in range(NF):
                eng.matmul(p, sK1[:, SEQ * j + 128 * kb:SEQ * j + 128 * (kb + 1)],
                           sQ1[:, TPC * j + 512 * qh2:TPC * j + 512 * (qh2 + 1)],
                           start=(j == 0), stop=False)
            inc(eng.matmul(p, sonerow[:],
                           snegrow[0:1, 512 * qh2:512 * (qh2 + 1)],
                           start=False, stop=True))

        def sf2(i, eng, inc):
            kb, qh2 = i // 2, i % 2
            inc(eng.activation(
                out=sPT[:, TPC * kb + 512 * qh2:TPC * kb + 512 * (qh2 + 1)],
                in_=pS[:, 512 * (i % 2):512 * (i % 2) + 512], func=AF.Exp))

        pipe(ph, 2 * NF, [("tensor", 0, sf1, 1), ("scalar", 1, sf2, 1)])

        def g1(eng, inc):
            for u in range(8):
                for kb in range(NF):
                    mm = eng.matmul(pS[:, 1536 + u:1537 + u],
                                    sPT[:, TPC * kb + 128 * u:TPC * kb + 128 * (u + 1)],
                                    sonecol[:], start=(kb == 0), stop=(kb == NF - 1))
                inc(mm)
        ph.phase(tensor=(g1, 8))

        def g2(eng, inc):
            for u in range(8):
                inc(eng.reciprocal(out=srd[:, u:u + 1], in_=pS[:, 1536 + u:1537 + u]))
        ph.phase(vector=(g2, 8))

        for up in range(2):
            for cE in range(4):
                def h1(kb, eng, inc, cE=cE):
                    inc(eng.dma_start(out=svs[:, 512 * (kb % 2):512 * (kb % 2) + 512],
                                      in_=v1t[128 * kb:128 * (kb + 1),
                                              512 * cE:512 * (cE + 1)]), 16)

                def h2(kb, eng, inc, up=up):
                    for u4 in range(4):
                        u = 4 * up + u4
                        mm = eng.matmul(pS[:, 512 * u4:512 * (u4 + 1)],
                                        sPT[:, TPC * kb + 128 * u:
                                             TPC * kb + 128 * (u + 1)],
                                        svs[:, 512 * (kb % 2):512 * (kb % 2) + 512],
                                        start=(kb == 0), stop=(kb == NF - 1))
                    inc(mm)

                pipe(ph, NF, [("sync", 0, h1, 16), ("tensor", 1, h2, 1)])

                def h3(eng, inc, up=up, cE=cE):
                    for u4 in range(4):
                        u = 4 * up + u4
                        inc(eng.activation(
                            out=sAT[:, SEQ * u + 512 * cE:SEQ * u + 512 * (cE + 1)],
                            in_=pS[:, 512 * u4:512 * (u4 + 1)],
                            func=AF.Identity, scale=srd[:, u:u + 1]))
                ph.phase(scalar=(h3, 4))

        def t1(u, eng, inc):
            h = 2048 * (u % 2)
            for j in range(NF):
                inc(eng.transpose(pCT[:, h + 128 * j:h + 128 * (j + 1)],
                                  sAT[:, SEQ * u + 128 * j:SEQ * u + 128 * (j + 1)],
                                  sidB[:]))

        def t2(u, eng, inc):
            h = 2048 * (u % 2)
            for j in range(NF):
                inc(eng.tensor_copy(
                    out=sAF[:, TPC * j + 128 * u:TPC * j + 128 * (u + 1)],
                    in_=pCT[:, h + 128 * j:h + 128 * (j + 1)]))

        pipe(ph, 8, [("tensor", 0, t1, NF), ("vector", 1, t2, NF)])

        for tp in range(2):
            for cP in range(4):
                def w1(j, eng, inc, cP=cP):
                    inc(eng.dma_start(out=swf[:, 512 * (j % 2):512 * (j % 2) + 512],
                                      in_=woT[128 * j:128 * (j + 1),
                                              512 * cP:512 * (cP + 1)]), 16)

                def w2(j, eng, inc):
                    s = j % 2
                    inc(eng.activation(out=swb[:, 512 * s:512 * (s + 1)],
                                       in_=swf[:, 512 * s:512 * (s + 1)],
                                       func=AF.Copy))

                def w3(j, eng, inc, tp=tp, cP=cP):
                    s = j % 2
                    for t4 in range(4):
                        t = 4 * tp + t4
                        mm = eng.matmul(pS[:, 512 * t4:512 * (t4 + 1)],
                                        sAF[:, TPC * j + 128 * t:TPC * j + 128 * (t + 1)],
                                        swb[:, 512 * s:512 * (s + 1)],
                                        start=(j == 0), stop=False)
                    if j == NF - 1:
                        for t4 in range(4):
                            mm = eng.matmul(pS[:, 512 * t4:512 * (t4 + 1)],
                                            sonerow[:],
                                            sbob[0:1, 512 * cP:512 * (cP + 1)],
                                            start=False, stop=True)
                    inc(mm)

                pipe(ph, NF, [("sync", 0, w1, 16), ("scalar", 1, w2, 1),
                              ("tensor", 2, w3, 1)])

                def w4(t4, eng, inc, tp=tp, cP=cP):
                    t = 4 * tp + t4
                    inc(eng.dma_start(out=sxs[:, 512 * (t4 % 2):512 * (t4 % 2) + 512],
                                      in_=x1[128 * t:128 * (t + 1),
                                             512 * cP:512 * (cP + 1)]), 16)

                def w5(t4, eng, inc):
                    inc(eng.tensor_add(out=sout[:, 512 * (t4 % 2):512 * (t4 % 2) + 512],
                                       in0=sxs[:, 512 * (t4 % 2):512 * (t4 % 2) + 512],
                                       in1=pS[:, 512 * t4:512 * (t4 + 1)]))

                def w6(t4, eng, inc, tp=tp, cP=cP):
                    t = 4 * tp + t4
                    inc(eng.dma_start(out=r2[128 * t:128 * (t + 1),
                                             512 * cP:512 * (cP + 1)],
                                      in_=sout[:, 512 * (t4 % 2):512 * (t4 % 2) + 512]), 16)

                pipe(ph, 4, [("sync", 0, w4, 16), ("vector", 1, w5, 1),
                             ("gpsimd", 2, w6, 16)])



def stage_lut(nc, ph, io, tag="lu"):
    lut_row = io["lut_row"]
    gidx, uidx, didx = io["gidx"], io["uidx"], io["didx"]
    wgt_sh, wut_sh, wd_sh = io["wgt_sh"], io["wut_sh"], io["wd_sh"]
    W = INTER1

    with contextlib.ExitStack() as ctx:
        T = lambda name, shape, dt: ctx.enter_context(
            nc.sbuf_tensor(f"{tag}_{name}", shape, dt))
        sI = T("sI", [128, W], I32)
        sB = T("sB", [128, W], BF16)
        sT = T("sT", [128, W], BF16)
        sAa = T("sAa", [128, W], BF16)
        sAb = T("sAb", [128, W], BF16)
        sI2 = T("sI2", [128, W], I32)
        sB2 = T("sB2", [128, W], BF16)
        sT2 = T("sT2", [128, W], BF16)
        sAc = T("sAc", [128, W], BF16)
        sAd = T("sAd", [128, W], BF16)
        sone = T("sone", [1, 128], F32)
        slr = T("slr", [1, 16], F32)
        slc = T("slc", [128, 16], F32)
        pRep = ctx.enter_context(nc.psum_tensor(f"{tag}_pRep", [128, 16], F32))

        def p0(eng, inc):
            inc(eng.dma_start(out=slr[:], in_=lut_row), 16)
        ph.phase(sync=(p0, 16))

        def p1(eng, inc):
            inc(eng.memset(sone[:], 1.0))
        ph.phase(vector=(p1, 1))

        def p2(eng, inc):
            inc(eng.matmul(pRep[:], sone[:], slr[:], start=True, stop=True))
        ph.phase(tensor=(p2, 1))

        def p3(eng, inc):
            inc(eng.tensor_copy(out=slc[:], in_=pRep[:]))
        ph.phase(vector=(p3, 1))

        jobs = []
        for t in range(2):
            jobs.append((gidx, "gt", 128 * t, W, t))
            jobs.append((uidx, "ut", 128 * t, W, t))
        for t in range(6):
            jobs.append((didx, "d", 128 * t, D, t))
        assert len(jobs) % 2 == 0

        for jp in range(0, len(jobs), 2):
            pair = [jobs[jp], jobs[jp + 1]]
            bufs = [(sI, sB, sT, sAa, sAb), (sI2, sB2, sT2, sAc, sAd)]

            def jl(eng, inc, pair=pair, bufs=bufs):
                for (srcd, kind, r0, Wj, trow), (bI, bB, bT, bA, bBk) in zip(pair, bufs):
                    inc(eng.dma_start(out=bI[:, 0:Wj], in_=srcd[r0:r0 + 128, :]), 16)
            ph.phase(sync=(jl, 32))

            def jc(eng, inc, pair=pair, bufs=bufs):
                for (srcd, kind, r0, Wj, trow), (bI, bB, bT, bA, bBk) in zip(pair, bufs):
                    inc(eng.tensor_copy(out=bB[:, 0:Wj], in_=bI[:, 0:Wj]))
            ph.phase(vector=(jc, 2))

            def j0(eng, inc, pair=pair, bufs=bufs):
                for (srcd, kind, r0, Wj, trow), (bI, bB, bT, bA, bBk) in zip(pair, bufs):
                    inc(eng.tensor_scalar(out=bA[:, 0:Wj], in0=bB[:, 0:Wj],
                                          scalar1=0.0, scalar2=slc[:, 0:1],
                                          op0=ALU.is_equal, op1=ALU.mult))
            ph.phase(vector=(j0, 2))
            curs = [(bufs[0][3], bufs[0][4]), (bufs[1][3], bufs[1][4])]
            for l in range(1, 16):
                def jt(eng, inc, pair=pair, bufs=bufs, l=l, curs=tuple(curs)):
                    for (srcd, kind, r0, Wj, trow), (bI, bB, bT, bA, bBk), (cur, oth)                             in zip(pair, bufs, curs):
                        inc(eng.tensor_scalar(out=bT[:, 0:Wj], in0=bB[:, 0:Wj],
                                              scalar1=float(l), scalar2=slc[:, l:l + 1],
                                              op0=ALU.is_equal, op1=ALU.mult))
                ph.phase(vector=(jt, 2))

                def ja(eng, inc, pair=pair, bufs=bufs, curs=tuple(curs)):
                    for (srcd, kind, r0, Wj, trow), (bI, bB, bT, bA, bBk), (cur, oth)                             in zip(pair, bufs, curs):
                        inc(eng.tensor_add(out=oth[:, 0:Wj], in0=cur[:, 0:Wj],
                                           in1=bT[:, 0:Wj]))
                ph.phase(vector=(ja, 2))
                curs = [(o, c) for (c, o) in curs]

            def js(eng, inc, pair=pair, curs=tuple(curs)):
                for (srcd, kind, r0, Wj, trow), (cur, oth) in zip(pair, curs):
                    if kind == "d":
                        inc(eng.dma_start(out=wd_sh[r0:r0 + 128, :], in_=cur[:, 0:D]), 16)
                    else:
                        dstd = wgt_sh if kind == "gt" else wut_sh
                        for m in range(NI1):
                            r0t = 256 * m + 128 * trow
                            inc(eng.dma_start(out=dstd[r0t:r0t + 128, :],
                                              in_=cur[:, 128 * m:128 * (m + 1)]), 16)
            ns = sum((16 if k == "d" else 16 * NI1) for (_, k, _, _, _) in pair)
            ph.phase(gpsimd=(js, ns))



def stage_ffn1(nc, ph, io, tag="f1"):
    xn1, identB = io["xn1"], io["identB"]
    wgg, wug, wdg = io["wgg"], io["wug"], io["wdg"]
    r2, outd = io["r2"], io["outd"]

    with contextlib.ExitStack() as ctx:
        T = lambda name, shape, dt: ctx.enter_context(
            nc.sbuf_tensor(f"{tag}_{name}", shape, dt))
        sXnF = T("sXnF", [128, NF * TPC], BF16)
        sxt = T("sxt", [128, 2 * D], F32)
        sxb = T("sxb", [128, 2 * D], BF16)
        sidB = T("sidB", [128, 128], BF16)
        sH1 = T("sH1", [128, NI1 * TPC], BF16)
        swg = T("swg", [128, 2 * D], BF16)
        swu = T("swu", [128, 2 * D], BF16)
        ssil = T("ssil", [128, 2 * 512], F32)
        sup = T("sup", [128, 2 * 512], F32)
        swd = T("swd", [128, 8 * 512], BF16)
        sxs = T("sxs", [128, 2 * 512], F32)
        sout = T("sout", [128, 2 * 512], F32)
        ptr2 = ctx.enter_context(nc.psum_tensor(f"{tag}_ptr2", [128, 4096], BF16))
        pBig = ctx.enter_context(nc.psum_tensor(f"{tag}_pBig", [128, 2048], F32))

        def p0(eng, inc):
            inc(eng.dma_start(out=sidB[:], in_=identB), 16)
        ph.phase(sync=(p0, 16))

        pipe_tm2fm(ph, xn1, sXnF, sxt, sxb, ptr2, sidB)

        def L(i, eng, inc):
            m = i // 2
            if i % 2 == 0:
                for a in range(NF // 2):
                    r0 = 11008 * a + 256 * m
                    inc(eng.dma_start(
                        out=swg[:, D * (m % 2) + 256 * a:D * (m % 2) + 256 * (a + 1)],
                        in_=wgg[r0:r0 + 256, :].rearrange("(j p) c -> p j c", p=128)), 16)

        def LU(i, eng, inc):
            m = i // 2
            if i % 2 == 0:
                for a in range(NF // 2):
                    r0 = 11008 * a + 256 * m
                    inc(eng.dma_start(
                        out=swu[:, D * (m % 2) + 256 * a:D * (m % 2) + 256 * (a + 1)],
                        in_=wug[r0:r0 + 256, :].rearrange("(j p) c -> p j c", p=128)), 16)

        def M(i, eng, inc):
            m, h = i // 2, i % 2
            sl = 1024 * (i % 2 == 1)
            sl = 1024 * (i % 2)
            for j in range(NF):
                mm = eng.matmul(pBig[:, sl:sl + 512],
                                swg[:, D * (m % 2) + 128 * j:D * (m % 2) + 128 * (j + 1)],
                                sXnF[:, TPC * j + 512 * h:TPC * j + 512 * (h + 1)],
                                start=(j == 0), stop=(j == NF - 1))
            inc(mm)
            for j in range(NF):
                mm = eng.matmul(pBig[:, sl + 512:sl + 1024],
                                swu[:, D * (m % 2) + 128 * j:D * (m % 2) + 128 * (j + 1)],
                                sXnF[:, TPC * j + 512 * h:TPC * j + 512 * (h + 1)],
                                start=(j == 0), stop=(j == NF - 1))
            inc(mm)

        def SI(i, eng, inc):
            sl = 1024 * (i % 2)
            so = 512 * (i % 2)
            eng.activation(out=ssil[:, so:so + 512], in_=pBig[:, sl:sl + 512],
                           func=AF.Silu)
            inc(eng.activation(out=sup[:, so:so + 512],
                               in_=pBig[:, sl + 512:sl + 1024], func=AF.Copy), 2)

        def MU(i, eng, inc):
            m, h = i // 2, i % 2
            so = 512 * (i % 2)
            inc(eng.tensor_mul(out=sH1[:, TPC * m + 512 * h:TPC * m + 512 * (h + 1)],
                               in0=ssil[:, so:so + 512], in1=sup[:, so:so + 512]))

        pipe(ph, 2 * NI1, [("sync", 0, L, lambda i: 16 * 8 if i % 2 == 0 else 0),
                           ("gpsimd", 0, LU, lambda i: 16 * 8 if i % 2 == 0 else 0),
                           ("tensor", 1, M, 2), ("scalar", 2, SI, 2),
                           ("vector", 3, MU, 1)])

        NG = 11
        for tp in range(2):
            for cP in range(4):
                def L2(g, eng, inc, cP=cP):
                    lo = 4 * g
                    for q in range(min(4, NI1 - lo)):
                        mj = lo + q
                        inc(eng.dma_start(
                            out=swd[:, 2048 * (g % 2) + 512 * q:
                                     2048 * (g % 2) + 512 * (q + 1)],
                            in_=wdg[128 * mj:128 * (mj + 1),
                                    512 * cP:512 * (cP + 1)]), 16)

                def M2(g, eng, inc, tp=tp):
                    lo = 4 * g
                    for q in range(min(4, NI1 - lo)):
                        mj = lo + q
                        for t4 in range(4):
                            t = 4 * tp + t4
                            mm = eng.matmul(pBig[:, 512 * t4:512 * (t4 + 1)],
                                            sH1[:, TPC * mj + 128 * t:
                                                 TPC * mj + 128 * (t + 1)],
                                            swd[:, 2048 * (g % 2) + 512 * q:
                                                 2048 * (g % 2) + 512 * (q + 1)],
                                            start=(mj == 0), stop=(mj == NI1 - 1))
                    inc(mm)

                pipe(ph, NG, [("sync", 0, L2, lambda g: 16 * min(4, NI1 - 4 * g)),
                              ("tensor", 1, M2, 1)])

                def EV(t4, eng, inc, tp=tp, cP=cP):
                    t = 4 * tp + t4
                    inc(eng.dma_start(out=sxs[:, 512 * (t4 % 2):512 * (t4 % 2) + 512],
                                      in_=r2[128 * t:128 * (t + 1),
                                             512 * cP:512 * (cP + 1)]), 16)

                def AD(t4, eng, inc):
                    inc(eng.tensor_add(out=sout[:, 512 * (t4 % 2):512 * (t4 % 2) + 512],
                                       in0=sxs[:, 512 * (t4 % 2):512 * (t4 % 2) + 512],
                                       in1=pBig[:, 512 * t4:512 * (t4 + 1)]))

                def ST(t4, eng, inc, tp=tp, cP=cP):
                    t = 4 * tp + t4
                    inc(eng.dma_start(out=outd[128 * t:128 * (t + 1),
                                               512 * cP:512 * (cP + 1)],
                                      in_=sout[:, 512 * (t4 % 2):512 * (t4 % 2) + 512]), 16)

                pipe(ph, 4, [("sync", 0, EV, 16), ("vector", 1, AD, 1),
                             ("gpsimd", 2, ST, 16)])



def prog_fused():
    nc = bass.Bass("TRN2", target_bir_lowering=False)
    EI, EO, IN = "ExternalInput", "ExternalOutput", "Internal"

    def dr(name, shape, dt, kind):
        return nc.dram_tensor(name, shape, dt, kind=kind).ap()

    x = dr("x", [TPC, D], F32, EI)
    wq0 = dr("wq0", [NF, D, 128], F32, EI)
    wk0 = dr("wk0", [NF, D, 128], F32, EI)
    wv0 = dr("wv0", [NF, D, 128], F32, EI)
    cosT = dr("cosT", [TPC, TPC], F32, EI)
    sinT = dr("sinT", [TPC, TPC], F32, EI)
    wfc_sh = dr("wfc_sh", [256, FFI], F32, EI)
    wproj_sh = dr("wproj_sh", [1024, D], F32, EI)
    woT0 = dr("woT0", [D, D], F32, EI)
    wq1 = dr("wq1", [NF, D, 128], F32, EI)
    wk1 = dr("wk1", [NF, D, 128], F32, EI)
    wv1 = dr("wv1", [NF, D, 128], F32, EI)
    woT1 = dr("woT1", [D, D], F32, EI)
    bo_row = dr("bo_row", [1, D], F32, EI)
    gate_row = dr("gate_row", [1, 16], F32, EI)
    sel_row = dr("sel_row", [1, 2], F32, EI)
    lut_row = dr("lut_row", [1, 16], F32, EI)
    gidx = dr("gidx", [256, INTER1], I32, EI)
    uidx = dr("uidx", [256, INTER1], I32, EI)
    didx = dr("didx", [768, D], I32, EI)
    identB = dr("identB", [128, 128], BF16, EI)
    identF = dr("identF", [128, 128], F32, EI)
    outd = dr("outd", [TPC, D], F32, EO)

    xn = dr("xn", [TPC, D], F32, IN)
    maxes = dr("maxes", [1, 3], F32, IN)
    gmax = dr("gmax", [1, 3], F32, IN)
    qf = dr("qf", [D, TPC], BF16, IN)
    kf = dr("kf", [D, TPC], BF16, IN)
    vf = dr("vf", [D, TPC], BF16, IN)
    qg = dr("qg", [2 * D, TPC], BF16, IN)
    kg = dr("kg", [2 * D, TPC], BF16, IN)
    vg = dr("vg", [2 * D, TPC], BF16, IN)
    x8f = dr("x8f", [D, TPC], BF16, IN)
    wfc8t_sh = dr("wfc8t_sh", [NFI * 256, 128], BF16, IN)
    wfc8g = dr("wfc8g", [8 * NFI * 256, 128], BF16, IN)
    wproj8_sh = dr("wproj8_sh", [1024, D], BF16, IN)
    wproj8g = dr("wproj8g", [FFI, D], BF16, IN)
    combh0 = dr("combh0", [TPC, TPC], BF16, IN)
    combh1 = dr("combh1", [TPC, TPC], BF16, IN)
    cgA = dr("cgA", [D, TPC], BF16, IN)
    cgB = dr("cgB", [D, TPC], BF16, IN)
    hbuf = dr("hbuf", [FFI, TPC], F32, IN)
    hmax = dr("hmax", [1, 1], F32, IN)
    ghmax = dr("ghmax", [1, 1], F32, IN)
    x1 = dr("x1", [TPC, D], F32, IN)
    xn1 = dr("xn1", [TPC, D], F32, IN)
    q1 = dr("q1", [D, TPC], BF16, IN)
    k1 = dr("k1", [D, TPC], BF16, IN)
    v1 = dr("v1", [D, TPC], BF16, IN)
    k1g = dr("k1g", [2 * D, TPC], BF16, IN)
    v1g = dr("v1g", [2 * D, TPC], BF16, IN)
    v1t = dr("v1t", [SEQ, D], BF16, IN)
    r2 = dr("r2", [TPC, D], F32, IN)
    wgt_sh = dr("wgt_sh", [NI1 * 256, 128], BF16, IN)
    wut_sh = dr("wut_sh", [NI1 * 256, 128], BF16, IN)
    wgg = dr("wgg", [8 * NI1 * 256, 128], BF16, IN)
    wug = dr("wug", [8 * NI1 * 256, 128], BF16, IN)
    wd_sh = dr("wd_sh", [768, D], BF16, IN)
    wdg = dr("wdg", [6144, D], BF16, IN)

    with contextlib.ExitStack() as ctx:
        ph = Phaser(nc, ctx)

        stage_ln(nc, ph, {"x": x, "xn": xn, "identF": identF,
                          "wfc_sh": wfc_sh, "wproj_sh": wproj_sh,
                          "maxes": maxes}, with_maxes=True, tag="l0")
        coll_phase(ph, "AllReduce", ALU.max, GROUP8, [(maxes, gmax)])
        stage_prep0(nc, ph, {
            "xn": xn, "gmax": gmax, "wq": wq0, "wk": wk0, "wv": wv0,
            "cosT": cosT, "sinT": sinT, "wfc_sh": wfc_sh, "wproj_sh": wproj_sh,
            "identB": identB, "qf": qf, "kf": kf, "vf": vf, "x8f": x8f,
            "wfc8t_sh": wfc8t_sh, "wproj8_sh": wproj8_sh})
        coll_phase(ph, "AllGather", ALU.bypass, PAIRS,
                   [(qf, qg), (kf, kg), (vf, vg)])
        coll_phase(ph, "AllGather", ALU.bypass, GROUP8,
                   [(wfc8t_sh, wfc8g), (wproj8_sh, wproj8g)])
        stage_fc0(nc, ph, {"x8f": x8f, "wfc8g": wfc8g, "gmax": gmax,
                           "identF": identF, "hbuf": hbuf, "hmax": hmax})
        coll_phase(ph, "AllReduce", ALU.max, GROUP8, [(hmax, ghmax)])
        stage_attn0(nc, ph, {
            "qg": qg, "kg": kg, "vg": vg, "gate_row": gate_row,
            "sel_row": sel_row, "identB": identB, "identF": identF,
            "combh0": combh0, "combh1": combh1})
        coll_phase(ph, "AllGather", ALU.bypass, PAIRS,
                   [(combh0, cgA), (combh1, cgB)])
        stage_proj0(nc, ph, {
            "x": x, "combA": cgA, "combB": cgB, "woT": woT0, "hbuf": hbuf,
            "wproj8g": wproj8g, "gmax": gmax, "ghmax": ghmax,
            "sel_row": sel_row, "x1": x1})
        stage_ln(nc, ph, {"x": x1, "xn": xn1, "identF": identF},
                 with_maxes=False, tag="l1")
        stage_qkv1(nc, ph, {"xn1": xn1, "identB": identB, "wq": wq1,
                            "wk": wk1, "wv": wv1, "q1": q1, "k1": k1, "v1": v1})
        coll_phase(ph, "AllGather", ALU.bypass, PAIRS, [(k1, k1g), (v1, v1g)])
        stage_v1t(nc, ph, {"v1g": v1g, "identB": identB, "v1t": v1t})
        stage_attn1(nc, ph, {
            "q1": q1, "k1g": k1g, "v1t": v1t, "woT": woT1, "bo_row": bo_row,
            "x1": x1, "identB": identB, "identF": identF, "r2": r2})
        stage_lut(nc, ph, {"lut_row": lut_row, "gidx": gidx, "uidx": uidx,
                           "didx": didx, "wgt_sh": wgt_sh, "wut_sh": wut_sh,
                           "wd_sh": wd_sh})
        coll_phase(ph, "AllGather", ALU.bypass, GROUP8,
                   [(wgt_sh, wgg), (wut_sh, wug), (wd_sh, wdg)])
        stage_ffn1(nc, ph, {"xn1": xn1, "identB": identB, "wgg": wgg,
                            "wug": wug, "wdg": wdg, "r2": r2, "outd": outd})
        ph.finish()
    return nc



def host_constants():
    c = {}
    c["identB"] = np.eye(128, dtype=np.float32).astype(ml_dtypes.bfloat16)
    c["identF"] = np.eye(128, dtype=np.float32)
    H = np.int32(D)
    inv_freq = (np.float32(1.0) /
                np.power(np.float32(10000.0),
                         np.arange(0, D, 2, dtype=np.float32) / np.float32(H)))
    t = np.arange(SEQ, dtype=np.float32)
    freqs = t[:, None] * inv_freq[None, :]
    cos = np.cos(freqs).astype(np.float32)
    sin = np.sin(freqs).astype(np.float32)
    c["cosT"] = [np.ascontiguousarray(cos[h * TPC:(h + 1) * TPC, :].T) for h in range(2)]
    c["sinT"] = [np.ascontiguousarray(sin[h * TPC:(h + 1) * TPC, :].T) for h in range(2)]
    return c


def _tile_wT(w):
    o, i = w.shape
    return np.ascontiguousarray(w.T.reshape(i, o // 128, 128).transpose(1, 0, 2))


def host_shards(inputs):
    d = {}
    x = np.ascontiguousarray(inputs["L0_attn_hidden_states"]).reshape(8192, D)
    d["x_slab"] = [np.ascontiguousarray(x[TPC * c:TPC * (c + 1)]) for c in range(NCORES)]
    wfcT = np.ascontiguousarray(inputs["L0_ffn_w_fc"].T)
    wprojT = np.ascontiguousarray(inputs["L0_ffn_w_proj"].T)
    d["wfcT_sh"] = [np.ascontiguousarray(wfcT[256 * c:256 * (c + 1)]) for c in range(NCORES)]
    d["wprojT_sh"] = [np.ascontiguousarray(wprojT[1024 * c:1024 * (c + 1)]) for c in range(NCORES)]
    for nm, key in (("wq0", "L0_attn_Wq"), ("wk0", "L0_attn_Wk"), ("wv0", "L0_attn_Wv"),
                    ("wq1", "L1_attn_Wq"), ("wk1", "L1_attn_Wk"), ("wv1", "L1_attn_Wv")):
        d[nm] = _tile_wT(np.ascontiguousarray(inputs[key]))
    d["woT0"] = np.ascontiguousarray(inputs["L0_attn_Wo"].T)
    d["woT1"] = np.ascontiguousarray(inputs["L1_attn_Wo"].T)
    d["bo_row"] = np.ascontiguousarray(inputs["L1_attn_bo"]).reshape(1, D)
    d["gate_row"] = np.ascontiguousarray(inputs["L0_attn_gate"]).reshape(1, 16)
    d["lut_row"] = np.ascontiguousarray(inputs["L1_ffn_lut"]).reshape(1, 16)
    gT = np.ascontiguousarray(inputs["L1_ffn_gate_idx"].T)
    uT = np.ascontiguousarray(inputs["L1_ffn_up_idx"].T)
    dT = np.ascontiguousarray(inputs["L1_ffn_down_idx"].T)
    d["gidxT_sh"] = [np.ascontiguousarray(gT[256 * c:256 * (c + 1)]) for c in range(NCORES)]
    d["uidxT_sh"] = [np.ascontiguousarray(uT[256 * c:256 * (c + 1)]) for c in range(NCORES)]
    dTp = np.concatenate([dT, np.zeros((6144 - INTER1, D), np.int32)], axis=0)
    d["didxT_sh"] = [np.ascontiguousarray(dTp[768 * c:768 * (c + 1)]) for c in range(NCORES)]
    return d


_PROG_CACHE = {}


def _cached(name, fn, *args):
    key = (name,) + args
    if key not in _PROG_CACHE:
        _PROG_CACHE[key] = fn(*args)
    return _PROG_CACHE[key]


def kernel(**inputs):
    consts = host_constants()
    shards = host_shards(inputs)
    nc = _cached("fused", prog_fused)
    in_maps = []
    for c in range(NCORES):
        p = c % 2
        gr = shards["gate_row"]
        if p == 1:
            gr = np.ascontiguousarray(np.concatenate([gr[:, 8:], gr[:, :8]], axis=1))
        in_maps.append({
            "x": shards["x_slab"][c],
            "wq0": shards["wq0"], "wk0": shards["wk0"], "wv0": shards["wv0"],
            "cosT": consts["cosT"][p], "sinT": consts["sinT"][p],
            "wfc_sh": shards["wfcT_sh"][c], "wproj_sh": shards["wprojT_sh"][c],
            "woT0": shards["woT0"],
            "wq1": shards["wq1"], "wk1": shards["wk1"], "wv1": shards["wv1"],
            "woT1": shards["woT1"], "bo_row": shards["bo_row"],
            "gate_row": gr,
            "sel_row": np.array([[1.0 - p, float(p)]], np.float32),
            "lut_row": shards["lut_row"],
            "gidx": shards["gidxT_sh"][c], "uidx": shards["uidxT_sh"][c],
            "didx": shards["didxT_sh"][c],
            "identB": consts["identB"], "identF": consts["identF"],
        })
    res = run_bass_kernel_spmd(nc, in_maps, core_ids=list(range(NCORES)))
    out = np.concatenate([res.results[c]["outd"] for c in range(NCORES)], axis=0)
    return out.reshape(4, SEQ, D)


# version 06
# speedup vs baseline: 1.1455x; 1.1455x over previous
import contextlib
import numpy as np
import ml_dtypes
import concourse.bass as bass
import concourse.mybir as mybir
from concourse.bass_utils import run_bass_kernel_spmd

F32 = mybir.dt.float32
BF16 = mybir.dt.bfloat16
FP8 = mybir.dt.float8e4
I32 = mybir.dt.int32
U8 = mybir.dt.uint8
AF = mybir.ActivationFunctionType
ALU = mybir.AluOpType
AX = mybir.AxisListType

NCORES = 8
D = 2048
SEQ = 2048
TPC = 1024
NF = D // 128
NT = TPC // 128
EPS = 1e-5
FP8MAX = 448.0
INTER1 = 5504
NI1 = INTER1 // 128
FFI = 8192
NFI = FFI // 128
SM_SCALE0 = float(np.float32(1.0) / np.sqrt(np.float32(128.0)))
SM_SCALE1 = float(np.float32(1.0) / np.sqrt(np.float32(2048.0)))

ENGINES = ("sync", "scalar", "vector", "tensor", "gpsimd")
GROUP8 = [[0, 1, 2, 3, 4, 5, 6, 7]]
PAIRS = [[0, 1], [2, 3], [4, 5], [6, 7]]


class Phaser:

    def __init__(self, nc, ctx):
        self.nc = nc
        self.ctx = ctx
        self.block = ctx.enter_context(nc.Block())
        self.bar = ctx.enter_context(nc.semaphore("bar0"))
        self.cum = 0
        self.epoch = 0
        self.fences = []
        self.engine_epoch = {e: 0 for e in ENGINES}

    def _catchup(self, eng, ename):
        while self.engine_epoch[ename] < self.epoch:
            s, tot = self.fences[self.engine_epoch[ename]]
            eng.wait_ge(s, tot)
            self.engine_epoch[ename] += 1

    def phase(self, **bodies):
        for ename, (emit, n_decl) in bodies.items():
            def body(eng, emit=emit, n_decl=n_decl, ename=ename):
                self._catchup(eng, ename)
                if self.cum > 0:
                    eng.wait_ge(self.bar, self.cum)
                got = [0]

                def inc(instr, amt=1):
                    instr.then_inc(self.bar, amt)
                    got[0] += amt
                    return instr

                emit(eng, inc)
                assert got[0] == n_decl, (
                    f"engine {ename}: declared {n_decl} incs, got {got[0]}")
            getattr(self.block, ename)(body)
        self.cum += sum(n for (_, n) in bodies.values())
        if self.cum > 40000:
            self.fences.append((self.bar, self.cum))
            self.bar = self.ctx.enter_context(
                self.nc.semaphore(f"bar{len(self.fences)}"))
            self.cum = 0
            self.epoch += 1

    def finish(self):
        def body(eng):
            self._catchup(eng, "gpsimd")
            if self.cum > 0:
                eng.wait_ge(self.bar, self.cum)
        self.block.gpsimd(body)


def _dma_tiles(eng, inc, sb, dram, rows, cols, to_sbuf=True, row0=0, col0=0):
    for k in range(rows // 128):
        s_ap = sb[:, col0 + cols * k: col0 + cols * (k + 1)]
        d_ap = dram[row0 + 128 * k: row0 + 128 * (k + 1), :]
        if to_sbuf:
            inc(eng.dma_start(out=s_ap, in_=d_ap), 16)
        else:
            inc(eng.dma_start(out=d_ap, in_=s_ap), 16)


def coll_phase(ph, kind, op, groups, pairs_io):
    def body(eng, inc):
        for (i_ap, o_ap) in pairs_io:
            inc(eng.collective_compute(kind, op, replica_groups=groups,
                                       ins=[i_ap.opt()], outs=[o_ap.opt()]), 1)
    ph.phase(gpsimd=(body, len(pairs_io)))


def pipe(ph, n, stages):
    maxd = max(d for (_, d, _, _) in stages)
    for t in range(n + maxd):
        todo = {}
        for (ename, d, fn, k) in stages:
            i = t - d
            if 0 <= i < n:
                kv = k(i) if callable(k) else k
                if kv:
                    todo.setdefault(ename, []).append((fn, i, kv))
        bodies = {}
        for ename, items in todo.items():
            def emit(eng, inc, items=items):
                for (fn, i, kv) in items:
                    fn(i, eng, inc)
            bodies[ename] = (emit, sum(kv for (_, _, kv) in items))
        if bodies:
            ph.phase(**bodies)


def pipe_tm2fm(ph, src_dram, sXnF, sxt, sxb, ptr2, sidB):
    def L(k, eng, inc):
        inc(eng.dma_start(out=sxt[:, D * (k % 2):D * (k % 2) + D],
                          in_=src_dram[128 * k:128 * (k + 1), :]), 16)

    def C(k, eng, inc):
        inc(eng.activation(out=sxb[:, D * (k % 2):D * (k % 2) + D],
                           in_=sxt[:, D * (k % 2):D * (k % 2) + D], func=AF.Copy))

    def Tr(k, eng, inc):
        h = 2048 * (k % 2)
        for j in range(NF):
            inc(eng.transpose(ptr2[:, h + 128 * j:h + 128 * (j + 1)],
                              sxb[:, D * (k % 2) + 128 * j:D * (k % 2) + 128 * (j + 1)],
                              sidB[:]))

    def Ev(k, eng, inc):
        h = 2048 * (k % 2)
        for j in range(NF):
            inc(eng.tensor_copy(
                out=sXnF[:, TPC * j + 128 * k:TPC * j + 128 * (k + 1)],
                in_=ptr2[:, h + 128 * j:h + 128 * (j + 1)]))

    pipe(ph, NT, [("sync", 0, L, 16), ("scalar", 1, C, 1),
                  ("tensor", 2, Tr, NF), ("vector", 3, Ev, NF)])



def stage_ln(nc, ph, io, with_maxes, tag):
    x, xn, identF = io["x"], io["xn"], io["identF"]
    with contextlib.ExitStack() as ctx:
        T = lambda name, shape, dt: ctx.enter_context(
            nc.sbuf_tensor(f"{tag}_{name}", shape, dt))
        sX = T("sX", [128, NT * D], F32)
        sXn = T("sXn", [128, NT * D], F32)
        sjunk = T("sjunk", [128, D], F32)
        sid = T("sid", [128, 128], F32)
        sms = T("sms", [128, NT], F32)
        sqs = T("sqs", [128, NT], F32)
        smean = T("smean", [128, NT], F32)
        stmp = T("stmp", [128, NT], F32)
        svar = T("svar", [128, NT], F32)
        srstd = T("srstd", [128, NT], F32)
        sbias = T("sbias", [128, NT], F32)
        sxm = T("sxm", [128, NT], F32)
        swm = T("swm", [128, 16], F32)
        scols = T("scols", [128, 3], F32)
        sred = T("sred", [3, 128], F32)
        sout3 = T("sout3", [3, 1], F32)
        srow3 = T("srow3", [1, 3], F32)
        seps = T("seps", [128, 1], F32)
        if with_maxes:
            swA = T("swA", [128, FFI], F32)
        pt3 = ctx.enter_context(nc.psum_tensor(f"{tag}_pt3", [3, 128], F32))

        def p0(eng, inc):
            inc(eng.dma_start(out=sid[:], in_=identF), 16)
            _dma_tiles(eng, inc, sX, x, TPC, D)
        ph.phase(sync=(p0, 16 * (1 + NT)))

        def p1s(eng, inc):
            for k in range(NT):
                inc(eng.activation(out=sjunk[:], in_=sX[:, D * k:D * (k + 1)],
                                   func=AF.Square, accum_out=sqs[:, k:k + 1]))

        def p1v(eng, inc):
            eng.memset(seps[:], EPS)
            for k in range(NT):
                inc(eng.tensor_reduce(out=sms[:, k:k + 1],
                                      in_=sX[:, D * k:D * (k + 1)],
                                      axis=AX.X, op=ALU.add))
        ph.phase(scalar=(p1s, NT), vector=(p1v, NT))

        if with_maxes:
            wjobs = [(io["wfc_sh"], 0, FFI), (io["wfc_sh"], 128, FFI)] + \
                    [(io["wproj_sh"], 128 * j, D) for j in range(8)]
            for wi, (wt, r0, cw) in enumerate(wjobs):
                def wl(eng, inc, wt=wt, r0=r0, cw=cw):
                    inc(eng.dma_start(out=swA[:, 0:cw], in_=wt[r0:r0 + 128, :]), 16)
                ph.phase(sync=(wl, 16))

                def wr(eng, inc, wi=wi, cw=cw):
                    inc(eng.tensor_reduce(out=swm[:, wi:wi + 1], in_=swA[:, 0:cw],
                                          axis=AX.X, op=ALU.max,
                                          apply_absolute_value=True))
                ph.phase(vector=(wr, 1))

        def p2v(eng, inc):
            inc(eng.tensor_scalar(out=smean[:], in0=sms[:], scalar1=1.0 / D,
                                  scalar2=None, op0=ALU.mult))
        ph.phase(vector=(p2v, 1))

        def p2b(eng, inc):
            inc(eng.activation(out=stmp[:], in_=smean[:], func=AF.Square))
        ph.phase(scalar=(p2b, 1))

        def p2c(eng, inc):
            inc(eng.scalar_tensor_tensor(out=svar[:], in0=sqs[:], scalar=1.0 / D,
                                         in1=stmp[:], op0=ALU.mult,
                                         op1=ALU.subtract))
        ph.phase(vector=(p2c, 1))

        def p3s(eng, inc):
            inc(eng.activation(out=srstd[:], in_=svar[:], func=AF.Sqrt,
                               bias=seps[:, 0:1]))
        ph.phase(scalar=(p3s, 1))

        def p4a(eng, inc):
            inc(eng.reciprocal(out=srstd[:], in_=srstd[:]))
        ph.phase(vector=(p4a, 1))

        def p4b(eng, inc):
            inc(eng.scalar_tensor_tensor(out=sbias[:], in0=smean[:], scalar=-1.0,
                                         in1=srstd[:], op0=ALU.mult, op1=ALU.mult))
        ph.phase(vector=(p4b, 1))

        def p5(eng, inc):
            for k in range(NT):
                inc(eng.activation(out=sXn[:, D * k:D * (k + 1)],
                                   in_=sX[:, D * k:D * (k + 1)], func=AF.Identity,
                                   bias=sbias[:, k:k + 1], scale=srstd[:, k:k + 1]))
        ph.phase(scalar=(p5, NT))

        def p6(eng, inc):
            for k in range(NT):
                inc(eng.tensor_reduce(out=sxm[:, k:k + 1],
                                      in_=sXn[:, D * k:D * (k + 1)], axis=AX.X,
                                      op=ALU.max, apply_absolute_value=True))
        ph.phase(vector=(p6, NT))

        def p7(eng, inc):
            eng.tensor_reduce(out=scols[:, 0:1], in_=sxm[:, 0:NT], axis=AX.X,
                              op=ALU.max)
            if with_maxes:
                eng.tensor_reduce(out=scols[:, 1:2], in_=swm[:, 0:2], axis=AX.X,
                                  op=ALU.max)
                inc(eng.tensor_reduce(out=scols[:, 2:3], in_=swm[:, 2:10],
                                      axis=AX.X, op=ALU.max))
            else:
                inc(eng.memset(scols[:, 1:3], 0.0))
        ph.phase(vector=(p7, 1))

        def p8(eng, inc):
            inc(eng.transpose(pt3[:], scols[:], sid[:]))
        ph.phase(tensor=(p8, 1))

        def p9(eng, inc):
            inc(eng.tensor_copy(out=sred[:], in_=pt3[:]))
        ph.phase(vector=(p9, 1))

        def p10(eng, inc):
            inc(eng.tensor_reduce(out=sout3[:], in_=sred[:], axis=AX.X, op=ALU.max))
        ph.phase(vector=(p10, 1))

        if with_maxes:
            def p11(eng, inc):
                inc(eng.transpose(pt3[0:1, 0:3], sout3[:], sid[0:3, 0:3]))
            ph.phase(tensor=(p11, 1))

            def p12(eng, inc):
                inc(eng.tensor_copy(out=srow3[:], in_=pt3[0:1, 0:3]))
            ph.phase(vector=(p12, 1))

            def p13(eng, inc):
                inc(eng.dma_start(out=io["maxes"], in_=srow3[:]), 16)
            ph.phase(gpsimd=(p13, 16))

        def pst(eng, inc):
            _dma_tiles(eng, inc, sXn, xn, TPC, D, to_sbuf=False)
        ph.phase(gpsimd=(pst, 16 * NT))



def stage_prep0(nc, ph, io, tag="pp"):
    xn, gmax = io["xn"], io["gmax"]
    wq, wk, wv = io["wq"], io["wk"], io["wv"]
    cosT, sinT = io["cosT"], io["sinT"]
    wfc_sh, wproj_sh = io["wfc_sh"], io["wproj_sh"]
    identB = io["identB"]
    qf, kf, vf, x8f = io["qf"], io["kf"], io["vf"], io["x8f"]
    wfc8t_sh, wproj8_sh = io["wfc8t_sh"], io["wproj8_sh"]

    with contextlib.ExitStack() as ctx:
        T = lambda name, shape, dt: ctx.enter_context(
            nc.sbuf_tensor(f"{tag}_{name}", shape, dt))
        sXnF = T("sXnF", [128, NF * TPC], BF16)
        sxt = T("sxt", [128, 2 * D], F32)
        sxb = T("sxb", [128, 2 * D], BF16)
        str_ = T("str", [128, NF * 128], BF16)
        srow9 = T("srow9", [1, 16], F32)
        sone = T("sone", [1, 128], F32)
        scols9 = T("scols9", [128, 12], F32)
        qf8a = T("qf8a", [128, D], FP8)
        qf8b = T("qf8b", [128, D], FP8)
        qua = T("qua", [128, D], BF16)
        qub = T("qub", [128, D], BF16)
        qmk = T("qmk", [128, D], U8)
        qmk2 = T("qmk2", [128, D], U8)
        qout = T("qout", [128, D], BF16)
        sid = T("sid", [128, 128], BF16)
        swf = T("swf", [128, 4 * D], F32)
        swb = T("swb", [128, 4 * D], BF16)
        scs = T("scs", [128, 2 * TPC], F32)
        ssn = T("ssn", [128, 2 * TPC], F32)
        st1 = T("st1", [128, 512], F32)
        st2 = T("st2", [128, 512], F32)
        sQa = T("sQa", [128, 2 * 512], BF16)
        sQb = T("sQb", [128, 2 * 512], BF16)
        sgm = T("sgm", [1, 3], F32)
        p9psum = ctx.enter_context(nc.psum_tensor(f"{tag}_p9", [128, 9], F32))
        p9 = p9psum[:]

        def p0(eng, inc):
            inc(eng.dma_start(out=sid[:], in_=identB), 16)
            inc(eng.dma_start(out=sgm[:], in_=gmax), 16)
        ph.phase(sync=(p0, 32))

        def p2(eng, inc):
            eng.memset(sone[:], 1.0)
            for j in range(3):
                inc(eng.tensor_scalar(out=srow9[0:1, 12 + j:13 + j],
                                      in0=sgm[0:1, j:j + 1],
                                      scalar1=1.0 / FP8MAX, scalar2=1e-6,
                                      op0=ALU.mult, op1=ALU.max))
        ph.phase(vector=(p2, 3))

        def p2x(eng, inc):
            for j in range(3):
                sc = srow9[0:1, 12 + j:13 + j]
                inc(eng.tensor_scalar(out=srow9[0:1, 3 * j + 2:3 * j + 3], in0=sc,
                                      scalar1=240.0, scalar2=None, op0=ALU.mult))
        ph.phase(vector=(p2x, 3))

        def p3(eng, inc):
            for j in range(3):
                inc(eng.reciprocal(out=srow9[0:1, 3 * j:3 * j + 1],
                                   in_=srow9[0:1, 12 + j:13 + j]))
        ph.phase(vector=(p3, 3))

        def p4(eng, inc):
            for j in range(3):
                inc(eng.tensor_scalar(out=srow9[0:1, 3 * j + 1:3 * j + 2],
                                      in0=srow9[0:1, 3 * j:3 * j + 1],
                                      scalar1=0.5, scalar2=None, op0=ALU.mult))
        ph.phase(vector=(p4, 3))

        def p5(eng, inc):
            inc(eng.matmul(p9, sone[:], srow9[0:1, 0:9], start=True, stop=True))
        ph.phase(tensor=(p5, 1))

        def p6(eng, inc):
            inc(eng.tensor_copy(out=scols9[:, 0:9], in_=p9))
        ph.phase(vector=(p6, 1))

        def p6b(eng, inc):
            for j in range(3):
                inc(eng.tensor_scalar(out=scols9[:, 9 + j:10 + j],
                                      in0=scols9[:, 3 * j + 2:3 * j + 3],
                                      scalar1=-1.0, scalar2=None, op0=ALU.mult))
        ph.phase(vector=(p6b, 3))

        def qc_of(j):
            return {"inv": scols9[:, 3 * j:3 * j + 1],
                    "invh": scols9[:, 3 * j + 1:3 * j + 2],
                    "thr": scols9[:, 3 * j + 2:3 * j + 3],
                    "nthr": scols9[:, 9 + j:10 + j]}

        def quant_phases(src_ap, qi, W=D):
            def pact(eng, inc):
                qc = qc_of(qi)
                eng.activation(out=qf8a[:, 0:W], in_=src_ap, func=AF.Identity,
                               scale=qc["inv"])
                inc(eng.activation(out=qua[:, 0:W], in_=qf8a[:, 0:W], func=AF.Copy))
                eng.activation(out=qf8b[:, 0:W], in_=src_ap, func=AF.Identity,
                               scale=qc["invh"])
                inc(eng.activation(out=qub[:, 0:W], in_=qf8b[:, 0:W], func=AF.Copy,
                                   scale=2.0))
            ph.phase(scalar=(pact, 2))

            def pm1(eng, inc):
                qc = qc_of(qi)
                inc(eng.tensor_scalar(out=qmk[:, 0:W], in0=src_ap,
                                      scalar1=qc["thr"], scalar2=None, op0=ALU.is_ge))
                inc(eng.tensor_scalar(out=qmk2[:, 0:W], in0=src_ap,
                                      scalar1=qc["nthr"], scalar2=None, op0=ALU.is_le))
            ph.phase(vector=(pm1, 2))

            def pm2(eng, inc):
                inc(eng.tensor_tensor(out=qmk[:, 0:W], in0=qmk[:, 0:W],
                                      in1=qmk2[:, 0:W], op=ALU.logical_or))
            ph.phase(vector=(pm2, 1))

            def psel(eng, inc):
                inc(eng.select(out=qout[:, 0:W], mask=qmk[:, 0:W],
                               on_true=qub[:, 0:W], on_false=qua[:, 0:W]))
            ph.phase(vector=(psel, 1))

        with nc.psum_tensor(f"{tag}_ptr2", [128, 4096], BF16) as ptr2:
            pipe_tm2fm(ph, xn, sXnF, sxt, sxb, ptr2, sid)

            for k in range(NT):
                s = k % 2

                def pl(eng, inc, k=k, s=s):
                    inc(eng.dma_start(out=sxt[:, D * s:D * (s + 1)],
                                      in_=xn[128 * k:128 * (k + 1), :]), 16)
                ph.phase(sync=(pl, 16))
                quant_phases(sxt[:, D * s:D * (s + 1)], 0)

                def ptj(eng, inc, k=k):
                    h = 2048 * (k % 2)
                    for j in range(NF):
                        inc(eng.transpose(ptr2[:, h + 128 * j:h + 128 * (j + 1)],
                                          qout[:, 128 * j:128 * (j + 1)], sid[:]))
                ph.phase(tensor=(ptj, NF))

                def pej(eng, inc, k=k):
                    h = 2048 * (k % 2)
                    for j in range(NF):
                        inc(eng.tensor_copy(out=str_[:, 128 * j:128 * (j + 1)],
                                            in_=ptr2[:, h + 128 * j:h + 128 * (j + 1)]))
                ph.phase(vector=(pej, NF))

                def psj(eng, inc, k=k):
                    for j in range(NF):
                        inc(eng.dma_start(
                            out=x8f[128 * j:128 * (j + 1), 128 * k:128 * (k + 1)],
                            in_=str_[:, 128 * j:128 * (j + 1)]), 16)
                ph.phase(gpsimd=(psj, 16 * NF))

            for t in range(2):
                for cchunk in range(4):
                    def pl(eng, inc, t=t, cchunk=cchunk):
                        inc(eng.dma_start(out=sxt[:, 0:D],
                                          in_=wfc_sh[128 * t:128 * (t + 1),
                                                     D * cchunk:D * (cchunk + 1)]), 16)
                    ph.phase(sync=(pl, 16))
                    quant_phases(sxt[:, 0:D], 1)

                    def pst(eng, inc, t=t, cchunk=cchunk):
                        for mm in range(16):
                            m = 16 * cchunk + mm
                            r0 = 256 * m + 128 * t
                            inc(eng.dma_start(out=wfc8t_sh[r0:r0 + 128, :],
                                              in_=qout[:, 128 * mm:128 * (mm + 1)]), 16)
                    ph.phase(gpsimd=(pst, 16 * 16))

            for t in range(8):
                def pl(eng, inc, t=t):
                    inc(eng.dma_start(out=sxt[:, 0:D],
                                      in_=wproj_sh[128 * t:128 * (t + 1), :]), 16)
                ph.phase(sync=(pl, 16))
                quant_phases(sxt[:, 0:D], 2)

                def pst(eng, inc, t=t):
                    inc(eng.dma_start(out=wproj8_sh[128 * t:128 * (t + 1), :],
                                      in_=qout[:, 0:D]), 16)
                ph.phase(gpsimd=(pst, 16))

        pq = ctx.enter_context(nc.psum_tensor(f"{tag}_pq", [128, 2048], F32))
        jobs = [(wq, qf, True), (wk, kf, True), (wv, vf, False)]

        def L(i, eng, inc):
            u = i // 2
            p, w = u // 3, u % 3
            if i % 2 == 1:
                return
            wten = jobs[w][0]
            so = 2 * D * ((i // 2) % 2)
            if w == 0:
                inc(eng.dma_start(out=scs[:, TPC * (p % 2):TPC * (p % 2) + TPC],
                                  in_=cosT[128 * p:128 * (p + 1), :]), 16)
                inc(eng.dma_start(out=ssn[:, TPC * (p % 2):TPC * (p % 2) + TPC],
                                  in_=sinT[128 * p:128 * (p + 1), :]), 16)
            for half, m in enumerate((p, p + 8)):
                inc(eng.dma_start(
                    out=swf[:, so + D * half:so + D * (half + 1)],
                    in_=wten[m].rearrange("(j p) c -> p j c", p=128)), 16)

        def C(i, eng, inc):
            if i % 2 == 1:
                return
            so = 2 * D * ((i // 2) % 2)
            inc(eng.activation(out=swb[:, so:so + 2 * D],
                               in_=swf[:, so:so + 2 * D], func=AF.Copy))

        def M(i, eng, inc):
            u, h = i // 2, i % 2
            so = 2 * D * ((i // 2) % 2)
            sl = 1024 * (i % 2)
            for half in range(2):
                for j in range(NF):
                    mm = eng.matmul(
                        pq[:, sl + 512 * half:sl + 512 * (half + 1)],
                        swb[:, so + D * half + 128 * j:so + D * half + 128 * (j + 1)],
                        sXnF[:, TPC * j + 512 * h:TPC * j + 512 * (h + 1)],
                        start=(j == 0), stop=(j == NF - 1))
                inc(mm)

        def RP(i, eng, inc):
            u, h = i // 2, i % 2
            p, w = u // 3, u % 3
            rope = jobs[w][2]
            sl = 1024 * (i % 2)
            qo = 512 * (i % 2)
            pm = pq[:, sl:sl + 512]
            pm8 = pq[:, sl + 512:sl + 1024]
            if rope:
                cc = scs[:, TPC * (p % 2) + 512 * h:TPC * (p % 2) + 512 * (h + 1)]
                ss = ssn[:, TPC * (p % 2) + 512 * h:TPC * (p % 2) + 512 * (h + 1)]
                eng.tensor_tensor(out=st1[:, 0:512], in0=cc, in1=pm, op=ALU.mult)
                eng.tensor_tensor(out=st2[:, 0:512], in0=ss, in1=pm8, op=ALU.mult)
                inc(eng.tensor_tensor(out=sQa[:, qo:qo + 512], in0=st1[:, 0:512],
                                      in1=st2[:, 0:512], op=ALU.subtract))
                eng.tensor_tensor(out=st1[:, 0:512], in0=cc, in1=pm8, op=ALU.mult)
                eng.tensor_tensor(out=st2[:, 0:512], in0=ss, in1=pm, op=ALU.mult)
                inc(eng.tensor_tensor(out=sQb[:, qo:qo + 512], in0=st1[:, 0:512],
                                      in1=st2[:, 0:512], op=ALU.add))
            else:
                inc(eng.tensor_copy(out=sQa[:, qo:qo + 512], in_=pm))
                inc(eng.tensor_copy(out=sQb[:, qo:qo + 512], in_=pm8))

        def ST(i, eng, inc):
            u, h = i // 2, i % 2
            p, w = u // 3, u % 3
            outd = jobs[w][1]
            qo = 512 * (i % 2)
            inc(eng.dma_start(out=outd[128 * p:128 * (p + 1), 512 * h:512 * (h + 1)],
                              in_=sQa[:, qo:qo + 512]), 16)
            inc(eng.dma_start(out=outd[128 * (p + 8):128 * (p + 9), 512 * h:512 * (h + 1)],
                              in_=sQb[:, qo:qo + 512]), 16)

        pipe(ph, 48, [
            ("sync", 0, L, lambda i: 0 if i % 2 else (16 * 4 if (i // 2) % 3 == 0 else 16 * 2)),
            ("scalar", 1, C, lambda i: 0 if i % 2 else 1),
            ("tensor", 2, M, 2),
            ("vector", 3, RP, 2),
            ("gpsimd", 4, ST, 32),
        ])



def stage_attn0(nc, ph, io, tag="a0"):
    qg, kg, vg = io["qg"], io["kg"], io["vg"]
    gate_row, sel_row = io["gate_row"], io["sel_row"]
    identB, identF = io["identB"], io["identF"]
    combh0, combh1 = io["combh0"], io["combh1"]
    NH = 8

    with contextlib.ExitStack() as ctx:
        T = lambda name, shape, dt: ctx.enter_context(
            nc.sbuf_tensor(f"{tag}_{name}", shape, dt))
        sQH = T("sQH", [128, NH * SEQ], BF16)
        sKH = T("sKH", [128, NH * SEQ], BF16)
        sVH = T("sVH", [128, NH * SEQ], BF16)
        sCMB = T("sCMB", [128, NH * SEQ], BF16)
        sCAND = T("sCAND", [128, 6 * TPC], BF16)
        sidB = T("sidB", [128, 128], BF16)
        sidF = T("sidF", [128, 128], F32)
        sqsc = T("sqsc", [128, SEQ], BF16)
        se1 = T("se1", [128, SEQ], F32)
        se2 = T("se2", [128, SEQ], F32)
        sqf = T("sqf", [128, SEQ], BF16)
        skf = T("skf", [128, SEQ], BF16)
        skfT = T("skfT", [128, SEQ], BF16)
        svT = T("svT", [128, 16 * 129], BF16)
        snt = T("snt", [128, 1], F32)
        smem = T("smem", [128, 129], BF16)
        sneg = T("sneg", [128, 16], F32)
        sneg2 = T("sneg2", [128, 32], F32)
        snegrb = T("snegrb", [16, 128], BF16)
        snegrow = T("snegrow", [1, SEQ], BF16)
        sonerow = T("sonerow", [1, 128], BF16)
        sonerowF = T("sonerowF", [1, 128], F32)
        sPT = T("sPT", [128, 16 * 512], BF16)
        sgr = T("sgr", [1, 40], F32)
        sgc = T("sgc", [128, 16], F32)
        sg1c = T("sg1c", [128, 16], F32)
        ssel = T("ssel", [128, 2], F32)
        srA = T("srA", [128, 4], F32)
        srM = T("srM", [128, 4], F32)
        stA = T("stA", [128, 512], F32)
        stM = T("stM", [128, 512], F32)
        sCT = T("sCT", [128, 512], BF16)
        pSA = ctx.enter_context(nc.psum_tensor(f"{tag}_pSA", [128, 2048], F32))
        pSS = ctx.enter_context(nc.psum_tensor(f"{tag}_pSS", [128, 4096], BF16))
        pCT = pSS

        def p0(eng, inc):
            inc(eng.dma_start(out=sidB[:], in_=identB), 16)
            inc(eng.dma_start(out=sidF[:], in_=identF), 16)
            inc(eng.dma_start(out=sgr[0:1, 0:16], in_=gate_row), 16)
            inc(eng.dma_start(out=sgr[0:1, 32:34], in_=sel_row), 16)
        ph.phase(sync=(p0, 64))

        def pg1(eng, inc):
            inc(eng.activation(out=sgr[0:1, 16:32], in_=sgr[0:1, 0:16],
                               func=AF.Sigmoid))
        ph.phase(scalar=(pg1, 1))

        def pg2(eng, inc):
            inc(eng.tensor_scalar(out=sgr[0:1, 0:16], in0=sgr[0:1, 16:32],
                                  scalar1=-1.0, scalar2=1.0, op0=ALU.mult,
                                  op1=ALU.add))
            eng.memset(sonerowF[:], 1.0)
            eng.memset(sonerow[:], 1.0)
            for j in range(16):
                eng.memset(svT[:, 129 * j + 128:129 * (j + 1)], 1.0)
            inc(eng.memset(ssel[:], 0.0))
        ph.phase(vector=(pg2, 2))

        def pg3(eng, inc):
            eng.matmul(pSA[0:128, 0:16], sonerowF[:], sgr[0:1, 16:32],
                       start=True, stop=True)
            eng.matmul(pSA[0:128, 16:32], sonerowF[:], sgr[0:1, 0:16],
                       start=True, stop=True)
            inc(eng.matmul(pSA[0:128, 32:34], sonerowF[:], sgr[0:1, 32:34],
                           start=True, stop=True))
        ph.phase(tensor=(pg3, 1))

        def pg4(eng, inc):
            eng.tensor_copy(out=sgc[:], in_=pSA[0:128, 0:16])
            eng.tensor_copy(out=sg1c[:], in_=pSA[0:128, 16:32])
            inc(eng.tensor_copy(out=ssel[:], in_=pSA[0:128, 32:34]))
        ph.phase(vector=(pg4, 1))

        for i in range(NH):
            def bl(eng, inc, i=i):
                for w, (gt, dst) in enumerate(((qg, sQH), (kg, sKH), (vg, sVH))):
                    for half in range(2):
                        r0 = 2048 * half + 128 * i
                        inc(eng.dma_start(
                            out=dst[:, SEQ * i + TPC * half:SEQ * i + TPC * (half + 1)],
                            in_=gt[r0:r0 + 128, :]), 16)
                        inc(eng.dma_start(
                            out=sCAND[:, TPC * (2 * w + half):TPC * (2 * w + half + 1)],
                            in_=gt[r0 + 1024:r0 + 1024 + 128, :]), 16)
            ph.phase(sync=(bl, 16 * 12))

            def b1(eng, inc, i=i):
                for w, (gt, dst) in enumerate(((qg, sQH), (kg, sKH), (vg, sVH))):
                    for half in range(2):
                        sl = dst[:, SEQ * i + TPC * half:SEQ * i + TPC * (half + 1)]
                        inc(eng.tensor_scalar(out=sl, in0=sl, scalar1=ssel[:, 0:1],
                                              scalar2=None, op0=ALU.mult))
            ph.phase(vector=(b1, 6))

            def b2(eng, inc, i=i):
                for w, (gt, dst) in enumerate(((qg, sQH), (kg, sKH), (vg, sVH))):
                    for half in range(2):
                        c = TPC * (2 * w + half)
                        sl = dst[:, SEQ * i + TPC * half:SEQ * i + TPC * (half + 1)]
                        inc(eng.scalar_tensor_tensor(
                            out=sl, in0=sCAND[:, c:c + TPC], scalar=ssel[:, 1:2],
                            in1=sl, op0=ALU.mult, op1=ALU.add))
            ph.phase(vector=(b2, 6))

        for i in range(NH):
            qi = sQH[:, SEQ * i:SEQ * (i + 1)]
            ki = sKH[:, SEQ * i:SEQ * (i + 1)]
            vi = sVH[:, SEQ * i:SEQ * (i + 1)]

            def a1(eng, inc, qi=qi):
                inc(eng.activation(out=sqsc[:], in_=qi, func=AF.Identity,
                                   scale=SM_SCALE0))
            ph.phase(scalar=(a1, 1))

            for (src_, dst_) in ((qi, sqf), (ki, skf)):
                def a2(eng, inc, src_=src_):
                    inc(eng.tensor_scalar(out=se1[:], in0=src_, scalar1=0.0,
                                          scalar2=None, op0=ALU.min))
                    inc(eng.tensor_scalar(out=se2[:], in0=src_, scalar1=0.0,
                                          scalar2=None, op0=ALU.max))
                ph.phase(vector=(a2, 2))

                def a3(eng, inc):
                    inc(eng.activation(out=se1[:], in_=se1[:], func=AF.Exp))
                ph.phase(scalar=(a3, 1))

                def a4(eng, inc, dst_=dst_):
                    inc(eng.tensor_add(out=dst_[:], in0=se1[:], in1=se2[:]))
                ph.phase(vector=(a4, 1))

            def a5(eng, inc):
                inc(eng.tensor_reduce(out=snt[:], in_=skf[:], axis=AX.X, op=ALU.add))
            ph.phase(vector=(a5, 1))

            def a6(eng, inc):
                inc(eng.tensor_copy(out=smem[:, 128:129], in_=snt[:]))
            ph.phase(vector=(a6, 1))

            for g4 in range(4):
                def b1(eng, inc, g4=g4):
                    for u in range(4):
                        j = 4 * g4 + u
                        inc(eng.transpose(pCT[:, 128 * u:128 * (u + 1)],
                                          skf[:, 128 * j:128 * (j + 1)], sidB[:]))
                ph.phase(tensor=(b1, 4))

                def b2(eng, inc, g4=g4):
                    for u in range(4):
                        j = 4 * g4 + u
                        inc(eng.tensor_copy(out=skfT[:, 128 * j:128 * (j + 1)],
                                            in_=pCT[:, 128 * u:128 * (u + 1)]))
                ph.phase(vector=(b2, 4))
            for g4 in range(4):
                def b3(eng, inc, g4=g4, vi=vi):
                    for u in range(4):
                        j = 4 * g4 + u
                        inc(eng.transpose(pCT[:, 128 * u:128 * (u + 1)],
                                          vi[:, 128 * j:128 * (j + 1)], sidB[:]))
                ph.phase(tensor=(b3, 4))

                def b4(eng, inc, g4=g4):
                    for u in range(4):
                        j = 4 * g4 + u
                        inc(eng.tensor_copy(out=svT[:, 129 * j:129 * j + 128],
                                            in_=pCT[:, 128 * u:128 * (u + 1)]))
                ph.phase(vector=(b4, 4))

            def c1(eng, inc):
                for j in range(16):
                    mm = eng.matmul(pSA[:, 1536:1664], skfT[:, 128 * j:128 * (j + 1)],
                                    svT[:, 129 * j:129 * j + 128],
                                    start=(j == 0), stop=(j == 15))
                inc(mm)
            ph.phase(tensor=(c1, 1))

            def c2(eng, inc):
                inc(eng.activation(out=smem[:, 0:128], in_=pSA[:, 1536:1664],
                                   func=AF.Copy))
            ph.phase(scalar=(c2, 1))

            def d1(i, eng, inc, ki=ki):
                u, hf = i // 2, i % 2
                base = 1024 * (i % 2)
                for c2 in range(2):
                    mm = eng.matmul(pSA[:, base + 512 * c2:base + 512 * (c2 + 1)],
                                    sqsc[:, 128 * u:128 * (u + 1)],
                                    ki[:, 1024 * hf + 512 * c2:
                                        1024 * hf + 512 * (c2 + 1)],
                                    start=True, stop=True)
                inc(mm)

            def d2(i, eng, inc):
                base = 1024 * (i % 2)
                inc(eng.tensor_reduce(out=sneg2[:, i:i + 1],
                                      in_=pSA[:, base:base + 1024],
                                      axis=AX.X, op=ALU.max, negate=True))

            pipe(ph, 32, [("tensor", 0, d1, 1), ("vector", 1, d2, 1)])

            def d3(eng, inc):
                for u in range(16):
                    inc(eng.tensor_reduce(out=sneg[:, u:u + 1],
                                          in_=sneg2[:, 2 * u:2 * (u + 1)],
                                          axis=AX.X, op=ALU.min))
            ph.phase(vector=(d3, 16))

            def e1(eng, inc):
                inc(eng.transpose(pSA[0:16, 0:128], sneg[:], sidF[:]))
            ph.phase(tensor=(e1, 1))

            def e2(eng, inc):
                inc(eng.tensor_copy(out=snegrb[:], in_=pSA[0:16, 0:128]))
            ph.phase(vector=(e2, 1))

            def e3(eng, inc):
                for r in range(16):
                    inc(eng.dma_start(out=snegrow[0:1, 128 * r:128 * (r + 1)],
                                      in_=snegrb[r:r + 1, :]), 16)
            ph.phase(gpsimd=(e3, 256))

            for qb in range(4):
                def f1(kp, eng, inc, qb=qb, ki=ki):
                    base = 1024 * (kp % 2)
                    for w in range(2):
                        kb = 2 * kp + w
                        p = pSA[:, base + 512 * w:base + 512 * (w + 1)]
                        eng.matmul(p, ki[:, 128 * kb:128 * (kb + 1)],
                                   sqsc[:, 512 * qb:512 * (qb + 1)],
                                   start=True, stop=False)
                        mm = eng.matmul(p, sonerow[:],
                                        snegrow[0:1, 512 * qb:512 * (qb + 1)],
                                        start=False, stop=True)
                    inc(mm)

                def f2(kp, eng, inc):
                    base = 1024 * (kp % 2)
                    inc(eng.activation(out=sPT[:, 1024 * kp:1024 * (kp + 1)],
                                       in_=pSA[:, base:base + 1024], func=AF.Exp))

                pipe(ph, 8, [("tensor", 0, f1, 1), ("scalar", 1, f2, 1)])

                AVs = [pSA[:, 0:129], pSA[:, 129:258], pSA[:, 512:641],
                       pSA[:, 641:770]]
                MOs = [pSA[:, 1024:1153], pSA[:, 1153:1282], pSA[:, 1536:1665],
                       pSA[:, 1665:1794]]

                def f3(eng, inc, qb=qb):
                    for v4 in range(4):
                        for kb in range(16):
                            mm = eng.matmul(
                                AVs[v4],
                                sPT[:, 512 * kb + 128 * v4:512 * kb + 128 * (v4 + 1)],
                                svT[:, 129 * kb:129 * (kb + 1)],
                                start=(kb == 0), stop=(kb == 15))
                        inc(mm)
                        inc(eng.matmul(MOs[v4], sqf[:, 512 * qb + 128 * v4:
                                                512 * qb + 128 * (v4 + 1)],
                                       smem[:, 0:129], start=True, stop=True))
                ph.phase(tensor=(f3, 8))

                def f4(eng, inc):
                    for v4 in range(4):
                        inc(eng.reciprocal(out=srA[:, v4:v4 + 1],
                                           in_=AVs[v4][:, 128:129]))
                        inc(eng.reciprocal(out=srM[:, v4:v4 + 1],
                                           in_=MOs[v4][:, 128:129]))
                ph.phase(vector=(f4, 8))

                def f5(eng, inc, i=i):
                    for v4 in range(4):
                        inc(eng.tensor_tensor(out=srA[:, v4:v4 + 1],
                                              in0=srA[:, v4:v4 + 1],
                                              in1=sg1c[:, i:i + 1], op=ALU.mult))
                        inc(eng.tensor_tensor(out=srM[:, v4:v4 + 1],
                                              in0=srM[:, v4:v4 + 1],
                                              in1=sgc[:, i:i + 1], op=ALU.mult))
                ph.phase(vector=(f5, 8))

                def f6(eng, inc):
                    for v4 in range(4):
                        inc(eng.activation(out=stA[:, 128 * v4:128 * (v4 + 1)],
                                           in_=AVs[v4][:, 0:128],
                                           func=AF.Identity, scale=srA[:, v4:v4 + 1]))
                        inc(eng.activation(out=stM[:, 128 * v4:128 * (v4 + 1)],
                                           in_=MOs[v4][:, 0:128],
                                           func=AF.Identity, scale=srM[:, v4:v4 + 1]))
                ph.phase(scalar=(f6, 8))

                def f7(eng, inc):
                    inc(eng.tensor_add(out=sCT[:], in0=stA[:], in1=stM[:]))
                ph.phase(vector=(f7, 1))

                def f8(eng, inc):
                    for v4 in range(4):
                        inc(eng.transpose(pCT[:, 128 * v4:128 * (v4 + 1)],
                                          sCT[:, 128 * v4:128 * (v4 + 1)], sidB[:]))
                ph.phase(tensor=(f8, 4))

                def f9(eng, inc, i=i, qb=qb):
                    for v4 in range(4):
                        inc(eng.tensor_copy(
                            out=sCMB[:, SEQ * i + 512 * qb + 128 * v4:
                                     SEQ * i + 512 * qb + 128 * (v4 + 1)],
                            in_=pCT[:, 128 * v4:128 * (v4 + 1)]))
                ph.phase(vector=(f9, 4))

        def pst(eng, inc):
            for i in range(NH):
                inc(eng.dma_start(out=combh0[128 * i:128 * (i + 1), :],
                                  in_=sCMB[:, SEQ * i:SEQ * i + TPC]), 16)
                inc(eng.dma_start(out=combh1[128 * i:128 * (i + 1), :],
                                  in_=sCMB[:, SEQ * i + TPC:SEQ * (i + 1)]), 16)
        ph.phase(gpsimd=(pst, 16 * 16))



def stage_fc0(nc, ph, io, tag="fc"):
    x8f, wfc8g, gmax, identF = io["x8f"], io["wfc8g"], io["gmax"], io["identF"]
    hbuf, hmax = io["hbuf"], io["hmax"]

    with contextlib.ExitStack() as ctx:
        T = lambda name, shape, dt: ctx.enter_context(
            nc.sbuf_tensor(f"{tag}_{name}", shape, dt))
        sX8 = T("sX8", [128, NF * TPC], BF16)
        sw = T("sw", [128, 2 * D], BF16)
        sidF = T("sidF", [128, 128], F32)
        srel = T("srel", [128, TPC], F32)
        sh = T("sh", [128, 2 * TPC], F32)
        srow = T("srow", [1, 8], F32)
        sone = T("sone", [1, 128], F32)
        sdq = T("sdq", [128, 1], F32)
        shm = T("shm", [128, NFI], F32)
        shm1 = T("shm1", [128, 1], F32)
        shmr = T("shmr", [1, 128], F32)
        sout1 = T("sout1", [1, 1], F32)
        sgm = T("sgm", [1, 3], F32)
        pacc = ctx.enter_context(nc.psum_tensor(f"{tag}_pacc", [128, 2 * TPC], F32))
        pRep = ctx.enter_context(nc.psum_tensor(f"{tag}_pRep", [128, 1], F32))

        def p0(eng, inc):
            inc(eng.dma_start(out=sidF[:], in_=identF), 16)
            inc(eng.dma_start(out=sgm[:], in_=gmax), 16)
            _dma_tiles(eng, inc, sX8, x8f, D, TPC)
        ph.phase(sync=(p0, 16 * 18))

        def p2(eng, inc):
            eng.memset(sone[:], 1.0)
            inc(eng.tensor_scalar(out=srow[0:1, 2:3], in0=sgm[0:1, 0:1],
                                  scalar1=1.0 / FP8MAX, scalar2=1e-6,
                                  op0=ALU.mult, op1=ALU.max))
            inc(eng.tensor_scalar(out=srow[0:1, 3:4], in0=sgm[0:1, 1:2],
                                  scalar1=1.0 / FP8MAX, scalar2=1e-6,
                                  op0=ALU.mult, op1=ALU.max))
        ph.phase(vector=(p2, 2))

        def p3(eng, inc):
            inc(eng.tensor_tensor(out=srow[0:1, 4:5], in0=srow[0:1, 2:3],
                                  in1=srow[0:1, 3:4], op=ALU.mult))
        ph.phase(vector=(p3, 1))

        def p4(eng, inc):
            inc(eng.matmul(pRep[:], sone[:], srow[0:1, 4:5], start=True, stop=True))
        ph.phase(tensor=(p4, 1))

        def p5(eng, inc):
            inc(eng.tensor_copy(out=sdq[:], in_=pRep[:]))
        ph.phase(vector=(p5, 1))

        def L(m, eng, inc):
            for a in range(NF // 2):
                r0 = 16384 * a + 256 * m
                inc(eng.dma_start(
                    out=sw[:, D * (m % 2) + 256 * a:D * (m % 2) + 256 * (a + 1)],
                    in_=wfc8g[r0:r0 + 256, :].rearrange("(j p) c -> p j c", p=128)), 16)

        def M(m, eng, inc):
            for h in range(2):
                for j in range(NF):
                    mm = eng.matmul(pacc[:, TPC * (m % 2) + 512 * h:
                                         TPC * (m % 2) + 512 * (h + 1)],
                                    sw[:, D * (m % 2) + 128 * j:
                                       D * (m % 2) + 128 * (j + 1)],
                                    sX8[:, TPC * j + 512 * h:TPC * j + 512 * (h + 1)],
                                    start=(j == 0), stop=(j == NF - 1))
                inc(mm)

        def R(m, eng, inc):
            eng.activation(out=srel[:], in_=pacc[:, TPC * (m % 2):TPC * (m % 2 + 1)],
                           func=AF.Relu)
            inc(eng.activation(out=sh[:, TPC * (m % 2):TPC * (m % 2 + 1)],
                               in_=srel[:], func=AF.Square, scale=sdq[:, 0:1]), 2)

        def V(m, eng, inc):
            inc(eng.tensor_reduce(out=shm[:, m:m + 1],
                                  in_=sh[:, TPC * (m % 2):TPC * (m % 2 + 1)],
                                  axis=AX.X, op=ALU.max))

        def S(m, eng, inc):
            inc(eng.dma_start(out=hbuf[128 * m:128 * (m + 1), :],
                              in_=sh[:, TPC * (m % 2):TPC * (m % 2 + 1)]), 16)

        pipe(ph, NFI, [("sync", 0, L, 16 * 8), ("tensor", 1, M, 2),
                       ("scalar", 2, R, 2), ("vector", 3, V, 1),
                       ("gpsimd", 3, S, 16)])

        def pf1(eng, inc):
            inc(eng.tensor_reduce(out=shm1[:], in_=shm[:], axis=AX.X, op=ALU.max))
        ph.phase(vector=(pf1, 1))

        def pf2(eng, inc):
            inc(eng.transpose(pacc[0:1, 0:128], shm1[:], sidF[:]))
        ph.phase(tensor=(pf2, 1))

        def pf3(eng, inc):
            inc(eng.tensor_copy(out=shmr[:], in_=pacc[0:1, 0:128]))
        ph.phase(vector=(pf3, 1))

        def pf4(eng, inc):
            inc(eng.tensor_reduce(out=sout1[:], in_=shmr[:], axis=AX.X, op=ALU.max))
        ph.phase(vector=(pf4, 1))

        def pf5(eng, inc):
            inc(eng.dma_start(out=hmax, in_=sout1[:]), 16)
        ph.phase(gpsimd=(pf5, 16))



def stage_proj0(nc, ph, io, tag="pj"):
    x, combA, combB, woT = io["x"], io["combA"], io["combB"], io["woT"]
    hbuf, wproj8g = io["hbuf"], io["wproj8g"]
    gmax, ghmax, sel_row = io["gmax"], io["ghmax"], io["sel_row"]
    x1 = io["x1"]

    with contextlib.ExitStack() as ctx:
        T = lambda name, shape, dt: ctx.enter_context(
            nc.sbuf_tensor(f"{tag}_{name}", shape, dt))
        sCMB = T("sCMB", [128, NF * TPC], BF16)
        sCA = T("sCA", [128, 2 * TPC], BF16)
        sCB = T("sCB", [128, 3 * TPC], BF16)
        sblt = T("sblt", [128, 2 * TPC], BF16)
        sR1 = T("sR1", [128, NT * D], F32)
        swf = T("swf", [128, 2 * 512], F32)
        swb = T("swb", [128, 2 * 512], BF16)
        sxs = T("sxs", [128, 2 * 512], F32)
        srow = T("srow", [1, 16], F32)
        ssel = T("ssel", [128, 2], F32)
        sselr = T("sselr", [1, 2], F32)
        sone = T("sone", [1, 128], F32)
        sqc = T("sqc", [128, 8], F32)
        sht = T("sht", [128, 2 * 512], F32)
        sh8 = T("sh8", [128, NFI * 512], BF16)
        swp = T("swp", [128, 8 * 512], BF16)
        qf8a = T("qf8a", [128, 512], FP8)
        qf8b = T("qf8b", [128, 512], FP8)
        qua = T("qua", [128, 3 * 512], BF16)
        qub = T("qub", [128, 3 * 512], BF16)
        qmk = T("qmk", [128, 3 * 512], U8)
        qmk2 = T("qmk2", [128, 2 * 512], U8)
        sgm = T("sgm", [1, 3], F32)
        sghm = T("sghm", [1, 1], F32)
        pW = ctx.enter_context(nc.psum_tensor(f"{tag}_pW", [128, 8 * 512], F32))

        def p0(eng, inc):
            inc(eng.dma_start(out=sselr[:], in_=sel_row), 16)
            inc(eng.dma_start(out=sgm[:], in_=gmax), 16)
            inc(eng.dma_start(out=sghm[:], in_=ghmax), 16)
        ph.phase(sync=(p0, 48))

        def p1(eng, inc):
            eng.memset(sone[:], 1.0)
            inc(eng.tensor_scalar(out=srow[0:1, 2:3], in0=sghm[0:1, 0:1],
                                  scalar1=1.0 / FP8MAX, scalar2=1e-6,
                                  op0=ALU.mult, op1=ALU.max))
            inc(eng.tensor_scalar(out=srow[0:1, 3:4], in0=sgm[0:1, 2:3],
                                  scalar1=1.0 / FP8MAX, scalar2=1e-6,
                                  op0=ALU.mult, op1=ALU.max))
        ph.phase(vector=(p1, 2))

        def p3(eng, inc):
            inc(eng.tensor_tensor(out=srow[0:1, 4:5], in0=srow[0:1, 2:3],
                                  in1=srow[0:1, 3:4], op=ALU.mult))
            inc(eng.reciprocal(out=srow[0:1, 5:6], in_=srow[0:1, 2:3]))
        ph.phase(vector=(p3, 2))

        def p4(eng, inc):
            inc(eng.tensor_scalar(out=srow[0:1, 6:7], in0=srow[0:1, 5:6],
                                  scalar1=0.5, scalar2=None, op0=ALU.mult))
            inc(eng.tensor_scalar(out=srow[0:1, 7:8], in0=srow[0:1, 2:3],
                                  scalar1=240.0, scalar2=None, op0=ALU.mult))
        ph.phase(vector=(p4, 2))

        def p4b(eng, inc):
            inc(eng.tensor_scalar(out=srow[0:1, 8:9], in0=srow[0:1, 7:8],
                                  scalar1=-1.0, scalar2=None, op0=ALU.mult))
        ph.phase(vector=(p4b, 1))

        def p5(eng, inc):
            eng.matmul(pW[0:128, 0:5], sone[:], srow[0:1, 4:9], start=True, stop=True)
            inc(eng.matmul(pW[0:128, 8:10], sone[:], sselr[:], start=True, stop=True))
        ph.phase(tensor=(p5, 1))

        def p6(eng, inc):
            inc(eng.tensor_copy(out=sqc[:, 0:5], in_=pW[0:128, 0:5]))
            inc(eng.tensor_copy(out=ssel[:], in_=pW[0:128, 8:10]))
        ph.phase(vector=(p6, 2))
        DQ2, INVH, INVH2, THR, NTHR = (sqc[:, j:j + 1] for j in range(5))

        def CL(j, eng, inc):
            s = j % 2
            inc(eng.dma_start(out=sCA[:, TPC * s:TPC * (s + 1)],
                              in_=combA[128 * j:128 * (j + 1), :]), 16)
            inc(eng.dma_start(out=sCB[:, TPC * (j % 3):TPC * (j % 3) + TPC],
                              in_=combB[128 * j:128 * (j + 1), :]), 16)

        def C1(j, eng, inc):
            s = j % 2
            inc(eng.tensor_scalar(out=sblt[:, TPC * s:TPC * (s + 1)],
                                  in0=sCA[:, TPC * s:TPC * (s + 1)],
                                  scalar1=ssel[:, 0:1], scalar2=None, op0=ALU.mult))

        def C2(j, eng, inc):
            s = j % 2
            inc(eng.scalar_tensor_tensor(out=sCMB[:, TPC * j:TPC * (j + 1)],
                                         in0=sCB[:, TPC * (j % 3):TPC * (j % 3) + TPC],
                                         scalar=ssel[:, 1:2],
                                         in1=sblt[:, TPC * s:TPC * (s + 1)],
                                         op0=ALU.mult, op1=ALU.add))

        pipe(ph, NF, [("sync", 0, CL, 32), ("vector", 1, C1, 1),
                      ("vector", 2, C2, 1)])

        for cP in range(4):
            def WL(j, eng, inc, cP=cP):
                inc(eng.dma_start(out=swf[:, 512 * (j % 2):512 * (j % 2) + 512],
                                  in_=woT[128 * j:128 * (j + 1),
                                          512 * cP:512 * (cP + 1)]), 16)

            def WC(j, eng, inc):
                s = j % 2
                inc(eng.activation(out=swb[:, 512 * s:512 * (s + 1)],
                                   in_=swf[:, 512 * s:512 * (s + 1)], func=AF.Copy))

            def WM(j, eng, inc):
                s = j % 2
                for t in range(8):
                    mm = eng.matmul(pW[:, 512 * t:512 * (t + 1)],
                                    sCMB[:, TPC * j + 128 * t:TPC * j + 128 * (t + 1)],
                                    swb[:, 512 * s:512 * (s + 1)],
                                    start=(j == 0), stop=(j == NF - 1))
                inc(mm)

            pipe(ph, NF, [("sync", 0, WL, 16), ("scalar", 1, WC, 1),
                          ("tensor", 2, WM, 1)])

            def XL(t, eng, inc, cP=cP):
                inc(eng.dma_start(out=sxs[:, 512 * (t % 2):512 * (t % 2) + 512],
                                  in_=x[128 * t:128 * (t + 1),
                                        512 * cP:512 * (cP + 1)]), 16)

            def XE(t, eng, inc, cP=cP):
                inc(eng.tensor_add(out=sR1[:, D * t + 512 * cP:D * t + 512 * (cP + 1)],
                                   in0=sxs[:, 512 * (t % 2):512 * (t % 2) + 512],
                                   in1=pW[:, 512 * t:512 * (t + 1)]))

            pipe(ph, 8, [("sync", 0, XL, 16), ("vector", 1, XE, 1)])

        for half in range(2):
            hs0 = 512 * half

            NH2 = NFI // 2
            def HL(j, eng, inc, hs0=hs0):
                inc(eng.dma_start(out=sht[:, 512 * (j % 2):512 * (j % 2) + 512],
                                  in_=hbuf[128 * j:128 * (j + 1), hs0:hs0 + 512]), 16)

            def HA(j, eng, inc):
                src_ = sht[:, 512 * (j % 2):512 * (j % 2) + 512]
                r = 512 * (j % 3)
                eng.activation(out=qf8a[:], in_=src_, func=AF.Identity, scale=INVH)
                inc(eng.activation(out=qua[:, r:r + 512], in_=qf8a[:],
                                   func=AF.Copy), 1)
                eng.activation(out=qf8b[:], in_=src_, func=AF.Identity, scale=INVH2)
                inc(eng.activation(out=qub[:, r:r + 512], in_=qf8b[:],
                                   func=AF.Copy, scale=2.0), 1)

            def HM(j, eng, inc):
                src_ = sht[:, 512 * (j % 2):512 * (j % 2) + 512]
                r = 512 * (j % 3)
                inc(eng.tensor_scalar(out=qmk[:, r:r + 512], in0=src_, scalar1=THR,
                                      scalar2=None, op0=ALU.is_ge))
                inc(eng.tensor_scalar(out=qmk2[:, 512 * (j % 2):512 * (j % 2) + 512],
                                      in0=src_, scalar1=NTHR,
                                      scalar2=None, op0=ALU.is_le))
            def HO(j, eng, inc):
                r = 512 * (j % 3)
                inc(eng.tensor_tensor(out=qmk[:, r:r + 512], in0=qmk[:, r:r + 512],
                                      in1=qmk2[:, 512 * (j % 2):512 * (j % 2) + 512],
                                      op=ALU.logical_or))

            def HS(j, eng, inc):
                r = 512 * (j % 3)
                inc(eng.select(out=sh8[:, 512 * j:512 * (j + 1)],
                               mask=qmk[:, r:r + 512], on_true=qub[:, r:r + 512],
                               on_false=qua[:, r:r + 512]))

            pipe(ph, NFI, [("sync", 0, HL, 16), ("scalar", 1, HA, 2),
                           ("vector", 1, HM, 2), ("vector", 2, HO, 1),
                           ("vector", 3, HS, 1)])

            NG = 16
            for cP in range(4):
                def L2(g, eng, inc, cP=cP):
                    for q in range(4):
                        mj = 4 * g + q
                        inc(eng.dma_start(
                            out=swp[:, 2048 * (g % 2) + 512 * q:
                                     2048 * (g % 2) + 512 * (q + 1)],
                            in_=wproj8g[128 * mj:128 * (mj + 1),
                                        512 * cP:512 * (cP + 1)]), 16)

                def M2(g, eng, inc):
                    for q in range(4):
                        mj = 4 * g + q
                        for t in range(4):
                            mm = eng.matmul(pW[:, 512 * t:512 * (t + 1)],
                                            sh8[:, 512 * mj + 128 * t:
                                                 512 * mj + 128 * (t + 1)],
                                            swp[:, 2048 * (g % 2) + 512 * q:
                                                 2048 * (g % 2) + 512 * (q + 1)],
                                            start=(mj == 0), stop=(mj == NFI - 1))
                    inc(mm)

                pipe(ph, NG, [("sync", 0, L2, 64), ("tensor", 1, M2, 1)])

                def SC(t, eng, inc):
                    inc(eng.tensor_scalar(out=sxs[:, 512 * (t % 2):512 * (t % 2) + 512],
                                          in0=pW[:, 512 * t:512 * (t + 1)],
                                          scalar1=DQ2, scalar2=None, op0=ALU.mult))

                def AD(t, eng, inc, half=half, cP=cP):
                    tg = 4 * half + t
                    sl = sR1[:, D * tg + 512 * cP:D * tg + 512 * (cP + 1)]
                    inc(eng.tensor_add(out=sl, in0=sl,
                                       in1=sxs[:, 512 * (t % 2):512 * (t % 2) + 512]))

                pipe(ph, 4, [("vector", 0, SC, 1), ("vector", 1, AD, 1)])

        def pst(eng, inc):
            _dma_tiles(eng, inc, sR1, x1, TPC, D, to_sbuf=False)
        ph.phase(gpsimd=(pst, 16 * NT))



def stage_qkv1(nc, ph, io, tag="q1"):
    xn1, identB = io["xn1"], io["identB"]
    wq, wk, wv = io["wq"], io["wk"], io["wv"]
    q1, k1, v1 = io["q1"], io["k1"], io["v1"]

    with contextlib.ExitStack() as ctx:
        T = lambda name, shape, dt: ctx.enter_context(
            nc.sbuf_tensor(f"{tag}_{name}", shape, dt))
        sXnF = T("sXnF", [128, NF * TPC], BF16)
        sxt = T("sxt", [128, 2 * D], F32)
        sxb = T("sxb", [128, 2 * D], BF16)
        sidB = T("sidB", [128, 128], BF16)
        swf = T("swf", [128, 2 * D], F32)
        swb = T("swb", [128, 2 * D], BF16)
        sQa = T("sQa", [128, 2 * TPC], BF16)
        ptr2 = ctx.enter_context(nc.psum_tensor(f"{tag}_ptr2", [128, 4096], BF16))
        pa = ctx.enter_context(nc.psum_tensor(f"{tag}_pa", [128, 2 * TPC], F32))

        def p0(eng, inc):
            inc(eng.dma_start(out=sidB[:], in_=identB), 16)
        ph.phase(sync=(p0, 16))

        pipe_tm2fm(ph, xn1, sXnF, sxt, sxb, ptr2, sidB)

        jobs = [(wq, q1, SM_SCALE1), (wk, k1, None), (wv, v1, None)]

        def L(i, eng, inc):
            wten = jobs[i // NF][0]
            m = i % NF
            inc(eng.dma_start(
                out=swf[:, D * (i % 2):D * (i % 2) + D],
                in_=wten[m].rearrange("(j p) c -> p j c", p=128)), 16)

        def C(i, eng, inc):
            s = i % 2
            inc(eng.activation(out=swb[:, D * s:D * (s + 1)],
                               in_=swf[:, D * s:D * (s + 1)], func=AF.Copy))

        def M(i, eng, inc):
            s = i % 2
            for h in range(2):
                for j in range(NF):
                    mm = eng.matmul(
                        pa[:, TPC * s + 512 * h:TPC * s + 512 * (h + 1)],
                        swb[:, D * s + 128 * j:D * s + 128 * (j + 1)],
                        sXnF[:, TPC * j + 512 * h:TPC * j + 512 * (h + 1)],
                        start=(j == 0), stop=(j == NF - 1))
                inc(mm)

        def E(i, eng, inc):
            s = i % 2
            scale = jobs[i // NF][2]
            if scale is None:
                inc(eng.activation(out=sQa[:, TPC * s:TPC * (s + 1)],
                                   in_=pa[:, TPC * s:TPC * (s + 1)], func=AF.Copy))
            else:
                inc(eng.activation(out=sQa[:, TPC * s:TPC * (s + 1)],
                                   in_=pa[:, TPC * s:TPC * (s + 1)],
                                   func=AF.Identity, scale=scale))

        def S(i, eng, inc):
            s = i % 2
            outd = jobs[i // NF][1]
            m = i % NF
            inc(eng.dma_start(out=outd[128 * m:128 * (m + 1), :],
                              in_=sQa[:, TPC * s:TPC * (s + 1)]), 16)

        pipe(ph, 3 * NF, [("sync", 0, L, 16), ("scalar", 1, C, 1),
                          ("tensor", 2, M, 2), ("scalar", 3, E, 1),
                          ("gpsimd", 4, S, 16)])



def stage_v1t(nc, ph, io, tag="vt"):
    v1g, identB, v1t = io["v1g"], io["identB"], io["v1t"]
    with contextlib.ExitStack() as ctx:
        T = lambda name, shape, dt: ctx.enter_context(
            nc.sbuf_tensor(f"{tag}_{name}", shape, dt))
        sin_ = T("sin", [128, 2 * TPC], BF16)
        stile = T("stile", [128, 2 * TPC], BF16)
        sidB = T("sidB", [128, 128], BF16)
        pT = ctx.enter_context(nc.psum_tensor(f"{tag}_pT", [128, 2048], BF16))

        def p0(eng, inc):
            inc(eng.dma_start(out=sidB[:], in_=identB), 16)
        ph.phase(sync=(p0, 16))

        def L(i, eng, inc):
            r, et = i // 16, i % 16
            inc(eng.dma_start(out=sin_[:, TPC * (i % 2):TPC * (i % 2) + TPC],
                              in_=v1g[2048 * r + 128 * et:2048 * r + 128 * (et + 1), :]), 16)

        def Tr(i, eng, inc):
            for b in range(8):
                inc(eng.transpose(pT[:, 1024 * (i % 2) + 128 * b:
                                     1024 * (i % 2) + 128 * (b + 1)],
                                  sin_[:, TPC * (i % 2) + 128 * b:
                                       TPC * (i % 2) + 128 * (b + 1)], sidB[:]))

        def Ev(i, eng, inc):
            for b in range(8):
                inc(eng.tensor_copy(out=stile[:, TPC * (i % 2) + 128 * b:
                                             TPC * (i % 2) + 128 * (b + 1)],
                                    in_=pT[:, 1024 * (i % 2) + 128 * b:
                                          1024 * (i % 2) + 128 * (b + 1)]))

        def St(i, eng, inc):
            r, et = i // 16, i % 16
            for b in range(8):
                kt_row = 1024 * r + 128 * b
                inc(eng.dma_start(out=v1t[kt_row:kt_row + 128,
                                          128 * et:128 * (et + 1)],
                                  in_=stile[:, TPC * (i % 2) + 128 * b:
                                           TPC * (i % 2) + 128 * (b + 1)]), 16)

        pipe(ph, 32, [("sync", 0, L, 16), ("tensor", 1, Tr, 8),
                      ("vector", 2, Ev, 8), ("gpsimd", 3, St, 16 * 8)])



def stage_attn1(nc, ph, io, tag="a1"):
    q1, k1g, v1t = io["q1"], io["k1g"], io["v1t"]
    woT, bo_row, x1 = io["woT"], io["bo_row"], io["x1"]
    identB, identF = io["identB"], io["identF"]
    r2 = io["r2"]

    with contextlib.ExitStack() as ctx:
        T = lambda name, shape, dt: ctx.enter_context(
            nc.sbuf_tensor(f"{tag}_{name}", shape, dt))
        sQ1 = T("sQ1", [128, NF * TPC], BF16)
        sK1 = T("sK1", [128, NF * SEQ], BF16)
        sPT = T("sPT", [128, NF * TPC], BF16)
        sidB = T("sidB", [128, 128], BF16)
        sidF = T("sidF", [128, 128], F32)
        sneg = T("sneg", [128, 8], F32)
        snegrb = T("snegrb", [8, 128], BF16)
        snegrow = T("snegrow", [1, TPC], BF16)
        sonerow = T("sonerow", [1, 128], BF16)
        sonecol = T("sonecol", [128, 1], BF16)
        srd = T("srd", [128, 8], F32)
        svs = T("svs", [128, 2 * 512], BF16)
        sbo = T("sbo", [1, D], F32)
        sbob = T("sbob", [1, D], BF16)
        swf = T("swf", [128, 2 * 512], F32)
        swb = T("swb", [128, 2 * 512], BF16)
        sxs = T("sxs", [128, 2 * 512], F32)
        sout = T("sout", [128, 2 * 512], F32)
        pS = ctx.enter_context(nc.psum_tensor(f"{tag}_pS", [128, 2048], F32))
        pCT = ctx.enter_context(nc.psum_tensor(f"{tag}_pCT", [128, 4096], BF16))

        sAT = sK1
        sAF = sQ1

        def p0(eng, inc):
            inc(eng.dma_start(out=sidB[:], in_=identB), 16)
            inc(eng.dma_start(out=sidF[:], in_=identF), 16)
            inc(eng.dma_start(out=sbo[:], in_=bo_row), 16)
            for j in range(NF):
                inc(eng.dma_start(out=sK1[:, SEQ * j:SEQ * j + TPC],
                                  in_=k1g[128 * j:128 * (j + 1), :]), 16)
                inc(eng.dma_start(out=sK1[:, SEQ * j + TPC:SEQ * (j + 1)],
                                  in_=k1g[2048 + 128 * j:2048 + 128 * (j + 1), :]), 16)

        def p0b(eng, inc):
            _dma_tiles(eng, inc, sQ1, q1, D, TPC)
        ph.phase(sync=(p0, 16 * 35), gpsimd=(p0b, 16 * 16))

        def p1(eng, inc):
            eng.memset(sonerow[:], 1.0)
            inc(eng.memset(sonecol[:], 1.0))
            inc(eng.tensor_copy(out=sbob[:], in_=sbo[:]))
        ph.phase(vector=(p1, 2))

        for u in range(8):
            def d1(eng, inc, u=u):
                for kc in range(4):
                    for j in range(NF):
                        mm = eng.matmul(pS[:, 512 * kc:512 * (kc + 1)],
                                        sQ1[:, TPC * j + 128 * u:TPC * j + 128 * (u + 1)],
                                        sK1[:, SEQ * j + 512 * kc:SEQ * j + 512 * (kc + 1)],
                                        start=(j == 0), stop=(j == NF - 1))
                inc(mm)
            ph.phase(tensor=(d1, 1))

            def d2(eng, inc, u=u):
                inc(eng.tensor_reduce(out=sneg[:, u:u + 1], in_=pS[:], axis=AX.X,
                                      op=ALU.max, negate=True))
            ph.phase(vector=(d2, 1))

        def e1(eng, inc):
            inc(eng.transpose(pS[0:8, 0:128], sneg[:], sidF[:]))
        ph.phase(tensor=(e1, 1))

        def e2(eng, inc):
            inc(eng.tensor_copy(out=snegrb[:], in_=pS[0:8, 0:128]))
        ph.phase(vector=(e2, 1))

        def e3(eng, inc):
            for r in range(8):
                inc(eng.dma_start(out=snegrow[0:1, 128 * r:128 * (r + 1)],
                                  in_=snegrb[r:r + 1, :]), 16)
        ph.phase(gpsimd=(e3, 128))

        def sf1(i, eng, inc):
            kb, qh2 = i // 2, i % 2
            p = pS[:, 512 * (i % 2):512 * (i % 2) + 512]
            for j in range(NF):
                eng.matmul(p, sK1[:, SEQ * j + 128 * kb:SEQ * j + 128 * (kb + 1)],
                           sQ1[:, TPC * j + 512 * qh2:TPC * j + 512 * (qh2 + 1)],
                           start=(j == 0), stop=False)
            inc(eng.matmul(p, sonerow[:],
                           snegrow[0:1, 512 * qh2:512 * (qh2 + 1)],
                           start=False, stop=True))

        def sf2(i, eng, inc):
            kb, qh2 = i // 2, i % 2
            inc(eng.activation(
                out=sPT[:, TPC * kb + 512 * qh2:TPC * kb + 512 * (qh2 + 1)],
                in_=pS[:, 512 * (i % 2):512 * (i % 2) + 512], func=AF.Exp))

        pipe(ph, 2 * NF, [("tensor", 0, sf1, 1), ("scalar", 1, sf2, 1)])

        def g1(eng, inc):
            for u in range(8):
                for kb in range(NF):
                    mm = eng.matmul(pS[:, 1536 + u:1537 + u],
                                    sPT[:, TPC * kb + 128 * u:TPC * kb + 128 * (u + 1)],
                                    sonecol[:], start=(kb == 0), stop=(kb == NF - 1))
                inc(mm)
        ph.phase(tensor=(g1, 8))

        def g2(eng, inc):
            for u in range(8):
                inc(eng.reciprocal(out=srd[:, u:u + 1], in_=pS[:, 1536 + u:1537 + u]))
        ph.phase(vector=(g2, 8))

        for up in range(2):
            for cE in range(4):
                def h1(kb, eng, inc, cE=cE):
                    inc(eng.dma_start(out=svs[:, 512 * (kb % 2):512 * (kb % 2) + 512],
                                      in_=v1t[128 * kb:128 * (kb + 1),
                                              512 * cE:512 * (cE + 1)]), 16)

                def h2(kb, eng, inc, up=up):
                    for u4 in range(4):
                        u = 4 * up + u4
                        mm = eng.matmul(pS[:, 512 * u4:512 * (u4 + 1)],
                                        sPT[:, TPC * kb + 128 * u:
                                             TPC * kb + 128 * (u + 1)],
                                        svs[:, 512 * (kb % 2):512 * (kb % 2) + 512],
                                        start=(kb == 0), stop=(kb == NF - 1))
                    inc(mm)

                pipe(ph, NF, [("sync", 0, h1, 16), ("tensor", 1, h2, 1)])

                def h3(eng, inc, up=up, cE=cE):
                    for u4 in range(4):
                        u = 4 * up + u4
                        inc(eng.activation(
                            out=sAT[:, SEQ * u + 512 * cE:SEQ * u + 512 * (cE + 1)],
                            in_=pS[:, 512 * u4:512 * (u4 + 1)],
                            func=AF.Identity, scale=srd[:, u:u + 1]))
                ph.phase(scalar=(h3, 4))

        def t1(u, eng, inc):
            h = 2048 * (u % 2)
            for j in range(NF):
                inc(eng.transpose(pCT[:, h + 128 * j:h + 128 * (j + 1)],
                                  sAT[:, SEQ * u + 128 * j:SEQ * u + 128 * (j + 1)],
                                  sidB[:]))

        def t2(u, eng, inc):
            h = 2048 * (u % 2)
            for j in range(NF):
                inc(eng.tensor_copy(
                    out=sAF[:, TPC * j + 128 * u:TPC * j + 128 * (u + 1)],
                    in_=pCT[:, h + 128 * j:h + 128 * (j + 1)]))

        pipe(ph, 8, [("tensor", 0, t1, NF), ("vector", 1, t2, NF)])

        for tp in range(2):
            for cP in range(4):
                def w1(j, eng, inc, cP=cP):
                    inc(eng.dma_start(out=swf[:, 512 * (j % 2):512 * (j % 2) + 512],
                                      in_=woT[128 * j:128 * (j + 1),
                                              512 * cP:512 * (cP + 1)]), 16)

                def w2(j, eng, inc):
                    s = j % 2
                    inc(eng.activation(out=swb[:, 512 * s:512 * (s + 1)],
                                       in_=swf[:, 512 * s:512 * (s + 1)],
                                       func=AF.Copy))

                def w3(j, eng, inc, tp=tp, cP=cP):
                    s = j % 2
                    for t4 in range(4):
                        t = 4 * tp + t4
                        mm = eng.matmul(pS[:, 512 * t4:512 * (t4 + 1)],
                                        sAF[:, TPC * j + 128 * t:TPC * j + 128 * (t + 1)],
                                        swb[:, 512 * s:512 * (s + 1)],
                                        start=(j == 0), stop=False)
                    if j == NF - 1:
                        for t4 in range(4):
                            mm = eng.matmul(pS[:, 512 * t4:512 * (t4 + 1)],
                                            sonerow[:],
                                            sbob[0:1, 512 * cP:512 * (cP + 1)],
                                            start=False, stop=True)
                    inc(mm)

                pipe(ph, NF, [("sync", 0, w1, 16), ("scalar", 1, w2, 1),
                              ("tensor", 2, w3, 1)])

                def w4(t4, eng, inc, tp=tp, cP=cP):
                    t = 4 * tp + t4
                    inc(eng.dma_start(out=sxs[:, 512 * (t4 % 2):512 * (t4 % 2) + 512],
                                      in_=x1[128 * t:128 * (t + 1),
                                             512 * cP:512 * (cP + 1)]), 16)

                def w5(t4, eng, inc):
                    inc(eng.tensor_add(out=sout[:, 512 * (t4 % 2):512 * (t4 % 2) + 512],
                                       in0=sxs[:, 512 * (t4 % 2):512 * (t4 % 2) + 512],
                                       in1=pS[:, 512 * t4:512 * (t4 + 1)]))

                def w6(t4, eng, inc, tp=tp, cP=cP):
                    t = 4 * tp + t4
                    inc(eng.dma_start(out=r2[128 * t:128 * (t + 1),
                                             512 * cP:512 * (cP + 1)],
                                      in_=sout[:, 512 * (t4 % 2):512 * (t4 % 2) + 512]), 16)

                pipe(ph, 4, [("sync", 0, w4, 16), ("vector", 1, w5, 1),
                             ("gpsimd", 2, w6, 16)])



def stage_lut(nc, ph, io, tag="lu"):
    lut_row = io["lut_row"]
    gidx, uidx, didx = io["gidx"], io["uidx"], io["didx"]
    wgt_sh, wut_sh, wd_sh = io["wgt_sh"], io["wut_sh"], io["wd_sh"]
    W = INTER1

    with contextlib.ExitStack() as ctx:
        T = lambda name, shape, dt: ctx.enter_context(
            nc.sbuf_tensor(f"{tag}_{name}", shape, dt))
        sI = T("sI", [128, W], I32)
        sB = T("sB", [128, W], BF16)
        sT = T("sT", [128, W], BF16)
        sAa = T("sAa", [128, W], BF16)
        sAb = T("sAb", [128, W], BF16)
        sI2 = T("sI2", [128, W], I32)
        sB2 = T("sB2", [128, W], BF16)
        sT2 = T("sT2", [128, W], BF16)
        sAc = T("sAc", [128, W], BF16)
        sAd = T("sAd", [128, W], BF16)
        sone = T("sone", [1, 128], F32)
        slr = T("slr", [1, 16], F32)
        slc = T("slc", [128, 16], F32)
        pRep = ctx.enter_context(nc.psum_tensor(f"{tag}_pRep", [128, 16], F32))

        def p0(eng, inc):
            inc(eng.dma_start(out=slr[:], in_=lut_row), 16)
        ph.phase(sync=(p0, 16))

        def p1(eng, inc):
            inc(eng.memset(sone[:], 1.0))
        ph.phase(vector=(p1, 1))

        def p2(eng, inc):
            inc(eng.matmul(pRep[:], sone[:], slr[:], start=True, stop=True))
        ph.phase(tensor=(p2, 1))

        def p3(eng, inc):
            inc(eng.tensor_copy(out=slc[:], in_=pRep[:]))
        ph.phase(vector=(p3, 1))

        jobs = []
        for t in range(2):
            jobs.append((gidx, "gt", 128 * t, W, t))
            jobs.append((uidx, "ut", 128 * t, W, t))
        for t in range(6):
            jobs.append((didx, "d", 128 * t, D, t))
        assert len(jobs) % 2 == 0

        for jp in range(0, len(jobs), 2):
            pair = [jobs[jp], jobs[jp + 1]]
            bufs = [(sI, sB, sT, sAa, sAb), (sI2, sB2, sT2, sAc, sAd)]

            def jl(eng, inc, pair=pair, bufs=bufs):
                for (srcd, kind, r0, Wj, trow), (bI, bB, bT, bA, bBk) in zip(pair, bufs):
                    inc(eng.dma_start(out=bI[:, 0:Wj], in_=srcd[r0:r0 + 128, :]), 16)
            ph.phase(sync=(jl, 32))

            def jc(eng, inc, pair=pair, bufs=bufs):
                for (srcd, kind, r0, Wj, trow), (bI, bB, bT, bA, bBk) in zip(pair, bufs):
                    inc(eng.tensor_copy(out=bB[:, 0:Wj], in_=bI[:, 0:Wj]))
            ph.phase(vector=(jc, 2))

            def j0(eng, inc, pair=pair, bufs=bufs):
                for (srcd, kind, r0, Wj, trow), (bI, bB, bT, bA, bBk) in zip(pair, bufs):
                    inc(eng.tensor_scalar(out=bA[:, 0:Wj], in0=bB[:, 0:Wj],
                                          scalar1=0.0, scalar2=slc[:, 0:1],
                                          op0=ALU.is_equal, op1=ALU.mult))
            ph.phase(vector=(j0, 2))
            curs = [(bufs[0][3], bufs[0][4]), (bufs[1][3], bufs[1][4])]
            for l in range(1, 16):
                def jt(eng, inc, pair=pair, bufs=bufs, l=l, curs=tuple(curs)):
                    for (srcd, kind, r0, Wj, trow), (bI, bB, bT, bA, bBk), (cur, oth)                             in zip(pair, bufs, curs):
                        inc(eng.tensor_scalar(out=bT[:, 0:Wj], in0=bB[:, 0:Wj],
                                              scalar1=float(l), scalar2=slc[:, l:l + 1],
                                              op0=ALU.is_equal, op1=ALU.mult))
                ph.phase(vector=(jt, 2))

                def ja(eng, inc, pair=pair, bufs=bufs, curs=tuple(curs)):
                    for (srcd, kind, r0, Wj, trow), (bI, bB, bT, bA, bBk), (cur, oth)                             in zip(pair, bufs, curs):
                        inc(eng.tensor_add(out=oth[:, 0:Wj], in0=cur[:, 0:Wj],
                                           in1=bT[:, 0:Wj]))
                ph.phase(vector=(ja, 2))
                curs = [(o, c) for (c, o) in curs]

            def js(eng, inc, pair=pair, curs=tuple(curs)):
                for (srcd, kind, r0, Wj, trow), (cur, oth) in zip(pair, curs):
                    if kind == "d":
                        inc(eng.dma_start(out=wd_sh[r0:r0 + 128, :], in_=cur[:, 0:D]), 16)
                    else:
                        dstd = wgt_sh if kind == "gt" else wut_sh
                        for m in range(NI1):
                            r0t = 256 * m + 128 * trow
                            inc(eng.dma_start(out=dstd[r0t:r0t + 128, :],
                                              in_=cur[:, 128 * m:128 * (m + 1)]), 16)
            ns = sum((16 if k == "d" else 16 * NI1) for (_, k, _, _, _) in pair)
            ph.phase(gpsimd=(js, ns))



def stage_ffn1(nc, ph, io, tag="f1"):
    xn1, identB = io["xn1"], io["identB"]
    wgg, wug, wdg = io["wgg"], io["wug"], io["wdg"]
    r2, outd = io["r2"], io["outd"]

    with contextlib.ExitStack() as ctx:
        T = lambda name, shape, dt: ctx.enter_context(
            nc.sbuf_tensor(f"{tag}_{name}", shape, dt))
        sXnF = T("sXnF", [128, NF * TPC], BF16)
        sxt = T("sxt", [128, 2 * D], F32)
        sxb = T("sxb", [128, 2 * D], BF16)
        sidB = T("sidB", [128, 128], BF16)
        sH1 = T("sH1", [128, NI1 * TPC], BF16)
        swg = T("swg", [128, 2 * D], BF16)
        swu = T("swu", [128, 2 * D], BF16)
        ssil = T("ssil", [128, 2 * 512], F32)
        sup = T("sup", [128, 2 * 512], F32)
        swd = T("swd", [128, 8 * 512], BF16)
        sxs = T("sxs", [128, 2 * 512], F32)
        sout = T("sout", [128, 2 * 512], F32)
        ptr2 = ctx.enter_context(nc.psum_tensor(f"{tag}_ptr2", [128, 4096], BF16))
        pBig = ctx.enter_context(nc.psum_tensor(f"{tag}_pBig", [128, 2048], F32))

        def p0(eng, inc):
            inc(eng.dma_start(out=sidB[:], in_=identB), 16)
        ph.phase(sync=(p0, 16))

        pipe_tm2fm(ph, xn1, sXnF, sxt, sxb, ptr2, sidB)

        def L(i, eng, inc):
            m = i // 2
            if i % 2 == 0:
                for a in range(NF // 2):
                    r0 = 11008 * a + 256 * m
                    inc(eng.dma_start(
                        out=swg[:, D * (m % 2) + 256 * a:D * (m % 2) + 256 * (a + 1)],
                        in_=wgg[r0:r0 + 256, :].rearrange("(j p) c -> p j c", p=128)), 16)

        def LU(i, eng, inc):
            m = i // 2
            if i % 2 == 0:
                for a in range(NF // 2):
                    r0 = 11008 * a + 256 * m
                    inc(eng.dma_start(
                        out=swu[:, D * (m % 2) + 256 * a:D * (m % 2) + 256 * (a + 1)],
                        in_=wug[r0:r0 + 256, :].rearrange("(j p) c -> p j c", p=128)), 16)

        def M(i, eng, inc):
            m, h = i // 2, i % 2
            sl = 1024 * (i % 2 == 1)
            sl = 1024 * (i % 2)
            for j in range(NF):
                mm = eng.matmul(pBig[:, sl:sl + 512],
                                swg[:, D * (m % 2) + 128 * j:D * (m % 2) + 128 * (j + 1)],
                                sXnF[:, TPC * j + 512 * h:TPC * j + 512 * (h + 1)],
                                start=(j == 0), stop=(j == NF - 1))
            inc(mm)
            for j in range(NF):
                mm = eng.matmul(pBig[:, sl + 512:sl + 1024],
                                swu[:, D * (m % 2) + 128 * j:D * (m % 2) + 128 * (j + 1)],
                                sXnF[:, TPC * j + 512 * h:TPC * j + 512 * (h + 1)],
                                start=(j == 0), stop=(j == NF - 1))
            inc(mm)

        def SI(i, eng, inc):
            sl = 1024 * (i % 2)
            so = 512 * (i % 2)
            eng.activation(out=ssil[:, so:so + 512], in_=pBig[:, sl:sl + 512],
                           func=AF.Silu)
            inc(eng.activation(out=sup[:, so:so + 512],
                               in_=pBig[:, sl + 512:sl + 1024], func=AF.Copy), 2)

        def MU(i, eng, inc):
            m, h = i // 2, i % 2
            so = 512 * (i % 2)
            inc(eng.tensor_mul(out=sH1[:, TPC * m + 512 * h:TPC * m + 512 * (h + 1)],
                               in0=ssil[:, so:so + 512], in1=sup[:, so:so + 512]))

        pipe(ph, 2 * NI1, [("sync", 0, L, lambda i: 16 * 8 if i % 2 == 0 else 0),
                           ("gpsimd", 0, LU, lambda i: 16 * 8 if i % 2 == 0 else 0),
                           ("tensor", 1, M, 2), ("scalar", 2, SI, 2),
                           ("vector", 3, MU, 1)])

        NG = 11
        for tp in range(2):
            for cP in range(4):
                def L2(g, eng, inc, cP=cP):
                    lo = 4 * g
                    for q in range(min(4, NI1 - lo)):
                        mj = lo + q
                        inc(eng.dma_start(
                            out=swd[:, 2048 * (g % 2) + 512 * q:
                                     2048 * (g % 2) + 512 * (q + 1)],
                            in_=wdg[128 * mj:128 * (mj + 1),
                                    512 * cP:512 * (cP + 1)]), 16)

                def M2(g, eng, inc, tp=tp):
                    lo = 4 * g
                    for q in range(min(4, NI1 - lo)):
                        mj = lo + q
                        for t4 in range(4):
                            t = 4 * tp + t4
                            mm = eng.matmul(pBig[:, 512 * t4:512 * (t4 + 1)],
                                            sH1[:, TPC * mj + 128 * t:
                                                 TPC * mj + 128 * (t + 1)],
                                            swd[:, 2048 * (g % 2) + 512 * q:
                                                 2048 * (g % 2) + 512 * (q + 1)],
                                            start=(mj == 0), stop=(mj == NI1 - 1))
                    inc(mm)

                pipe(ph, NG, [("sync", 0, L2, lambda g: 16 * min(4, NI1 - 4 * g)),
                              ("tensor", 1, M2, 1)])

                def EV(t4, eng, inc, tp=tp, cP=cP):
                    t = 4 * tp + t4
                    inc(eng.dma_start(out=sxs[:, 512 * (t4 % 2):512 * (t4 % 2) + 512],
                                      in_=r2[128 * t:128 * (t + 1),
                                             512 * cP:512 * (cP + 1)]), 16)

                def AD(t4, eng, inc):
                    inc(eng.tensor_add(out=sout[:, 512 * (t4 % 2):512 * (t4 % 2) + 512],
                                       in0=sxs[:, 512 * (t4 % 2):512 * (t4 % 2) + 512],
                                       in1=pBig[:, 512 * t4:512 * (t4 + 1)]))

                def ST(t4, eng, inc, tp=tp, cP=cP):
                    t = 4 * tp + t4
                    inc(eng.dma_start(out=outd[128 * t:128 * (t + 1),
                                               512 * cP:512 * (cP + 1)],
                                      in_=sout[:, 512 * (t4 % 2):512 * (t4 % 2) + 512]), 16)

                pipe(ph, 4, [("sync", 0, EV, 16), ("vector", 1, AD, 1),
                             ("gpsimd", 2, ST, 16)])



def prog_fused():
    nc = bass.Bass("TRN2", target_bir_lowering=False)
    EI, EO, IN = "ExternalInput", "ExternalOutput", "Internal"

    def dr(name, shape, dt, kind):
        return nc.dram_tensor(name, shape, dt, kind=kind).ap()

    x = dr("x", [TPC, D], F32, EI)
    wq0 = dr("wq0", [NF, D, 128], F32, EI)
    wk0 = dr("wk0", [NF, D, 128], F32, EI)
    wv0 = dr("wv0", [NF, D, 128], F32, EI)
    cosT = dr("cosT", [TPC, TPC], F32, EI)
    sinT = dr("sinT", [TPC, TPC], F32, EI)
    wfc_sh = dr("wfc_sh", [256, FFI], F32, EI)
    wproj_sh = dr("wproj_sh", [1024, D], F32, EI)
    woT0 = dr("woT0", [D, D], F32, EI)
    wq1 = dr("wq1", [NF, D, 128], F32, EI)
    wk1 = dr("wk1", [NF, D, 128], F32, EI)
    wv1 = dr("wv1", [NF, D, 128], F32, EI)
    woT1 = dr("woT1", [D, D], F32, EI)
    bo_row = dr("bo_row", [1, D], F32, EI)
    gate_row = dr("gate_row", [1, 16], F32, EI)
    sel_row = dr("sel_row", [1, 2], F32, EI)
    lut_row = dr("lut_row", [1, 16], F32, EI)
    gidx = dr("gidx", [256, INTER1], I32, EI)
    uidx = dr("uidx", [256, INTER1], I32, EI)
    didx = dr("didx", [768, D], I32, EI)
    identB = dr("identB", [128, 128], BF16, EI)
    identF = dr("identF", [128, 128], F32, EI)
    outd = dr("outd", [TPC, D], F32, EO)

    xn = dr("xn", [TPC, D], F32, IN)
    maxes = dr("maxes", [1, 3], F32, IN)
    gmax = dr("gmax", [1, 3], F32, IN)
    qf = dr("qf", [D, TPC], BF16, IN)
    kf = dr("kf", [D, TPC], BF16, IN)
    vf = dr("vf", [D, TPC], BF16, IN)
    qg = dr("qg", [2 * D, TPC], BF16, IN)
    kg = dr("kg", [2 * D, TPC], BF16, IN)
    vg = dr("vg", [2 * D, TPC], BF16, IN)
    x8f = dr("x8f", [D, TPC], BF16, IN)
    wfc8t_sh = dr("wfc8t_sh", [NFI * 256, 128], BF16, IN)
    wfc8g = dr("wfc8g", [8 * NFI * 256, 128], BF16, IN)
    wproj8_sh = dr("wproj8_sh", [1024, D], BF16, IN)
    wproj8g = dr("wproj8g", [FFI, D], BF16, IN)
    combh0 = dr("combh0", [TPC, TPC], BF16, IN)
    combh1 = dr("combh1", [TPC, TPC], BF16, IN)
    cgA = dr("cgA", [D, TPC], BF16, IN)
    cgB = dr("cgB", [D, TPC], BF16, IN)
    hbuf = dr("hbuf", [FFI, TPC], F32, IN)
    hmax = dr("hmax", [1, 1], F32, IN)
    ghmax = dr("ghmax", [1, 1], F32, IN)
    x1 = dr("x1", [TPC, D], F32, IN)
    xn1 = dr("xn1", [TPC, D], F32, IN)
    q1 = dr("q1", [D, TPC], BF16, IN)
    k1 = dr("k1", [D, TPC], BF16, IN)
    v1 = dr("v1", [D, TPC], BF16, IN)
    k1g = dr("k1g", [2 * D, TPC], BF16, IN)
    v1g = dr("v1g", [2 * D, TPC], BF16, IN)
    v1t = dr("v1t", [SEQ, D], BF16, IN)
    r2 = dr("r2", [TPC, D], F32, IN)
    wgt_sh = dr("wgt_sh", [NI1 * 256, 128], BF16, IN)
    wut_sh = dr("wut_sh", [NI1 * 256, 128], BF16, IN)
    wgg = dr("wgg", [8 * NI1 * 256, 128], BF16, IN)
    wug = dr("wug", [8 * NI1 * 256, 128], BF16, IN)
    wd_sh = dr("wd_sh", [768, D], BF16, IN)
    wdg = dr("wdg", [6144, D], BF16, IN)

    with contextlib.ExitStack() as ctx:
        ph = Phaser(nc, ctx)

        stage_ln(nc, ph, {"x": x, "xn": xn, "identF": identF,
                          "wfc_sh": wfc_sh, "wproj_sh": wproj_sh,
                          "maxes": maxes}, with_maxes=True, tag="l0")
        coll_phase(ph, "AllReduce", ALU.max, GROUP8, [(maxes, gmax)])
        stage_prep0(nc, ph, {
            "xn": xn, "gmax": gmax, "wq": wq0, "wk": wk0, "wv": wv0,
            "cosT": cosT, "sinT": sinT, "wfc_sh": wfc_sh, "wproj_sh": wproj_sh,
            "identB": identB, "qf": qf, "kf": kf, "vf": vf, "x8f": x8f,
            "wfc8t_sh": wfc8t_sh, "wproj8_sh": wproj8_sh})
        coll_phase(ph, "AllGather", ALU.bypass, PAIRS,
                   [(qf, qg), (kf, kg), (vf, vg)])
        coll_phase(ph, "AllGather", ALU.bypass, GROUP8,
                   [(wfc8t_sh, wfc8g), (wproj8_sh, wproj8g)])
        stage_fc0(nc, ph, {"x8f": x8f, "wfc8g": wfc8g, "gmax": gmax,
                           "identF": identF, "hbuf": hbuf, "hmax": hmax})
        coll_phase(ph, "AllReduce", ALU.max, GROUP8, [(hmax, ghmax)])
        stage_attn0(nc, ph, {
            "qg": qg, "kg": kg, "vg": vg, "gate_row": gate_row,
            "sel_row": sel_row, "identB": identB, "identF": identF,
            "combh0": combh0, "combh1": combh1})
        coll_phase(ph, "AllGather", ALU.bypass, PAIRS,
                   [(combh0, cgA), (combh1, cgB)])
        stage_proj0(nc, ph, {
            "x": x, "combA": cgA, "combB": cgB, "woT": woT0, "hbuf": hbuf,
            "wproj8g": wproj8g, "gmax": gmax, "ghmax": ghmax,
            "sel_row": sel_row, "x1": x1})
        stage_ln(nc, ph, {"x": x1, "xn": xn1, "identF": identF},
                 with_maxes=False, tag="l1")
        stage_qkv1(nc, ph, {"xn1": xn1, "identB": identB, "wq": wq1,
                            "wk": wk1, "wv": wv1, "q1": q1, "k1": k1, "v1": v1})
        coll_phase(ph, "AllGather", ALU.bypass, PAIRS, [(k1, k1g), (v1, v1g)])
        stage_v1t(nc, ph, {"v1g": v1g, "identB": identB, "v1t": v1t})
        stage_attn1(nc, ph, {
            "q1": q1, "k1g": k1g, "v1t": v1t, "woT": woT1, "bo_row": bo_row,
            "x1": x1, "identB": identB, "identF": identF, "r2": r2})
        stage_lut(nc, ph, {"lut_row": lut_row, "gidx": gidx, "uidx": uidx,
                           "didx": didx, "wgt_sh": wgt_sh, "wut_sh": wut_sh,
                           "wd_sh": wd_sh})
        coll_phase(ph, "AllGather", ALU.bypass, GROUP8,
                   [(wgt_sh, wgg), (wut_sh, wug), (wd_sh, wdg)])
        stage_ffn1(nc, ph, {"xn1": xn1, "identB": identB, "wgg": wgg,
                            "wug": wug, "wdg": wdg, "r2": r2, "outd": outd})
        ph.finish()
    return nc



def host_constants():
    c = {}
    c["identB"] = np.eye(128, dtype=np.float32).astype(ml_dtypes.bfloat16)
    c["identF"] = np.eye(128, dtype=np.float32)
    H = np.int32(D)
    inv_freq = (np.float32(1.0) /
                np.power(np.float32(10000.0),
                         np.arange(0, D, 2, dtype=np.float32) / np.float32(H)))
    t = np.arange(SEQ, dtype=np.float32)
    freqs = t[:, None] * inv_freq[None, :]
    cos = np.cos(freqs).astype(np.float32)
    sin = np.sin(freqs).astype(np.float32)
    c["cosT"] = [np.ascontiguousarray(cos[h * TPC:(h + 1) * TPC, :].T) for h in range(2)]
    c["sinT"] = [np.ascontiguousarray(sin[h * TPC:(h + 1) * TPC, :].T) for h in range(2)]
    return c


def _tile_wT(w):
    o, i = w.shape
    return np.ascontiguousarray(w.T.reshape(i, o // 128, 128).transpose(1, 0, 2))


def host_shards(inputs):
    d = {}
    x = np.ascontiguousarray(inputs["L0_attn_hidden_states"]).reshape(8192, D)
    d["x_slab"] = [np.ascontiguousarray(x[TPC * c:TPC * (c + 1)]) for c in range(NCORES)]
    wfcT = np.ascontiguousarray(inputs["L0_ffn_w_fc"].T)
    wprojT = np.ascontiguousarray(inputs["L0_ffn_w_proj"].T)
    d["wfcT_sh"] = [np.ascontiguousarray(wfcT[256 * c:256 * (c + 1)]) for c in range(NCORES)]
    d["wprojT_sh"] = [np.ascontiguousarray(wprojT[1024 * c:1024 * (c + 1)]) for c in range(NCORES)]
    for nm, key in (("wq0", "L0_attn_Wq"), ("wk0", "L0_attn_Wk"), ("wv0", "L0_attn_Wv"),
                    ("wq1", "L1_attn_Wq"), ("wk1", "L1_attn_Wk"), ("wv1", "L1_attn_Wv")):
        d[nm] = _tile_wT(np.ascontiguousarray(inputs[key]))
    d["woT0"] = np.ascontiguousarray(inputs["L0_attn_Wo"].T)
    d["woT1"] = np.ascontiguousarray(inputs["L1_attn_Wo"].T)
    d["bo_row"] = np.ascontiguousarray(inputs["L1_attn_bo"]).reshape(1, D)
    d["gate_row"] = np.ascontiguousarray(inputs["L0_attn_gate"]).reshape(1, 16)
    d["lut_row"] = np.ascontiguousarray(inputs["L1_ffn_lut"]).reshape(1, 16)
    gT = np.ascontiguousarray(inputs["L1_ffn_gate_idx"].T)
    uT = np.ascontiguousarray(inputs["L1_ffn_up_idx"].T)
    dT = np.ascontiguousarray(inputs["L1_ffn_down_idx"].T)
    d["gidxT_sh"] = [np.ascontiguousarray(gT[256 * c:256 * (c + 1)]) for c in range(NCORES)]
    d["uidxT_sh"] = [np.ascontiguousarray(uT[256 * c:256 * (c + 1)]) for c in range(NCORES)]
    dTp = np.concatenate([dT, np.zeros((6144 - INTER1, D), np.int32)], axis=0)
    d["didxT_sh"] = [np.ascontiguousarray(dTp[768 * c:768 * (c + 1)]) for c in range(NCORES)]
    return d


_PROG_CACHE = {}


def _cached(name, fn, *args):
    key = (name,) + args
    if key not in _PROG_CACHE:
        _PROG_CACHE[key] = fn(*args)
    return _PROG_CACHE[key]


def kernel(**inputs):
    consts = host_constants()
    shards = host_shards(inputs)
    nc = _cached("fused", prog_fused)
    in_maps = []
    for c in range(NCORES):
        p = c % 2
        gr = shards["gate_row"]
        if p == 1:
            gr = np.ascontiguousarray(np.concatenate([gr[:, 8:], gr[:, :8]], axis=1))
        in_maps.append({
            "x": shards["x_slab"][c],
            "wq0": shards["wq0"], "wk0": shards["wk0"], "wv0": shards["wv0"],
            "cosT": consts["cosT"][p], "sinT": consts["sinT"][p],
            "wfc_sh": shards["wfcT_sh"][c], "wproj_sh": shards["wprojT_sh"][c],
            "woT0": shards["woT0"],
            "wq1": shards["wq1"], "wk1": shards["wk1"], "wv1": shards["wv1"],
            "woT1": shards["woT1"], "bo_row": shards["bo_row"],
            "gate_row": gr,
            "sel_row": np.array([[1.0 - p, float(p)]], np.float32),
            "lut_row": shards["lut_row"],
            "gidx": shards["gidxT_sh"][c], "uidx": shards["uidxT_sh"][c],
            "didx": shards["didxT_sh"][c],
            "identB": consts["identB"], "identF": consts["identF"],
        })
    res = run_bass_kernel_spmd(nc, in_maps, core_ids=list(range(NCORES)))
    out = np.concatenate([res.results[c]["outd"] for c in range(NCORES)], axis=0)
    return out.reshape(4, SEQ, D)


# version 07
# speedup vs baseline: 1.1903x; 1.0391x over previous
import contextlib
import numpy as np
import ml_dtypes
import concourse.bass as bass
import concourse.mybir as mybir
from concourse.bass_utils import run_bass_kernel_spmd

F32 = mybir.dt.float32
BF16 = mybir.dt.bfloat16
FP8 = mybir.dt.float8e4
I32 = mybir.dt.int32
U8 = mybir.dt.uint8
AF = mybir.ActivationFunctionType
ALU = mybir.AluOpType
AX = mybir.AxisListType

NCORES = 8
D = 2048
SEQ = 2048
TPC = 1024
NF = D // 128
NT = TPC // 128
EPS = 1e-5
FP8MAX = 448.0
INTER1 = 5504
NI1 = INTER1 // 128
FFI = 8192
NFI = FFI // 128
SM_SCALE0 = float(np.float32(1.0) / np.sqrt(np.float32(128.0)))
SM_SCALE1 = float(np.float32(1.0) / np.sqrt(np.float32(2048.0)))

ENGINES = ("sync", "scalar", "vector", "tensor", "gpsimd")
GROUP8 = [[0, 1, 2, 3, 4, 5, 6, 7]]
PAIRS = [[0, 1], [2, 3], [4, 5], [6, 7]]


class Phaser:

    def __init__(self, nc, ctx):
        self.nc = nc
        self.ctx = ctx
        self.block = ctx.enter_context(nc.Block())
        self.bar = ctx.enter_context(nc.semaphore("bar0"))
        self.cum = 0
        self.epoch = 0
        self.fences = []
        self.engine_epoch = {e: 0 for e in ENGINES}

    def _catchup(self, eng, ename):
        while self.engine_epoch[ename] < self.epoch:
            s, tot = self.fences[self.engine_epoch[ename]]
            eng.wait_ge(s, tot)
            self.engine_epoch[ename] += 1

    def phase(self, **bodies):
        for ename, (emit, n_decl) in bodies.items():
            def body(eng, emit=emit, n_decl=n_decl, ename=ename):
                self._catchup(eng, ename)
                if self.cum > 0:
                    eng.wait_ge(self.bar, self.cum)
                got = [0]

                def inc(instr, amt=1):
                    instr.then_inc(self.bar, amt)
                    got[0] += amt
                    return instr

                emit(eng, inc)
                assert got[0] == n_decl, (
                    f"engine {ename}: declared {n_decl} incs, got {got[0]}")
            getattr(self.block, ename)(body)
        self.cum += sum(n for (_, n) in bodies.values())
        if self.cum > 40000:
            self.fences.append((self.bar, self.cum))
            self.bar = self.ctx.enter_context(
                self.nc.semaphore(f"bar{len(self.fences)}"))
            self.cum = 0
            self.epoch += 1

    def finish(self):
        def body(eng):
            self._catchup(eng, "gpsimd")
            if self.cum > 0:
                eng.wait_ge(self.bar, self.cum)
        self.block.gpsimd(body)


def _dma_tiles(eng, inc, sb, dram, rows, cols, to_sbuf=True, row0=0, col0=0):
    for k in range(rows // 128):
        s_ap = sb[:, col0 + cols * k: col0 + cols * (k + 1)]
        d_ap = dram[row0 + 128 * k: row0 + 128 * (k + 1), :]
        if to_sbuf:
            inc(eng.dma_start(out=s_ap, in_=d_ap), 16)
        else:
            inc(eng.dma_start(out=d_ap, in_=s_ap), 16)


def coll_phase(ph, kind, op, groups, pairs_io, sem=None):
    def body(eng, inc):
        for (i_ap, o_ap) in pairs_io:
            ins = eng.collective_compute(kind, op, replica_groups=groups,
                                         ins=[i_ap.opt()], outs=[o_ap.opt()])
            if sem is None:
                inc(ins, 1)
            else:
                ins.then_inc(sem, 1)
    ph.phase(gpsimd=(body, 0 if sem is not None else len(pairs_io)))


def coll_await(ph, sem, val, engines=ENGINES):
    bodies = {}
    for e in engines:
        def w(eng, inc, ):
            eng.wait_ge(sem, val)
        bodies[e] = (w, 0)
    ph.phase(**bodies)


def pipe(ph, n, stages):
    maxd = max(d for (_, d, _, _) in stages)
    for t in range(n + maxd):
        todo = {}
        for (ename, d, fn, k) in stages:
            i = t - d
            if 0 <= i < n:
                kv = k(i) if callable(k) else k
                if kv:
                    todo.setdefault(ename, []).append((fn, i, kv))
        bodies = {}
        for ename, items in todo.items():
            def emit(eng, inc, items=items):
                for (fn, i, kv) in items:
                    fn(i, eng, inc)
            bodies[ename] = (emit, sum(kv for (_, _, kv) in items))
        if bodies:
            ph.phase(**bodies)


def pipe_tm2fm(ph, src_dram, sXnF, sxt, sxb, ptr2, sidB):
    def L(k, eng, inc):
        inc(eng.dma_start(out=sxt[:, D * (k % 2):D * (k % 2) + D],
                          in_=src_dram[128 * k:128 * (k + 1), :]), 16)

    def C(k, eng, inc):
        inc(eng.activation(out=sxb[:, D * (k % 2):D * (k % 2) + D],
                           in_=sxt[:, D * (k % 2):D * (k % 2) + D], func=AF.Copy))

    def Tr(k, eng, inc):
        h = 2048 * (k % 2)
        for j in range(NF):
            inc(eng.transpose(ptr2[:, h + 128 * j:h + 128 * (j + 1)],
                              sxb[:, D * (k % 2) + 128 * j:D * (k % 2) + 128 * (j + 1)],
                              sidB[:]))

    def Ev(k, eng, inc):
        h = 2048 * (k % 2)
        for j in range(NF):
            inc(eng.tensor_copy(
                out=sXnF[:, TPC * j + 128 * k:TPC * j + 128 * (k + 1)],
                in_=ptr2[:, h + 128 * j:h + 128 * (j + 1)]))

    pipe(ph, NT, [("sync", 0, L, 16), ("scalar", 1, C, 1),
                  ("tensor", 2, Tr, NF), ("vector", 3, Ev, NF)])



def stage_ln(nc, ph, io, with_maxes, tag):
    x, xn, identF = io["x"], io["xn"], io["identF"]
    with contextlib.ExitStack() as ctx:
        T = lambda name, shape, dt: ctx.enter_context(
            nc.sbuf_tensor(f"{tag}_{name}", shape, dt))
        sX = T("sX", [128, NT * D], F32)
        sXn = T("sXn", [128, NT * D], F32)
        sjunk = T("sjunk", [128, D], F32)
        sid = T("sid", [128, 128], F32)
        sms = T("sms", [128, NT], F32)
        sqs = T("sqs", [128, NT], F32)
        smean = T("smean", [128, NT], F32)
        stmp = T("stmp", [128, NT], F32)
        svar = T("svar", [128, NT], F32)
        srstd = T("srstd", [128, NT], F32)
        sbias = T("sbias", [128, NT], F32)
        sxm = T("sxm", [128, NT], F32)
        swm = T("swm", [128, 16], F32)
        scols = T("scols", [128, 3], F32)
        sred = T("sred", [3, 128], F32)
        sout3 = T("sout3", [3, 1], F32)
        srow3 = T("srow3", [1, 3], F32)
        seps = T("seps", [128, 1], F32)
        if with_maxes:
            swA = T("swA", [128, FFI], F32)
        pt3 = ctx.enter_context(nc.psum_tensor(f"{tag}_pt3", [3, 128], F32))

        def p0(eng, inc):
            inc(eng.dma_start(out=sid[:], in_=identF), 16)
            _dma_tiles(eng, inc, sX, x, TPC, D)
        ph.phase(sync=(p0, 16 * (1 + NT)))

        def p1s(eng, inc):
            for k in range(NT):
                inc(eng.activation(out=sjunk[:], in_=sX[:, D * k:D * (k + 1)],
                                   func=AF.Square, accum_out=sqs[:, k:k + 1]))

        def p1v(eng, inc):
            eng.memset(seps[:], EPS)
            for k in range(NT):
                inc(eng.tensor_reduce(out=sms[:, k:k + 1],
                                      in_=sX[:, D * k:D * (k + 1)],
                                      axis=AX.X, op=ALU.add))
        ph.phase(scalar=(p1s, NT), vector=(p1v, NT))

        if with_maxes:
            wjobs = [(io["wfc_sh"], 0, FFI), (io["wfc_sh"], 128, FFI)] + \
                    [(io["wproj_sh"], 128 * j, D) for j in range(8)]
            for wi, (wt, r0, cw) in enumerate(wjobs):
                def wl(eng, inc, wt=wt, r0=r0, cw=cw):
                    inc(eng.dma_start(out=swA[:, 0:cw], in_=wt[r0:r0 + 128, :]), 16)
                ph.phase(sync=(wl, 16))

                def wr(eng, inc, wi=wi, cw=cw):
                    inc(eng.tensor_reduce(out=swm[:, wi:wi + 1], in_=swA[:, 0:cw],
                                          axis=AX.X, op=ALU.max,
                                          apply_absolute_value=True))
                ph.phase(vector=(wr, 1))

        def p2v(eng, inc):
            inc(eng.tensor_scalar(out=smean[:], in0=sms[:], scalar1=1.0 / D,
                                  scalar2=None, op0=ALU.mult))
        ph.phase(vector=(p2v, 1))

        def p2b(eng, inc):
            inc(eng.activation(out=stmp[:], in_=smean[:], func=AF.Square))
        ph.phase(scalar=(p2b, 1))

        def p2c(eng, inc):
            inc(eng.scalar_tensor_tensor(out=svar[:], in0=sqs[:], scalar=1.0 / D,
                                         in1=stmp[:], op0=ALU.mult,
                                         op1=ALU.subtract))
        ph.phase(vector=(p2c, 1))

        def p3s(eng, inc):
            inc(eng.activation(out=srstd[:], in_=svar[:], func=AF.Sqrt,
                               bias=seps[:, 0:1]))
        ph.phase(scalar=(p3s, 1))

        def p4a(eng, inc):
            inc(eng.reciprocal(out=srstd[:], in_=srstd[:]))
        ph.phase(vector=(p4a, 1))

        def p4b(eng, inc):
            inc(eng.scalar_tensor_tensor(out=sbias[:], in0=smean[:], scalar=-1.0,
                                         in1=srstd[:], op0=ALU.mult, op1=ALU.mult))
        ph.phase(vector=(p4b, 1))

        def p5(eng, inc):
            for k in range(NT):
                inc(eng.activation(out=sXn[:, D * k:D * (k + 1)],
                                   in_=sX[:, D * k:D * (k + 1)], func=AF.Identity,
                                   bias=sbias[:, k:k + 1], scale=srstd[:, k:k + 1]))
        ph.phase(scalar=(p5, NT))

        def p6(eng, inc):
            for k in range(NT):
                inc(eng.tensor_reduce(out=sxm[:, k:k + 1],
                                      in_=sXn[:, D * k:D * (k + 1)], axis=AX.X,
                                      op=ALU.max, apply_absolute_value=True))
        ph.phase(vector=(p6, NT))

        def p7(eng, inc):
            eng.tensor_reduce(out=scols[:, 0:1], in_=sxm[:, 0:NT], axis=AX.X,
                              op=ALU.max)
            if with_maxes:
                eng.tensor_reduce(out=scols[:, 1:2], in_=swm[:, 0:2], axis=AX.X,
                                  op=ALU.max)
                inc(eng.tensor_reduce(out=scols[:, 2:3], in_=swm[:, 2:10],
                                      axis=AX.X, op=ALU.max))
            else:
                inc(eng.memset(scols[:, 1:3], 0.0))
        ph.phase(vector=(p7, 1))

        def p8(eng, inc):
            inc(eng.transpose(pt3[:], scols[:], sid[:]))
        ph.phase(tensor=(p8, 1))

        def p9(eng, inc):
            inc(eng.tensor_copy(out=sred[:], in_=pt3[:]))
        ph.phase(vector=(p9, 1))

        def p10(eng, inc):
            inc(eng.tensor_reduce(out=sout3[:], in_=sred[:], axis=AX.X, op=ALU.max))
        ph.phase(vector=(p10, 1))

        if with_maxes:
            def p11(eng, inc):
                inc(eng.transpose(pt3[0:1, 0:3], sout3[:], sid[0:3, 0:3]))
            ph.phase(tensor=(p11, 1))

            def p12(eng, inc):
                inc(eng.tensor_copy(out=srow3[:], in_=pt3[0:1, 0:3]))
            ph.phase(vector=(p12, 1))

            def p13(eng, inc):
                inc(eng.dma_start(out=io["maxes"], in_=srow3[:]), 16)
            ph.phase(gpsimd=(p13, 16))

        def pst(eng, inc):
            _dma_tiles(eng, inc, sXn, xn, TPC, D, to_sbuf=False)
        ph.phase(gpsimd=(pst, 16 * NT))



def stage_prep0(nc, ph, io, tag="pp"):
    xn, gmax = io["xn"], io["gmax"]
    wq, wk, wv = io["wq"], io["wk"], io["wv"]
    cosT, sinT = io["cosT"], io["sinT"]
    wfc_sh, wproj_sh = io["wfc_sh"], io["wproj_sh"]
    identB = io["identB"]
    qf, kf, vf, x8f = io["qf"], io["kf"], io["vf"], io["x8f"]
    wfc8t_sh, wproj8_sh = io["wfc8t_sh"], io["wproj8_sh"]

    with contextlib.ExitStack() as ctx:
        T = lambda name, shape, dt: ctx.enter_context(
            nc.sbuf_tensor(f"{tag}_{name}", shape, dt))
        sXnF = T("sXnF", [128, NF * TPC], BF16)
        sxt = T("sxt", [128, 2 * D], F32)
        sxb = T("sxb", [128, 2 * D], BF16)
        str_ = T("str", [128, NF * 128], BF16)
        srow9 = T("srow9", [1, 16], F32)
        sone = T("sone", [1, 128], F32)
        scols9 = T("scols9", [128, 12], F32)
        qf8a = T("qf8a", [128, D], FP8)
        qf8b = T("qf8b", [128, D], FP8)
        qua = T("qua", [128, D], BF16)
        qub = T("qub", [128, D], BF16)
        qmk = T("qmk", [128, D], U8)
        qmk2 = T("qmk2", [128, D], U8)
        qout = T("qout", [128, D], BF16)
        sid = T("sid", [128, 128], BF16)
        swf = T("swf", [128, 4 * D], F32)
        swb = T("swb", [128, 4 * D], BF16)
        scs = T("scs", [128, 2 * TPC], F32)
        ssn = T("ssn", [128, 2 * TPC], F32)
        st1 = T("st1", [128, 512], F32)
        st2 = T("st2", [128, 512], F32)
        sQa = T("sQa", [128, 2 * 512], BF16)
        sQb = T("sQb", [128, 2 * 512], BF16)
        sgm = T("sgm", [1, 3], F32)
        p9psum = ctx.enter_context(nc.psum_tensor(f"{tag}_p9", [128, 9], F32))
        p9 = p9psum[:]

        def p0(eng, inc):
            inc(eng.dma_start(out=sid[:], in_=identB), 16)
            inc(eng.dma_start(out=sgm[:], in_=gmax), 16)
        ph.phase(sync=(p0, 32))

        def p2(eng, inc):
            eng.memset(sone[:], 1.0)
            for j in range(3):
                inc(eng.tensor_scalar(out=srow9[0:1, 12 + j:13 + j],
                                      in0=sgm[0:1, j:j + 1],
                                      scalar1=1.0 / FP8MAX, scalar2=1e-6,
                                      op0=ALU.mult, op1=ALU.max))
        ph.phase(vector=(p2, 3))

        def p2x(eng, inc):
            for j in range(3):
                sc = srow9[0:1, 12 + j:13 + j]
                inc(eng.tensor_scalar(out=srow9[0:1, 3 * j + 2:3 * j + 3], in0=sc,
                                      scalar1=240.0, scalar2=None, op0=ALU.mult))
        ph.phase(vector=(p2x, 3))

        def p3(eng, inc):
            for j in range(3):
                inc(eng.reciprocal(out=srow9[0:1, 3 * j:3 * j + 1],
                                   in_=srow9[0:1, 12 + j:13 + j]))
        ph.phase(vector=(p3, 3))

        def p4(eng, inc):
            for j in range(3):
                inc(eng.tensor_scalar(out=srow9[0:1, 3 * j + 1:3 * j + 2],
                                      in0=srow9[0:1, 3 * j:3 * j + 1],
                                      scalar1=0.5, scalar2=None, op0=ALU.mult))
        ph.phase(vector=(p4, 3))

        def p5(eng, inc):
            inc(eng.matmul(p9, sone[:], srow9[0:1, 0:9], start=True, stop=True))
        ph.phase(tensor=(p5, 1))

        def p6(eng, inc):
            inc(eng.tensor_copy(out=scols9[:, 0:9], in_=p9))
        ph.phase(vector=(p6, 1))

        def p6b(eng, inc):
            for j in range(3):
                inc(eng.tensor_scalar(out=scols9[:, 9 + j:10 + j],
                                      in0=scols9[:, 3 * j + 2:3 * j + 3],
                                      scalar1=-1.0, scalar2=None, op0=ALU.mult))
        ph.phase(vector=(p6b, 3))

        def qc_of(j):
            return {"inv": scols9[:, 3 * j:3 * j + 1],
                    "invh": scols9[:, 3 * j + 1:3 * j + 2],
                    "thr": scols9[:, 3 * j + 2:3 * j + 3],
                    "nthr": scols9[:, 9 + j:10 + j]}

        def quant_phases(src_ap, qi, W=D):
            def pact(eng, inc):
                qc = qc_of(qi)
                eng.activation(out=qf8a[:, 0:W], in_=src_ap, func=AF.Identity,
                               scale=qc["inv"])
                inc(eng.activation(out=qua[:, 0:W], in_=qf8a[:, 0:W], func=AF.Copy))
                eng.activation(out=qf8b[:, 0:W], in_=src_ap, func=AF.Identity,
                               scale=qc["invh"])
                inc(eng.activation(out=qub[:, 0:W], in_=qf8b[:, 0:W], func=AF.Copy,
                                   scale=2.0))
            ph.phase(scalar=(pact, 2))

            def pm1(eng, inc):
                qc = qc_of(qi)
                inc(eng.tensor_scalar(out=qmk[:, 0:W], in0=src_ap,
                                      scalar1=qc["thr"], scalar2=None, op0=ALU.is_ge))
                inc(eng.tensor_scalar(out=qmk2[:, 0:W], in0=src_ap,
                                      scalar1=qc["nthr"], scalar2=None, op0=ALU.is_le))
            ph.phase(vector=(pm1, 2))

            def pm2(eng, inc):
                inc(eng.tensor_tensor(out=qmk[:, 0:W], in0=qmk[:, 0:W],
                                      in1=qmk2[:, 0:W], op=ALU.logical_or))
            ph.phase(vector=(pm2, 1))

            def psel(eng, inc):
                inc(eng.select(out=qout[:, 0:W], mask=qmk[:, 0:W],
                               on_true=qub[:, 0:W], on_false=qua[:, 0:W]))
            ph.phase(vector=(psel, 1))

        with nc.psum_tensor(f"{tag}_ptr2", [128, 4096], BF16) as ptr2:
            pipe_tm2fm(ph, xn, sXnF, sxt, sxb, ptr2, sid)

            for k in range(NT):
                s = k % 2

                def pl(eng, inc, k=k, s=s):
                    inc(eng.dma_start(out=sxt[:, D * s:D * (s + 1)],
                                      in_=xn[128 * k:128 * (k + 1), :]), 16)
                ph.phase(sync=(pl, 16))
                quant_phases(sxt[:, D * s:D * (s + 1)], 0)

                def ptj(eng, inc, k=k):
                    h = 2048 * (k % 2)
                    for j in range(NF):
                        inc(eng.transpose(ptr2[:, h + 128 * j:h + 128 * (j + 1)],
                                          qout[:, 128 * j:128 * (j + 1)], sid[:]))
                ph.phase(tensor=(ptj, NF))

                def pej(eng, inc, k=k):
                    h = 2048 * (k % 2)
                    for j in range(NF):
                        inc(eng.tensor_copy(out=str_[:, 128 * j:128 * (j + 1)],
                                            in_=ptr2[:, h + 128 * j:h + 128 * (j + 1)]))
                ph.phase(vector=(pej, NF))

                def psj(eng, inc, k=k):
                    for j in range(NF):
                        inc(eng.dma_start(
                            out=x8f[128 * j:128 * (j + 1), 128 * k:128 * (k + 1)],
                            in_=str_[:, 128 * j:128 * (j + 1)]), 16)
                ph.phase(gpsimd=(psj, 16 * NF))

            for t in range(2):
                for cchunk in range(4):
                    def pl(eng, inc, t=t, cchunk=cchunk):
                        inc(eng.dma_start(out=sxt[:, 0:D],
                                          in_=wfc_sh[128 * t:128 * (t + 1),
                                                     D * cchunk:D * (cchunk + 1)]), 16)
                    ph.phase(sync=(pl, 16))
                    quant_phases(sxt[:, 0:D], 1)

                    def pst(eng, inc, t=t, cchunk=cchunk):
                        for mm in range(16):
                            m = 16 * cchunk + mm
                            r0 = 256 * m + 128 * t
                            inc(eng.dma_start(out=wfc8t_sh[r0:r0 + 128, :],
                                              in_=qout[:, 128 * mm:128 * (mm + 1)]), 16)
                    ph.phase(gpsimd=(pst, 16 * 16))

            for t in range(8):
                def pl(eng, inc, t=t):
                    inc(eng.dma_start(out=sxt[:, 0:D],
                                      in_=wproj_sh[128 * t:128 * (t + 1), :]), 16)
                ph.phase(sync=(pl, 16))
                quant_phases(sxt[:, 0:D], 2)

                def pst(eng, inc, t=t):
                    inc(eng.dma_start(out=wproj8_sh[128 * t:128 * (t + 1), :],
                                      in_=qout[:, 0:D]), 16)
                ph.phase(gpsimd=(pst, 16))

        if "wcoll" in io:
            io["wcoll"]()

        pq = ctx.enter_context(nc.psum_tensor(f"{tag}_pq", [128, 2048], F32))
        jobs = [(wq, qf, True), (wk, kf, True), (wv, vf, False)]

        def L(i, eng, inc):
            u = i // 2
            p, w = u // 3, u % 3
            if i % 2 == 1:
                return
            wten = jobs[w][0]
            so = 2 * D * ((i // 2) % 2)
            if w == 0:
                inc(eng.dma_start(out=scs[:, TPC * (p % 2):TPC * (p % 2) + TPC],
                                  in_=cosT[128 * p:128 * (p + 1), :]), 16)
                inc(eng.dma_start(out=ssn[:, TPC * (p % 2):TPC * (p % 2) + TPC],
                                  in_=sinT[128 * p:128 * (p + 1), :]), 16)
            for half, m in enumerate((p, p + 8)):
                inc(eng.dma_start(
                    out=swf[:, so + D * half:so + D * (half + 1)],
                    in_=wten[m].rearrange("(j p) c -> p j c", p=128)), 16)

        def C(i, eng, inc):
            if i % 2 == 1:
                return
            so = 2 * D * ((i // 2) % 2)
            inc(eng.activation(out=swb[:, so:so + 2 * D],
                               in_=swf[:, so:so + 2 * D], func=AF.Copy))

        def M(i, eng, inc):
            u, h = i // 2, i % 2
            so = 2 * D * ((i // 2) % 2)
            sl = 1024 * (i % 2)
            for half in range(2):
                for j in range(NF):
                    mm = eng.matmul(
                        pq[:, sl + 512 * half:sl + 512 * (half + 1)],
                        swb[:, so + D * half + 128 * j:so + D * half + 128 * (j + 1)],
                        sXnF[:, TPC * j + 512 * h:TPC * j + 512 * (h + 1)],
                        start=(j == 0), stop=(j == NF - 1))
                inc(mm)

        def RP(i, eng, inc):
            u, h = i // 2, i % 2
            p, w = u // 3, u % 3
            rope = jobs[w][2]
            sl = 1024 * (i % 2)
            qo = 512 * (i % 2)
            pm = pq[:, sl:sl + 512]
            pm8 = pq[:, sl + 512:sl + 1024]
            if rope:
                cc = scs[:, TPC * (p % 2) + 512 * h:TPC * (p % 2) + 512 * (h + 1)]
                ss = ssn[:, TPC * (p % 2) + 512 * h:TPC * (p % 2) + 512 * (h + 1)]
                eng.tensor_tensor(out=st1[:, 0:512], in0=cc, in1=pm, op=ALU.mult)
                eng.tensor_tensor(out=st2[:, 0:512], in0=ss, in1=pm8, op=ALU.mult)
                inc(eng.tensor_tensor(out=sQa[:, qo:qo + 512], in0=st1[:, 0:512],
                                      in1=st2[:, 0:512], op=ALU.subtract))
                eng.tensor_tensor(out=st1[:, 0:512], in0=cc, in1=pm8, op=ALU.mult)
                eng.tensor_tensor(out=st2[:, 0:512], in0=ss, in1=pm, op=ALU.mult)
                inc(eng.tensor_tensor(out=sQb[:, qo:qo + 512], in0=st1[:, 0:512],
                                      in1=st2[:, 0:512], op=ALU.add))
            else:
                inc(eng.tensor_copy(out=sQa[:, qo:qo + 512], in_=pm))
                inc(eng.tensor_copy(out=sQb[:, qo:qo + 512], in_=pm8))

        def ST(i, eng, inc):
            u, h = i // 2, i % 2
            p, w = u // 3, u % 3
            outd = jobs[w][1]
            qo = 512 * (i % 2)
            inc(eng.dma_start(out=outd[128 * p:128 * (p + 1), 512 * h:512 * (h + 1)],
                              in_=sQa[:, qo:qo + 512]), 16)
            inc(eng.dma_start(out=outd[128 * (p + 8):128 * (p + 9), 512 * h:512 * (h + 1)],
                              in_=sQb[:, qo:qo + 512]), 16)

        pipe(ph, 48, [
            ("sync", 0, L, lambda i: 0 if i % 2 else (16 * 4 if (i // 2) % 3 == 0 else 16 * 2)),
            ("scalar", 1, C, lambda i: 0 if i % 2 else 1),
            ("tensor", 2, M, 2),
            ("vector", 3, RP, 2),
            ("gpsimd", 4, ST, 32),
        ])



def stage_attn0(nc, ph, io, tag="a0"):
    qg, kg, vg = io["qg"], io["kg"], io["vg"]
    gate_row, sel_row = io["gate_row"], io["sel_row"]
    identB, identF = io["identB"], io["identF"]
    combh0, combh1 = io["combh0"], io["combh1"]
    NH = 8

    with contextlib.ExitStack() as ctx:
        T = lambda name, shape, dt: ctx.enter_context(
            nc.sbuf_tensor(f"{tag}_{name}", shape, dt))
        sQH = T("sQH", [128, NH * SEQ], BF16)
        sKH = T("sKH", [128, NH * SEQ], BF16)
        sVH = T("sVH", [128, NH * SEQ], BF16)
        sCMB = T("sCMB", [128, NH * SEQ], BF16)
        sCAND = T("sCAND", [128, 6 * TPC], BF16)
        sidB = T("sidB", [128, 128], BF16)
        sidF = T("sidF", [128, 128], F32)
        sqsc = T("sqsc", [128, SEQ], BF16)
        se1 = T("se1", [128, SEQ], F32)
        se2 = T("se2", [128, SEQ], F32)
        sqf = T("sqf", [128, SEQ], BF16)
        skf = T("skf", [128, SEQ], BF16)
        skfT = T("skfT", [128, SEQ], BF16)
        svT = T("svT", [128, 16 * 129], BF16)
        snt = T("snt", [128, 1], F32)
        smem = T("smem", [128, 129], BF16)
        sneg = T("sneg", [128, 16], F32)
        sneg2 = T("sneg2", [128, 32], F32)
        snegrb = T("snegrb", [16, 128], BF16)
        snegrow = T("snegrow", [1, SEQ], BF16)
        sonerow = T("sonerow", [1, 128], BF16)
        sonerowF = T("sonerowF", [1, 128], F32)
        sPT = T("sPT", [128, 16 * 512], BF16)
        sgr = T("sgr", [1, 40], F32)
        sgc = T("sgc", [128, 16], F32)
        sg1c = T("sg1c", [128, 16], F32)
        ssel = T("ssel", [128, 2], F32)
        srA = T("srA", [128, 4], F32)
        srM = T("srM", [128, 4], F32)
        stA = T("stA", [128, 512], F32)
        stM = T("stM", [128, 512], F32)
        sCT = T("sCT", [128, 512], BF16)
        pSA = ctx.enter_context(nc.psum_tensor(f"{tag}_pSA", [128, 2048], F32))
        pSS = ctx.enter_context(nc.psum_tensor(f"{tag}_pSS", [128, 4096], BF16))
        pCT = pSS

        def p0(eng, inc):
            inc(eng.dma_start(out=sidB[:], in_=identB), 16)
            inc(eng.dma_start(out=sidF[:], in_=identF), 16)
            inc(eng.dma_start(out=sgr[0:1, 0:16], in_=gate_row), 16)
            inc(eng.dma_start(out=sgr[0:1, 32:34], in_=sel_row), 16)
        ph.phase(sync=(p0, 64))

        def pg1(eng, inc):
            inc(eng.activation(out=sgr[0:1, 16:32], in_=sgr[0:1, 0:16],
                               func=AF.Sigmoid))
        ph.phase(scalar=(pg1, 1))

        def pg2(eng, inc):
            inc(eng.tensor_scalar(out=sgr[0:1, 0:16], in0=sgr[0:1, 16:32],
                                  scalar1=-1.0, scalar2=1.0, op0=ALU.mult,
                                  op1=ALU.add))
            eng.memset(sonerowF[:], 1.0)
            eng.memset(sonerow[:], 1.0)
            for j in range(16):
                eng.memset(svT[:, 129 * j + 128:129 * (j + 1)], 1.0)
            inc(eng.memset(ssel[:], 0.0))
        ph.phase(vector=(pg2, 2))

        def pg3(eng, inc):
            eng.matmul(pSA[0:128, 0:16], sonerowF[:], sgr[0:1, 16:32],
                       start=True, stop=True)
            eng.matmul(pSA[0:128, 16:32], sonerowF[:], sgr[0:1, 0:16],
                       start=True, stop=True)
            inc(eng.matmul(pSA[0:128, 32:34], sonerowF[:], sgr[0:1, 32:34],
                           start=True, stop=True))
        ph.phase(tensor=(pg3, 1))

        def pg4(eng, inc):
            eng.tensor_copy(out=sgc[:], in_=pSA[0:128, 0:16])
            eng.tensor_copy(out=sg1c[:], in_=pSA[0:128, 16:32])
            inc(eng.tensor_copy(out=ssel[:], in_=pSA[0:128, 32:34]))
        ph.phase(vector=(pg4, 1))

        for i in range(NH):
            def bl(eng, inc, i=i):
                for w, (gt, dst) in enumerate(((qg, sQH), (kg, sKH), (vg, sVH))):
                    for half in range(2):
                        r0 = 2048 * half + 128 * i
                        inc(eng.dma_start(
                            out=dst[:, SEQ * i + TPC * half:SEQ * i + TPC * (half + 1)],
                            in_=gt[r0:r0 + 128, :]), 16)
                        inc(eng.dma_start(
                            out=sCAND[:, TPC * (2 * w + half):TPC * (2 * w + half + 1)],
                            in_=gt[r0 + 1024:r0 + 1024 + 128, :]), 16)
            ph.phase(sync=(bl, 16 * 12))

            def b1(eng, inc, i=i):
                for w, (gt, dst) in enumerate(((qg, sQH), (kg, sKH), (vg, sVH))):
                    for half in range(2):
                        sl = dst[:, SEQ * i + TPC * half:SEQ * i + TPC * (half + 1)]
                        inc(eng.tensor_scalar(out=sl, in0=sl, scalar1=ssel[:, 0:1],
                                              scalar2=None, op0=ALU.mult))
            ph.phase(vector=(b1, 6))

            def b2(eng, inc, i=i):
                for w, (gt, dst) in enumerate(((qg, sQH), (kg, sKH), (vg, sVH))):
                    for half in range(2):
                        c = TPC * (2 * w + half)
                        sl = dst[:, SEQ * i + TPC * half:SEQ * i + TPC * (half + 1)]
                        inc(eng.scalar_tensor_tensor(
                            out=sl, in0=sCAND[:, c:c + TPC], scalar=ssel[:, 1:2],
                            in1=sl, op0=ALU.mult, op1=ALU.add))
            ph.phase(vector=(b2, 6))

        for i in range(NH):
            qi = sQH[:, SEQ * i:SEQ * (i + 1)]
            ki = sKH[:, SEQ * i:SEQ * (i + 1)]
            vi = sVH[:, SEQ * i:SEQ * (i + 1)]

            def a1(eng, inc, qi=qi):
                inc(eng.activation(out=sqsc[:], in_=qi, func=AF.Identity,
                                   scale=SM_SCALE0))
            ph.phase(scalar=(a1, 1))

            for (src_, dst_) in ((qi, sqf), (ki, skf)):
                def a2(eng, inc, src_=src_):
                    inc(eng.tensor_scalar(out=se1[:], in0=src_, scalar1=0.0,
                                          scalar2=None, op0=ALU.min))
                    inc(eng.tensor_scalar(out=se2[:], in0=src_, scalar1=0.0,
                                          scalar2=None, op0=ALU.max))
                ph.phase(vector=(a2, 2))

                def a3(eng, inc):
                    inc(eng.activation(out=se1[:], in_=se1[:], func=AF.Exp))
                ph.phase(scalar=(a3, 1))

                def a4(eng, inc, dst_=dst_):
                    inc(eng.tensor_add(out=dst_[:], in0=se1[:], in1=se2[:]))
                ph.phase(vector=(a4, 1))

            def a5(eng, inc):
                inc(eng.tensor_reduce(out=snt[:], in_=skf[:], axis=AX.X, op=ALU.add))
            ph.phase(vector=(a5, 1))

            def a6(eng, inc):
                inc(eng.tensor_copy(out=smem[:, 128:129], in_=snt[:]))
            ph.phase(vector=(a6, 1))

            for g4 in range(4):
                def b1(eng, inc, g4=g4):
                    for u in range(4):
                        j = 4 * g4 + u
                        inc(eng.transpose(pCT[:, 128 * u:128 * (u + 1)],
                                          skf[:, 128 * j:128 * (j + 1)], sidB[:]))
                ph.phase(tensor=(b1, 4))

                def b2(eng, inc, g4=g4):
                    for u in range(4):
                        j = 4 * g4 + u
                        inc(eng.tensor_copy(out=skfT[:, 128 * j:128 * (j + 1)],
                                            in_=pCT[:, 128 * u:128 * (u + 1)]))
                ph.phase(vector=(b2, 4))
            for g4 in range(4):
                def b3(eng, inc, g4=g4, vi=vi):
                    for u in range(4):
                        j = 4 * g4 + u
                        inc(eng.transpose(pCT[:, 128 * u:128 * (u + 1)],
                                          vi[:, 128 * j:128 * (j + 1)], sidB[:]))
                ph.phase(tensor=(b3, 4))

                def b4(eng, inc, g4=g4):
                    for u in range(4):
                        j = 4 * g4 + u
                        inc(eng.tensor_copy(out=svT[:, 129 * j:129 * j + 128],
                                            in_=pCT[:, 128 * u:128 * (u + 1)]))
                ph.phase(vector=(b4, 4))

            def c1(eng, inc):
                for j in range(16):
                    mm = eng.matmul(pSA[:, 1536:1664], skfT[:, 128 * j:128 * (j + 1)],
                                    svT[:, 129 * j:129 * j + 128],
                                    start=(j == 0), stop=(j == 15))
                inc(mm)
            ph.phase(tensor=(c1, 1))

            def c2(eng, inc):
                inc(eng.activation(out=smem[:, 0:128], in_=pSA[:, 1536:1664],
                                   func=AF.Copy))
            ph.phase(scalar=(c2, 1))

            def d1(i, eng, inc, ki=ki):
                u, hf = i // 2, i % 2
                base = 1024 * (i % 2)
                for c2 in range(2):
                    mm = eng.matmul(pSA[:, base + 512 * c2:base + 512 * (c2 + 1)],
                                    sqsc[:, 128 * u:128 * (u + 1)],
                                    ki[:, 1024 * hf + 512 * c2:
                                        1024 * hf + 512 * (c2 + 1)],
                                    start=True, stop=True)
                inc(mm)

            def d2(i, eng, inc):
                base = 1024 * (i % 2)
                inc(eng.tensor_reduce(out=sneg2[:, i:i + 1],
                                      in_=pSA[:, base:base + 1024],
                                      axis=AX.X, op=ALU.max, negate=True))

            pipe(ph, 32, [("tensor", 0, d1, 1), ("vector", 1, d2, 1)])

            def d3(eng, inc):
                for u in range(16):
                    inc(eng.tensor_reduce(out=sneg[:, u:u + 1],
                                          in_=sneg2[:, 2 * u:2 * (u + 1)],
                                          axis=AX.X, op=ALU.min))
            ph.phase(vector=(d3, 16))

            def e1(eng, inc):
                inc(eng.transpose(pSA[0:16, 0:128], sneg[:], sidF[:]))
            ph.phase(tensor=(e1, 1))

            def e2(eng, inc):
                inc(eng.tensor_copy(out=snegrb[:], in_=pSA[0:16, 0:128]))
            ph.phase(vector=(e2, 1))

            def e3(eng, inc):
                for r in range(16):
                    inc(eng.dma_start(out=snegrow[0:1, 128 * r:128 * (r + 1)],
                                      in_=snegrb[r:r + 1, :]), 16)
            ph.phase(gpsimd=(e3, 256))

            for qb in range(4):
                def f1(kp, eng, inc, qb=qb, ki=ki):
                    base = 1024 * (kp % 2)
                    for w in range(2):
                        kb = 2 * kp + w
                        p = pSA[:, base + 512 * w:base + 512 * (w + 1)]
                        eng.matmul(p, ki[:, 128 * kb:128 * (kb + 1)],
                                   sqsc[:, 512 * qb:512 * (qb + 1)],
                                   start=True, stop=False)
                        mm = eng.matmul(p, sonerow[:],
                                        snegrow[0:1, 512 * qb:512 * (qb + 1)],
                                        start=False, stop=True)
                    inc(mm)

                def f2(kp, eng, inc):
                    base = 1024 * (kp % 2)
                    inc(eng.activation(out=sPT[:, 1024 * kp:1024 * (kp + 1)],
                                       in_=pSA[:, base:base + 1024], func=AF.Exp))

                pipe(ph, 8, [("tensor", 0, f1, 1), ("scalar", 1, f2, 1)])

                AVs = [pSA[:, 0:129], pSA[:, 129:258], pSA[:, 512:641],
                       pSA[:, 641:770]]
                MOs = [pSA[:, 1024:1153], pSA[:, 1153:1282], pSA[:, 1536:1665],
                       pSA[:, 1665:1794]]

                def f3(eng, inc, qb=qb):
                    for v4 in range(4):
                        for kb in range(16):
                            mm = eng.matmul(
                                AVs[v4],
                                sPT[:, 512 * kb + 128 * v4:512 * kb + 128 * (v4 + 1)],
                                svT[:, 129 * kb:129 * (kb + 1)],
                                start=(kb == 0), stop=(kb == 15))
                        inc(mm)
                        inc(eng.matmul(MOs[v4], sqf[:, 512 * qb + 128 * v4:
                                                512 * qb + 128 * (v4 + 1)],
                                       smem[:, 0:129], start=True, stop=True))
                ph.phase(tensor=(f3, 8))

                def f4(eng, inc):
                    for v4 in range(4):
                        inc(eng.reciprocal(out=srA[:, v4:v4 + 1],
                                           in_=AVs[v4][:, 128:129]))
                        inc(eng.reciprocal(out=srM[:, v4:v4 + 1],
                                           in_=MOs[v4][:, 128:129]))
                ph.phase(vector=(f4, 8))

                def f5(eng, inc, i=i):
                    for v4 in range(4):
                        inc(eng.tensor_tensor(out=srA[:, v4:v4 + 1],
                                              in0=srA[:, v4:v4 + 1],
                                              in1=sg1c[:, i:i + 1], op=ALU.mult))
                        inc(eng.tensor_tensor(out=srM[:, v4:v4 + 1],
                                              in0=srM[:, v4:v4 + 1],
                                              in1=sgc[:, i:i + 1], op=ALU.mult))
                ph.phase(vector=(f5, 8))

                def f6(eng, inc):
                    for v4 in range(4):
                        inc(eng.activation(out=stA[:, 128 * v4:128 * (v4 + 1)],
                                           in_=AVs[v4][:, 0:128],
                                           func=AF.Identity, scale=srA[:, v4:v4 + 1]))
                        inc(eng.activation(out=stM[:, 128 * v4:128 * (v4 + 1)],
                                           in_=MOs[v4][:, 0:128],
                                           func=AF.Identity, scale=srM[:, v4:v4 + 1]))
                ph.phase(scalar=(f6, 8))

                def f7(eng, inc):
                    inc(eng.tensor_add(out=sCT[:], in0=stA[:], in1=stM[:]))
                ph.phase(vector=(f7, 1))

                def f8(eng, inc):
                    for v4 in range(4):
                        inc(eng.transpose(pCT[:, 128 * v4:128 * (v4 + 1)],
                                          sCT[:, 128 * v4:128 * (v4 + 1)], sidB[:]))
                ph.phase(tensor=(f8, 4))

                def f9(eng, inc, i=i, qb=qb):
                    for v4 in range(4):
                        inc(eng.tensor_copy(
                            out=sCMB[:, SEQ * i + 512 * qb + 128 * v4:
                                     SEQ * i + 512 * qb + 128 * (v4 + 1)],
                            in_=pCT[:, 128 * v4:128 * (v4 + 1)]))
                ph.phase(vector=(f9, 4))

        def pst(eng, inc):
            for i in range(NH):
                inc(eng.dma_start(out=combh0[128 * i:128 * (i + 1), :],
                                  in_=sCMB[:, SEQ * i:SEQ * i + TPC]), 16)
                inc(eng.dma_start(out=combh1[128 * i:128 * (i + 1), :],
                                  in_=sCMB[:, SEQ * i + TPC:SEQ * (i + 1)]), 16)
        ph.phase(gpsimd=(pst, 16 * 16))



def stage_fc0(nc, ph, io, tag="fc"):
    x8f, wfc8g, gmax, identF = io["x8f"], io["wfc8g"], io["gmax"], io["identF"]
    hbuf, hmax = io["hbuf"], io["hmax"]

    with contextlib.ExitStack() as ctx:
        T = lambda name, shape, dt: ctx.enter_context(
            nc.sbuf_tensor(f"{tag}_{name}", shape, dt))
        sX8 = T("sX8", [128, NF * TPC], BF16)
        sw = T("sw", [128, 2 * D], BF16)
        sidF = T("sidF", [128, 128], F32)
        srel = T("srel", [128, TPC], F32)
        sh = T("sh", [128, 2 * TPC], F32)
        srow = T("srow", [1, 8], F32)
        sone = T("sone", [1, 128], F32)
        sdq = T("sdq", [128, 1], F32)
        shm = T("shm", [128, NFI], F32)
        shm1 = T("shm1", [128, 1], F32)
        shmr = T("shmr", [1, 128], F32)
        sout1 = T("sout1", [1, 1], F32)
        sgm = T("sgm", [1, 3], F32)
        pacc = ctx.enter_context(nc.psum_tensor(f"{tag}_pacc", [128, 2 * TPC], F32))
        pRep = ctx.enter_context(nc.psum_tensor(f"{tag}_pRep", [128, 1], F32))

        def p0(eng, inc):
            inc(eng.dma_start(out=sidF[:], in_=identF), 16)
            inc(eng.dma_start(out=sgm[:], in_=gmax), 16)
            _dma_tiles(eng, inc, sX8, x8f, D, TPC)
        ph.phase(sync=(p0, 16 * 18))

        def p2(eng, inc):
            eng.memset(sone[:], 1.0)
            inc(eng.tensor_scalar(out=srow[0:1, 2:3], in0=sgm[0:1, 0:1],
                                  scalar1=1.0 / FP8MAX, scalar2=1e-6,
                                  op0=ALU.mult, op1=ALU.max))
            inc(eng.tensor_scalar(out=srow[0:1, 3:4], in0=sgm[0:1, 1:2],
                                  scalar1=1.0 / FP8MAX, scalar2=1e-6,
                                  op0=ALU.mult, op1=ALU.max))
        ph.phase(vector=(p2, 2))

        def p3(eng, inc):
            inc(eng.tensor_tensor(out=srow[0:1, 4:5], in0=srow[0:1, 2:3],
                                  in1=srow[0:1, 3:4], op=ALU.mult))
        ph.phase(vector=(p3, 1))

        def p4(eng, inc):
            inc(eng.matmul(pRep[:], sone[:], srow[0:1, 4:5], start=True, stop=True))
        ph.phase(tensor=(p4, 1))

        def p5(eng, inc):
            inc(eng.tensor_copy(out=sdq[:], in_=pRep[:]))
        ph.phase(vector=(p5, 1))

        def L(m, eng, inc):
            for a in range(NF // 2):
                r0 = 16384 * a + 256 * m
                inc(eng.dma_start(
                    out=sw[:, D * (m % 2) + 256 * a:D * (m % 2) + 256 * (a + 1)],
                    in_=wfc8g[r0:r0 + 256, :].rearrange("(j p) c -> p j c", p=128)), 16)

        def M(m, eng, inc):
            for h in range(2):
                for j in range(NF):
                    mm = eng.matmul(pacc[:, TPC * (m % 2) + 512 * h:
                                         TPC * (m % 2) + 512 * (h + 1)],
                                    sw[:, D * (m % 2) + 128 * j:
                                       D * (m % 2) + 128 * (j + 1)],
                                    sX8[:, TPC * j + 512 * h:TPC * j + 512 * (h + 1)],
                                    start=(j == 0), stop=(j == NF - 1))
                inc(mm)

        def R(m, eng, inc):
            eng.activation(out=srel[:], in_=pacc[:, TPC * (m % 2):TPC * (m % 2 + 1)],
                           func=AF.Relu)
            inc(eng.activation(out=sh[:, TPC * (m % 2):TPC * (m % 2 + 1)],
                               in_=srel[:], func=AF.Square, scale=sdq[:, 0:1]), 2)

        def V(m, eng, inc):
            inc(eng.tensor_reduce(out=shm[:, m:m + 1],
                                  in_=sh[:, TPC * (m % 2):TPC * (m % 2 + 1)],
                                  axis=AX.X, op=ALU.max))

        def S(m, eng, inc):
            inc(eng.dma_start(out=hbuf[128 * m:128 * (m + 1), :],
                              in_=sh[:, TPC * (m % 2):TPC * (m % 2 + 1)]), 16)

        pipe(ph, NFI, [("sync", 0, L, 16 * 8), ("tensor", 1, M, 2),
                       ("scalar", 2, R, 2), ("vector", 3, V, 1),
                       ("gpsimd", 3, S, 16)])

        def pf1(eng, inc):
            inc(eng.tensor_reduce(out=shm1[:], in_=shm[:], axis=AX.X, op=ALU.max))
        ph.phase(vector=(pf1, 1))

        def pf2(eng, inc):
            inc(eng.transpose(pacc[0:1, 0:128], shm1[:], sidF[:]))
        ph.phase(tensor=(pf2, 1))

        def pf3(eng, inc):
            inc(eng.tensor_copy(out=shmr[:], in_=pacc[0:1, 0:128]))
        ph.phase(vector=(pf3, 1))

        def pf4(eng, inc):
            inc(eng.tensor_reduce(out=sout1[:], in_=shmr[:], axis=AX.X, op=ALU.max))
        ph.phase(vector=(pf4, 1))

        def pf5(eng, inc):
            inc(eng.dma_start(out=hmax, in_=sout1[:]), 16)
        ph.phase(gpsimd=(pf5, 16))



def stage_proj0(nc, ph, io, tag="pj"):
    x, combA, combB, woT = io["x"], io["combA"], io["combB"], io["woT"]
    hbuf, wproj8g = io["hbuf"], io["wproj8g"]
    gmax, ghmax, sel_row = io["gmax"], io["ghmax"], io["sel_row"]
    x1 = io["x1"]

    with contextlib.ExitStack() as ctx:
        T = lambda name, shape, dt: ctx.enter_context(
            nc.sbuf_tensor(f"{tag}_{name}", shape, dt))
        sCMB = T("sCMB", [128, NF * TPC], BF16)
        sCA = T("sCA", [128, 2 * TPC], BF16)
        sCB = T("sCB", [128, 3 * TPC], BF16)
        sblt = T("sblt", [128, 2 * TPC], BF16)
        sR1 = T("sR1", [128, NT * D], F32)
        swf = T("swf", [128, 2 * 512], F32)
        swb = T("swb", [128, 2 * 512], BF16)
        sxs = T("sxs", [128, 2 * 512], F32)
        srow = T("srow", [1, 16], F32)
        ssel = T("ssel", [128, 2], F32)
        sselr = T("sselr", [1, 2], F32)
        sone = T("sone", [1, 128], F32)
        sqc = T("sqc", [128, 8], F32)
        sht = T("sht", [128, 2 * 512], F32)
        sh8 = T("sh8", [128, NFI * 512], BF16)
        swp = T("swp", [128, 8 * 512], BF16)
        qf8a = T("qf8a", [128, 512], FP8)
        qf8b = T("qf8b", [128, 512], FP8)
        qua = T("qua", [128, 3 * 512], BF16)
        qub = T("qub", [128, 3 * 512], BF16)
        qmk = T("qmk", [128, 3 * 512], U8)
        qmk2 = T("qmk2", [128, 2 * 512], U8)
        sgm = T("sgm", [1, 3], F32)
        sghm = T("sghm", [1, 1], F32)
        pW = ctx.enter_context(nc.psum_tensor(f"{tag}_pW", [128, 8 * 512], F32))

        def p0(eng, inc):
            inc(eng.dma_start(out=sselr[:], in_=sel_row), 16)
            inc(eng.dma_start(out=sgm[:], in_=gmax), 16)
            inc(eng.dma_start(out=sghm[:], in_=ghmax), 16)
        ph.phase(sync=(p0, 48))

        def p1(eng, inc):
            eng.memset(sone[:], 1.0)
            inc(eng.tensor_scalar(out=srow[0:1, 2:3], in0=sghm[0:1, 0:1],
                                  scalar1=1.0 / FP8MAX, scalar2=1e-6,
                                  op0=ALU.mult, op1=ALU.max))
            inc(eng.tensor_scalar(out=srow[0:1, 3:4], in0=sgm[0:1, 2:3],
                                  scalar1=1.0 / FP8MAX, scalar2=1e-6,
                                  op0=ALU.mult, op1=ALU.max))
        ph.phase(vector=(p1, 2))

        def p3(eng, inc):
            inc(eng.tensor_tensor(out=srow[0:1, 4:5], in0=srow[0:1, 2:3],
                                  in1=srow[0:1, 3:4], op=ALU.mult))
            inc(eng.reciprocal(out=srow[0:1, 5:6], in_=srow[0:1, 2:3]))
        ph.phase(vector=(p3, 2))

        def p4(eng, inc):
            inc(eng.tensor_scalar(out=srow[0:1, 6:7], in0=srow[0:1, 5:6],
                                  scalar1=0.5, scalar2=None, op0=ALU.mult))
            inc(eng.tensor_scalar(out=srow[0:1, 7:8], in0=srow[0:1, 2:3],
                                  scalar1=240.0, scalar2=None, op0=ALU.mult))
        ph.phase(vector=(p4, 2))

        def p4b(eng, inc):
            inc(eng.tensor_scalar(out=srow[0:1, 8:9], in0=srow[0:1, 7:8],
                                  scalar1=-1.0, scalar2=None, op0=ALU.mult))
        ph.phase(vector=(p4b, 1))

        def p5(eng, inc):
            eng.matmul(pW[0:128, 0:5], sone[:], srow[0:1, 4:9], start=True, stop=True)
            inc(eng.matmul(pW[0:128, 8:10], sone[:], sselr[:], start=True, stop=True))
        ph.phase(tensor=(p5, 1))

        def p6(eng, inc):
            inc(eng.tensor_copy(out=sqc[:, 0:5], in_=pW[0:128, 0:5]))
            inc(eng.tensor_copy(out=ssel[:], in_=pW[0:128, 8:10]))
        ph.phase(vector=(p6, 2))
        DQ2, INVH, INVH2, THR, NTHR = (sqc[:, j:j + 1] for j in range(5))

        if "comb_await" in io:
            io["comb_await"]()

        def CL(j, eng, inc):
            s = j % 2
            inc(eng.dma_start(out=sCA[:, TPC * s:TPC * (s + 1)],
                              in_=combA[128 * j:128 * (j + 1), :]), 16)
            inc(eng.dma_start(out=sCB[:, TPC * (j % 3):TPC * (j % 3) + TPC],
                              in_=combB[128 * j:128 * (j + 1), :]), 16)

        def C1(j, eng, inc):
            s = j % 2
            inc(eng.tensor_scalar(out=sblt[:, TPC * s:TPC * (s + 1)],
                                  in0=sCA[:, TPC * s:TPC * (s + 1)],
                                  scalar1=ssel[:, 0:1], scalar2=None, op0=ALU.mult))

        def C2(j, eng, inc):
            s = j % 2
            inc(eng.scalar_tensor_tensor(out=sCMB[:, TPC * j:TPC * (j + 1)],
                                         in0=sCB[:, TPC * (j % 3):TPC * (j % 3) + TPC],
                                         scalar=ssel[:, 1:2],
                                         in1=sblt[:, TPC * s:TPC * (s + 1)],
                                         op0=ALU.mult, op1=ALU.add))

        pipe(ph, NF, [("sync", 0, CL, 32), ("vector", 1, C1, 1),
                      ("vector", 2, C2, 1)])

        for cP in range(4):
            def WL(j, eng, inc, cP=cP):
                inc(eng.dma_start(out=swf[:, 512 * (j % 2):512 * (j % 2) + 512],
                                  in_=woT[128 * j:128 * (j + 1),
                                          512 * cP:512 * (cP + 1)]), 16)

            def WC(j, eng, inc):
                s = j % 2
                inc(eng.activation(out=swb[:, 512 * s:512 * (s + 1)],
                                   in_=swf[:, 512 * s:512 * (s + 1)], func=AF.Copy))

            def WM(j, eng, inc):
                s = j % 2
                for t in range(8):
                    mm = eng.matmul(pW[:, 512 * t:512 * (t + 1)],
                                    sCMB[:, TPC * j + 128 * t:TPC * j + 128 * (t + 1)],
                                    swb[:, 512 * s:512 * (s + 1)],
                                    start=(j == 0), stop=(j == NF - 1))
                inc(mm)

            pipe(ph, NF, [("sync", 0, WL, 16), ("scalar", 1, WC, 1),
                          ("tensor", 2, WM, 1)])

            def XL(t, eng, inc, cP=cP):
                inc(eng.dma_start(out=sxs[:, 512 * (t % 2):512 * (t % 2) + 512],
                                  in_=x[128 * t:128 * (t + 1),
                                        512 * cP:512 * (cP + 1)]), 16)

            def XE(t, eng, inc, cP=cP):
                inc(eng.tensor_add(out=sR1[:, D * t + 512 * cP:D * t + 512 * (cP + 1)],
                                   in0=sxs[:, 512 * (t % 2):512 * (t % 2) + 512],
                                   in1=pW[:, 512 * t:512 * (t + 1)]))

            pipe(ph, 8, [("sync", 0, XL, 16), ("vector", 1, XE, 1)])

        for half in range(2):
            hs0 = 512 * half

            NH2 = NFI // 2
            def HL(j, eng, inc, hs0=hs0):
                inc(eng.dma_start(out=sht[:, 512 * (j % 2):512 * (j % 2) + 512],
                                  in_=hbuf[128 * j:128 * (j + 1), hs0:hs0 + 512]), 16)

            def HA(j, eng, inc):
                src_ = sht[:, 512 * (j % 2):512 * (j % 2) + 512]
                r = 512 * (j % 3)
                eng.activation(out=qf8a[:], in_=src_, func=AF.Identity, scale=INVH)
                inc(eng.activation(out=qua[:, r:r + 512], in_=qf8a[:],
                                   func=AF.Copy), 1)
                eng.activation(out=qf8b[:], in_=src_, func=AF.Identity, scale=INVH2)
                inc(eng.activation(out=qub[:, r:r + 512], in_=qf8b[:],
                                   func=AF.Copy, scale=2.0), 1)

            def HM(j, eng, inc):
                src_ = sht[:, 512 * (j % 2):512 * (j % 2) + 512]
                r = 512 * (j % 3)
                inc(eng.tensor_scalar(out=qmk[:, r:r + 512], in0=src_, scalar1=THR,
                                      scalar2=None, op0=ALU.is_ge))
                inc(eng.tensor_scalar(out=qmk2[:, 512 * (j % 2):512 * (j % 2) + 512],
                                      in0=src_, scalar1=NTHR,
                                      scalar2=None, op0=ALU.is_le))
            def HO(j, eng, inc):
                r = 512 * (j % 3)
                inc(eng.tensor_tensor(out=qmk[:, r:r + 512], in0=qmk[:, r:r + 512],
                                      in1=qmk2[:, 512 * (j % 2):512 * (j % 2) + 512],
                                      op=ALU.logical_or))

            def HS(j, eng, inc):
                r = 512 * (j % 3)
                inc(eng.select(out=sh8[:, 512 * j:512 * (j + 1)],
                               mask=qmk[:, r:r + 512], on_true=qub[:, r:r + 512],
                               on_false=qua[:, r:r + 512]))

            pipe(ph, NFI, [("sync", 0, HL, 16), ("scalar", 1, HA, 2),
                           ("vector", 1, HM, 2), ("vector", 2, HO, 1),
                           ("vector", 3, HS, 1)])

            NG = 16
            for cP in range(4):
                def L2(g, eng, inc, cP=cP):
                    for q in range(4):
                        mj = 4 * g + q
                        inc(eng.dma_start(
                            out=swp[:, 2048 * (g % 2) + 512 * q:
                                     2048 * (g % 2) + 512 * (q + 1)],
                            in_=wproj8g[128 * mj:128 * (mj + 1),
                                        512 * cP:512 * (cP + 1)]), 16)

                def M2(g, eng, inc):
                    for q in range(4):
                        mj = 4 * g + q
                        for t in range(4):
                            mm = eng.matmul(pW[:, 512 * t:512 * (t + 1)],
                                            sh8[:, 512 * mj + 128 * t:
                                                 512 * mj + 128 * (t + 1)],
                                            swp[:, 2048 * (g % 2) + 512 * q:
                                                 2048 * (g % 2) + 512 * (q + 1)],
                                            start=(mj == 0), stop=(mj == NFI - 1))
                    inc(mm)

                pipe(ph, NG, [("sync", 0, L2, 64), ("tensor", 1, M2, 1)])

                def SC(t, eng, inc):
                    inc(eng.tensor_scalar(out=sxs[:, 512 * (t % 2):512 * (t % 2) + 512],
                                          in0=pW[:, 512 * t:512 * (t + 1)],
                                          scalar1=DQ2, scalar2=None, op0=ALU.mult))

                def AD(t, eng, inc, half=half, cP=cP):
                    tg = 4 * half + t
                    sl = sR1[:, D * tg + 512 * cP:D * tg + 512 * (cP + 1)]
                    inc(eng.tensor_add(out=sl, in0=sl,
                                       in1=sxs[:, 512 * (t % 2):512 * (t % 2) + 512]))

                pipe(ph, 4, [("vector", 0, SC, 1), ("vector", 1, AD, 1)])

        def pst(eng, inc):
            _dma_tiles(eng, inc, sR1, x1, TPC, D, to_sbuf=False)
        ph.phase(gpsimd=(pst, 16 * NT))



def stage_qkv1(nc, ph, io, tag="q1"):
    xn1, identB = io["xn1"], io["identB"]
    wq, wk, wv = io["wq"], io["wk"], io["wv"]
    q1, k1, v1 = io["q1"], io["k1"], io["v1"]

    with contextlib.ExitStack() as ctx:
        T = lambda name, shape, dt: ctx.enter_context(
            nc.sbuf_tensor(f"{tag}_{name}", shape, dt))
        sXnF = T("sXnF", [128, NF * TPC], BF16)
        sxt = T("sxt", [128, 2 * D], F32)
        sxb = T("sxb", [128, 2 * D], BF16)
        sidB = T("sidB", [128, 128], BF16)
        swf = T("swf", [128, 2 * D], F32)
        swb = T("swb", [128, 2 * D], BF16)
        sQa = T("sQa", [128, 2 * TPC], BF16)
        ptr2 = ctx.enter_context(nc.psum_tensor(f"{tag}_ptr2", [128, 4096], BF16))
        pa = ctx.enter_context(nc.psum_tensor(f"{tag}_pa", [128, 2 * TPC], F32))

        def p0(eng, inc):
            inc(eng.dma_start(out=sidB[:], in_=identB), 16)
        ph.phase(sync=(p0, 16))

        pipe_tm2fm(ph, xn1, sXnF, sxt, sxb, ptr2, sidB)

        jobs = [(wq, q1, SM_SCALE1), (wk, k1, None), (wv, v1, None)]

        def L(i, eng, inc):
            wten = jobs[i // NF][0]
            m = i % NF
            inc(eng.dma_start(
                out=swf[:, D * (i % 2):D * (i % 2) + D],
                in_=wten[m].rearrange("(j p) c -> p j c", p=128)), 16)

        def C(i, eng, inc):
            s = i % 2
            inc(eng.activation(out=swb[:, D * s:D * (s + 1)],
                               in_=swf[:, D * s:D * (s + 1)], func=AF.Copy))

        def M(i, eng, inc):
            s = i % 2
            for h in range(2):
                for j in range(NF):
                    mm = eng.matmul(
                        pa[:, TPC * s + 512 * h:TPC * s + 512 * (h + 1)],
                        swb[:, D * s + 128 * j:D * s + 128 * (j + 1)],
                        sXnF[:, TPC * j + 512 * h:TPC * j + 512 * (h + 1)],
                        start=(j == 0), stop=(j == NF - 1))
                inc(mm)

        def E(i, eng, inc):
            s = i % 2
            scale = jobs[i // NF][2]
            if scale is None:
                inc(eng.activation(out=sQa[:, TPC * s:TPC * (s + 1)],
                                   in_=pa[:, TPC * s:TPC * (s + 1)], func=AF.Copy))
            else:
                inc(eng.activation(out=sQa[:, TPC * s:TPC * (s + 1)],
                                   in_=pa[:, TPC * s:TPC * (s + 1)],
                                   func=AF.Identity, scale=scale))

        def S(i, eng, inc):
            s = i % 2
            outd = jobs[i // NF][1]
            m = i % NF
            inc(eng.dma_start(out=outd[128 * m:128 * (m + 1), :],
                              in_=sQa[:, TPC * s:TPC * (s + 1)]), 16)

        pipe(ph, 3 * NF, [("sync", 0, L, 16), ("scalar", 1, C, 1),
                          ("tensor", 2, M, 2), ("scalar", 3, E, 1),
                          ("gpsimd", 4, S, 16)])



def stage_v1t(nc, ph, io, tag="vt"):
    v1g, identB, v1t = io["v1g"], io["identB"], io["v1t"]
    with contextlib.ExitStack() as ctx:
        T = lambda name, shape, dt: ctx.enter_context(
            nc.sbuf_tensor(f"{tag}_{name}", shape, dt))
        sin_ = T("sin", [128, 2 * TPC], BF16)
        stile = T("stile", [128, 2 * TPC], BF16)
        sidB = T("sidB", [128, 128], BF16)
        pT = ctx.enter_context(nc.psum_tensor(f"{tag}_pT", [128, 2048], BF16))

        def p0(eng, inc):
            inc(eng.dma_start(out=sidB[:], in_=identB), 16)
        ph.phase(sync=(p0, 16))

        def L(i, eng, inc):
            r, et = i // 16, i % 16
            inc(eng.dma_start(out=sin_[:, TPC * (i % 2):TPC * (i % 2) + TPC],
                              in_=v1g[2048 * r + 128 * et:2048 * r + 128 * (et + 1), :]), 16)

        def Tr(i, eng, inc):
            for b in range(8):
                inc(eng.transpose(pT[:, 1024 * (i % 2) + 128 * b:
                                     1024 * (i % 2) + 128 * (b + 1)],
                                  sin_[:, TPC * (i % 2) + 128 * b:
                                       TPC * (i % 2) + 128 * (b + 1)], sidB[:]))

        def Ev(i, eng, inc):
            for b in range(8):
                inc(eng.tensor_copy(out=stile[:, TPC * (i % 2) + 128 * b:
                                             TPC * (i % 2) + 128 * (b + 1)],
                                    in_=pT[:, 1024 * (i % 2) + 128 * b:
                                          1024 * (i % 2) + 128 * (b + 1)]))

        def St(i, eng, inc):
            r, et = i // 16, i % 16
            for b in range(8):
                kt_row = 1024 * r + 128 * b
                inc(eng.dma_start(out=v1t[kt_row:kt_row + 128,
                                          128 * et:128 * (et + 1)],
                                  in_=stile[:, TPC * (i % 2) + 128 * b:
                                           TPC * (i % 2) + 128 * (b + 1)]), 16)

        pipe(ph, 32, [("sync", 0, L, 16), ("tensor", 1, Tr, 8),
                      ("vector", 2, Ev, 8), ("gpsimd", 3, St, 16 * 8)])



def stage_attn1(nc, ph, io, tag="a1"):
    q1, k1g, v1t = io["q1"], io["k1g"], io["v1t"]
    woT, bo_row, x1 = io["woT"], io["bo_row"], io["x1"]
    identB, identF = io["identB"], io["identF"]
    r2 = io["r2"]

    with contextlib.ExitStack() as ctx:
        T = lambda name, shape, dt: ctx.enter_context(
            nc.sbuf_tensor(f"{tag}_{name}", shape, dt))
        sQ1 = T("sQ1", [128, NF * TPC], BF16)
        sK1 = T("sK1", [128, NF * SEQ], BF16)
        sPT = T("sPT", [128, NF * TPC], BF16)
        sidB = T("sidB", [128, 128], BF16)
        sidF = T("sidF", [128, 128], F32)
        sneg = T("sneg", [128, 8], F32)
        snegrb = T("snegrb", [8, 128], BF16)
        snegrow = T("snegrow", [1, TPC], BF16)
        sonerow = T("sonerow", [1, 128], BF16)
        sonecol = T("sonecol", [128, 1], BF16)
        srd = T("srd", [128, 8], F32)
        svs = T("svs", [128, 2 * 512], BF16)
        sbo = T("sbo", [1, D], F32)
        sbob = T("sbob", [1, D], BF16)
        swf = T("swf", [128, 2 * 512], F32)
        swb = T("swb", [128, 2 * 512], BF16)
        sxs = T("sxs", [128, 2 * 512], F32)
        sout = T("sout", [128, 2 * 512], F32)
        pS = ctx.enter_context(nc.psum_tensor(f"{tag}_pS", [128, 2048], F32))
        pCT = ctx.enter_context(nc.psum_tensor(f"{tag}_pCT", [128, 4096], BF16))

        sAT = sK1
        sAF = sQ1

        def p0(eng, inc):
            inc(eng.dma_start(out=sidB[:], in_=identB), 16)
            inc(eng.dma_start(out=sidF[:], in_=identF), 16)
            inc(eng.dma_start(out=sbo[:], in_=bo_row), 16)
            for j in range(NF):
                inc(eng.dma_start(out=sK1[:, SEQ * j:SEQ * j + TPC],
                                  in_=k1g[128 * j:128 * (j + 1), :]), 16)
                inc(eng.dma_start(out=sK1[:, SEQ * j + TPC:SEQ * (j + 1)],
                                  in_=k1g[2048 + 128 * j:2048 + 128 * (j + 1), :]), 16)

        def p0b(eng, inc):
            _dma_tiles(eng, inc, sQ1, q1, D, TPC)
        ph.phase(sync=(p0, 16 * 35), gpsimd=(p0b, 16 * 16))

        def p1(eng, inc):
            eng.memset(sonerow[:], 1.0)
            inc(eng.memset(sonecol[:], 1.0))
            inc(eng.tensor_copy(out=sbob[:], in_=sbo[:]))
        ph.phase(vector=(p1, 2))

        for u in range(8):
            def d1(eng, inc, u=u):
                for kc in range(4):
                    for j in range(NF):
                        mm = eng.matmul(pS[:, 512 * kc:512 * (kc + 1)],
                                        sQ1[:, TPC * j + 128 * u:TPC * j + 128 * (u + 1)],
                                        sK1[:, SEQ * j + 512 * kc:SEQ * j + 512 * (kc + 1)],
                                        start=(j == 0), stop=(j == NF - 1))
                inc(mm)
            ph.phase(tensor=(d1, 1))

            def d2(eng, inc, u=u):
                inc(eng.tensor_reduce(out=sneg[:, u:u + 1], in_=pS[:], axis=AX.X,
                                      op=ALU.max, negate=True))
            ph.phase(vector=(d2, 1))

        def e1(eng, inc):
            inc(eng.transpose(pS[0:8, 0:128], sneg[:], sidF[:]))
        ph.phase(tensor=(e1, 1))

        def e2(eng, inc):
            inc(eng.tensor_copy(out=snegrb[:], in_=pS[0:8, 0:128]))
        ph.phase(vector=(e2, 1))

        def e3(eng, inc):
            for r in range(8):
                inc(eng.dma_start(out=snegrow[0:1, 128 * r:128 * (r + 1)],
                                  in_=snegrb[r:r + 1, :]), 16)
        ph.phase(gpsimd=(e3, 128))

        def sf1(i, eng, inc):
            kb, qh2 = i // 2, i % 2
            p = pS[:, 512 * (i % 2):512 * (i % 2) + 512]
            for j in range(NF):
                eng.matmul(p, sK1[:, SEQ * j + 128 * kb:SEQ * j + 128 * (kb + 1)],
                           sQ1[:, TPC * j + 512 * qh2:TPC * j + 512 * (qh2 + 1)],
                           start=(j == 0), stop=False)
            inc(eng.matmul(p, sonerow[:],
                           snegrow[0:1, 512 * qh2:512 * (qh2 + 1)],
                           start=False, stop=True))

        def sf2(i, eng, inc):
            kb, qh2 = i // 2, i % 2
            inc(eng.activation(
                out=sPT[:, TPC * kb + 512 * qh2:TPC * kb + 512 * (qh2 + 1)],
                in_=pS[:, 512 * (i % 2):512 * (i % 2) + 512], func=AF.Exp))

        pipe(ph, 2 * NF, [("tensor", 0, sf1, 1), ("scalar", 1, sf2, 1)])

        def g1(eng, inc):
            for u in range(8):
                for kb in range(NF):
                    mm = eng.matmul(pS[:, 1536 + u:1537 + u],
                                    sPT[:, TPC * kb + 128 * u:TPC * kb + 128 * (u + 1)],
                                    sonecol[:], start=(kb == 0), stop=(kb == NF - 1))
                inc(mm)
        ph.phase(tensor=(g1, 8))

        def g2(eng, inc):
            for u in range(8):
                inc(eng.reciprocal(out=srd[:, u:u + 1], in_=pS[:, 1536 + u:1537 + u]))
        ph.phase(vector=(g2, 8))

        for up in range(2):
            for cE in range(4):
                def h1(kb, eng, inc, cE=cE):
                    inc(eng.dma_start(out=svs[:, 512 * (kb % 2):512 * (kb % 2) + 512],
                                      in_=v1t[128 * kb:128 * (kb + 1),
                                              512 * cE:512 * (cE + 1)]), 16)

                def h2(kb, eng, inc, up=up):
                    for u4 in range(4):
                        u = 4 * up + u4
                        mm = eng.matmul(pS[:, 512 * u4:512 * (u4 + 1)],
                                        sPT[:, TPC * kb + 128 * u:
                                             TPC * kb + 128 * (u + 1)],
                                        svs[:, 512 * (kb % 2):512 * (kb % 2) + 512],
                                        start=(kb == 0), stop=(kb == NF - 1))
                    inc(mm)

                pipe(ph, NF, [("sync", 0, h1, 16), ("tensor", 1, h2, 1)])

                def h3(eng, inc, up=up, cE=cE):
                    for u4 in range(4):
                        u = 4 * up + u4
                        inc(eng.activation(
                            out=sAT[:, SEQ * u + 512 * cE:SEQ * u + 512 * (cE + 1)],
                            in_=pS[:, 512 * u4:512 * (u4 + 1)],
                            func=AF.Identity, scale=srd[:, u:u + 1]))
                ph.phase(scalar=(h3, 4))

        def t1(u, eng, inc):
            h = 2048 * (u % 2)
            for j in range(NF):
                inc(eng.transpose(pCT[:, h + 128 * j:h + 128 * (j + 1)],
                                  sAT[:, SEQ * u + 128 * j:SEQ * u + 128 * (j + 1)],
                                  sidB[:]))

        def t2(u, eng, inc):
            h = 2048 * (u % 2)
            for j in range(NF):
                inc(eng.tensor_copy(
                    out=sAF[:, TPC * j + 128 * u:TPC * j + 128 * (u + 1)],
                    in_=pCT[:, h + 128 * j:h + 128 * (j + 1)]))

        pipe(ph, 8, [("tensor", 0, t1, NF), ("vector", 1, t2, NF)])

        for tp in range(2):
            for cP in range(4):
                def w1(j, eng, inc, cP=cP):
                    inc(eng.dma_start(out=swf[:, 512 * (j % 2):512 * (j % 2) + 512],
                                      in_=woT[128 * j:128 * (j + 1),
                                              512 * cP:512 * (cP + 1)]), 16)

                def w2(j, eng, inc):
                    s = j % 2
                    inc(eng.activation(out=swb[:, 512 * s:512 * (s + 1)],
                                       in_=swf[:, 512 * s:512 * (s + 1)],
                                       func=AF.Copy))

                def w3(j, eng, inc, tp=tp, cP=cP):
                    s = j % 2
                    for t4 in range(4):
                        t = 4 * tp + t4
                        mm = eng.matmul(pS[:, 512 * t4:512 * (t4 + 1)],
                                        sAF[:, TPC * j + 128 * t:TPC * j + 128 * (t + 1)],
                                        swb[:, 512 * s:512 * (s + 1)],
                                        start=(j == 0), stop=False)
                    if j == NF - 1:
                        for t4 in range(4):
                            mm = eng.matmul(pS[:, 512 * t4:512 * (t4 + 1)],
                                            sonerow[:],
                                            sbob[0:1, 512 * cP:512 * (cP + 1)],
                                            start=False, stop=True)
                    inc(mm)

                pipe(ph, NF, [("sync", 0, w1, 16), ("scalar", 1, w2, 1),
                              ("tensor", 2, w3, 1)])

                def w4(t4, eng, inc, tp=tp, cP=cP):
                    t = 4 * tp + t4
                    inc(eng.dma_start(out=sxs[:, 512 * (t4 % 2):512 * (t4 % 2) + 512],
                                      in_=x1[128 * t:128 * (t + 1),
                                             512 * cP:512 * (cP + 1)]), 16)

                def w5(t4, eng, inc):
                    inc(eng.tensor_add(out=sout[:, 512 * (t4 % 2):512 * (t4 % 2) + 512],
                                       in0=sxs[:, 512 * (t4 % 2):512 * (t4 % 2) + 512],
                                       in1=pS[:, 512 * t4:512 * (t4 + 1)]))

                def w6(t4, eng, inc, tp=tp, cP=cP):
                    t = 4 * tp + t4
                    inc(eng.dma_start(out=r2[128 * t:128 * (t + 1),
                                             512 * cP:512 * (cP + 1)],
                                      in_=sout[:, 512 * (t4 % 2):512 * (t4 % 2) + 512]), 16)

                pipe(ph, 4, [("sync", 0, w4, 16), ("vector", 1, w5, 1),
                             ("gpsimd", 2, w6, 16)])



def stage_lut(nc, ph, io, tag="lu"):
    lut_row = io["lut_row"]
    gidx, uidx, didx = io["gidx"], io["uidx"], io["didx"]
    wgt_sh, wut_sh, wd_sh = io["wgt_sh"], io["wut_sh"], io["wd_sh"]
    W = INTER1

    with contextlib.ExitStack() as ctx:
        T = lambda name, shape, dt: ctx.enter_context(
            nc.sbuf_tensor(f"{tag}_{name}", shape, dt))
        sI = T("sI", [128, W], I32)
        sB = T("sB", [128, W], BF16)
        sT = T("sT", [128, W], BF16)
        sAa = T("sAa", [128, W], BF16)
        sAb = T("sAb", [128, W], BF16)
        sI2 = T("sI2", [128, W], I32)
        sB2 = T("sB2", [128, W], BF16)
        sT2 = T("sT2", [128, W], BF16)
        sAc = T("sAc", [128, W], BF16)
        sAd = T("sAd", [128, W], BF16)
        sone = T("sone", [1, 128], F32)
        slr = T("slr", [1, 16], F32)
        slc = T("slc", [128, 16], F32)
        pRep = ctx.enter_context(nc.psum_tensor(f"{tag}_pRep", [128, 16], F32))

        def p0(eng, inc):
            inc(eng.dma_start(out=slr[:], in_=lut_row), 16)
        ph.phase(sync=(p0, 16))

        def p1(eng, inc):
            inc(eng.memset(sone[:], 1.0))
        ph.phase(vector=(p1, 1))

        def p2(eng, inc):
            inc(eng.matmul(pRep[:], sone[:], slr[:], start=True, stop=True))
        ph.phase(tensor=(p2, 1))

        def p3(eng, inc):
            inc(eng.tensor_copy(out=slc[:], in_=pRep[:]))
        ph.phase(vector=(p3, 1))

        jobs = []
        for t in range(2):
            jobs.append((gidx, "gt", 128 * t, W, t))
            jobs.append((uidx, "ut", 128 * t, W, t))
        for t in range(6):
            jobs.append((didx, "d", 128 * t, D, t))
        assert len(jobs) % 2 == 0

        for jp in range(0, len(jobs), 2):
            pair = [jobs[jp], jobs[jp + 1]]
            bufs = [(sI, sB, sT, sAa, sAb), (sI2, sB2, sT2, sAc, sAd)]

            def jl(eng, inc, pair=pair, bufs=bufs):
                for (srcd, kind, r0, Wj, trow), (bI, bB, bT, bA, bBk) in zip(pair, bufs):
                    inc(eng.dma_start(out=bI[:, 0:Wj], in_=srcd[r0:r0 + 128, :]), 16)
            ph.phase(sync=(jl, 32))

            def jc(eng, inc, pair=pair, bufs=bufs):
                for (srcd, kind, r0, Wj, trow), (bI, bB, bT, bA, bBk) in zip(pair, bufs):
                    inc(eng.tensor_copy(out=bB[:, 0:Wj], in_=bI[:, 0:Wj]))
            ph.phase(vector=(jc, 2))

            def j0(eng, inc, pair=pair, bufs=bufs):
                for (srcd, kind, r0, Wj, trow), (bI, bB, bT, bA, bBk) in zip(pair, bufs):
                    inc(eng.tensor_scalar(out=bA[:, 0:Wj], in0=bB[:, 0:Wj],
                                          scalar1=0.0, scalar2=slc[:, 0:1],
                                          op0=ALU.is_equal, op1=ALU.mult))
            ph.phase(vector=(j0, 2))
            curs = [(bufs[0][3], bufs[0][4]), (bufs[1][3], bufs[1][4])]
            for l in range(1, 16):
                def jt(eng, inc, pair=pair, bufs=bufs, l=l, curs=tuple(curs)):
                    for (srcd, kind, r0, Wj, trow), (bI, bB, bT, bA, bBk), (cur, oth)                             in zip(pair, bufs, curs):
                        inc(eng.tensor_scalar(out=bT[:, 0:Wj], in0=bB[:, 0:Wj],
                                              scalar1=float(l), scalar2=slc[:, l:l + 1],
                                              op0=ALU.is_equal, op1=ALU.mult))
                ph.phase(vector=(jt, 2))

                def ja(eng, inc, pair=pair, bufs=bufs, curs=tuple(curs)):
                    for (srcd, kind, r0, Wj, trow), (bI, bB, bT, bA, bBk), (cur, oth)                             in zip(pair, bufs, curs):
                        inc(eng.tensor_add(out=oth[:, 0:Wj], in0=cur[:, 0:Wj],
                                           in1=bT[:, 0:Wj]))
                ph.phase(vector=(ja, 2))
                curs = [(o, c) for (c, o) in curs]

            def js(eng, inc, pair=pair, curs=tuple(curs)):
                for (srcd, kind, r0, Wj, trow), (cur, oth) in zip(pair, curs):
                    if kind == "d":
                        inc(eng.dma_start(out=wd_sh[r0:r0 + 128, :], in_=cur[:, 0:D]), 16)
                    else:
                        dstd = wgt_sh if kind == "gt" else wut_sh
                        for m in range(NI1):
                            r0t = 256 * m + 128 * trow
                            inc(eng.dma_start(out=dstd[r0t:r0t + 128, :],
                                              in_=cur[:, 128 * m:128 * (m + 1)]), 16)
            ns = sum((16 if k == "d" else 16 * NI1) for (_, k, _, _, _) in pair)
            ph.phase(gpsimd=(js, ns))



def stage_ffn1(nc, ph, io, tag="f1"):
    xn1, identB = io["xn1"], io["identB"]
    wgg, wug, wdg = io["wgg"], io["wug"], io["wdg"]
    r2, outd = io["r2"], io["outd"]

    with contextlib.ExitStack() as ctx:
        T = lambda name, shape, dt: ctx.enter_context(
            nc.sbuf_tensor(f"{tag}_{name}", shape, dt))
        sXnF = T("sXnF", [128, NF * TPC], BF16)
        sxt = T("sxt", [128, 2 * D], F32)
        sxb = T("sxb", [128, 2 * D], BF16)
        sidB = T("sidB", [128, 128], BF16)
        sH1 = T("sH1", [128, NI1 * TPC], BF16)
        swg = T("swg", [128, 2 * D], BF16)
        swu = T("swu", [128, 2 * D], BF16)
        ssil = T("ssil", [128, 2 * 512], F32)
        sup = T("sup", [128, 2 * 512], F32)
        swd = T("swd", [128, 8 * 512], BF16)
        sxs = T("sxs", [128, 2 * 512], F32)
        sout = T("sout", [128, 2 * 512], F32)
        ptr2 = ctx.enter_context(nc.psum_tensor(f"{tag}_ptr2", [128, 4096], BF16))
        pBig = ctx.enter_context(nc.psum_tensor(f"{tag}_pBig", [128, 2048], F32))

        def p0(eng, inc):
            inc(eng.dma_start(out=sidB[:], in_=identB), 16)
        ph.phase(sync=(p0, 16))

        pipe_tm2fm(ph, xn1, sXnF, sxt, sxb, ptr2, sidB)

        if "w_await" in io:
            io["w_await"]()

        def L(i, eng, inc):
            m = i // 2
            if i % 2 == 0:
                for a in range(NF // 2):
                    r0 = 11008 * a + 256 * m
                    inc(eng.dma_start(
                        out=swg[:, D * (m % 2) + 256 * a:D * (m % 2) + 256 * (a + 1)],
                        in_=wgg[r0:r0 + 256, :].rearrange("(j p) c -> p j c", p=128)), 16)

        def LU(i, eng, inc):
            m = i // 2
            if i % 2 == 0:
                for a in range(NF // 2):
                    r0 = 11008 * a + 256 * m
                    inc(eng.dma_start(
                        out=swu[:, D * (m % 2) + 256 * a:D * (m % 2) + 256 * (a + 1)],
                        in_=wug[r0:r0 + 256, :].rearrange("(j p) c -> p j c", p=128)), 16)

        def M(i, eng, inc):
            m, h = i // 2, i % 2
            sl = 1024 * (i % 2 == 1)
            sl = 1024 * (i % 2)
            for j in range(NF):
                mm = eng.matmul(pBig[:, sl:sl + 512],
                                swg[:, D * (m % 2) + 128 * j:D * (m % 2) + 128 * (j + 1)],
                                sXnF[:, TPC * j + 512 * h:TPC * j + 512 * (h + 1)],
                                start=(j == 0), stop=(j == NF - 1))
            inc(mm)
            for j in range(NF):
                mm = eng.matmul(pBig[:, sl + 512:sl + 1024],
                                swu[:, D * (m % 2) + 128 * j:D * (m % 2) + 128 * (j + 1)],
                                sXnF[:, TPC * j + 512 * h:TPC * j + 512 * (h + 1)],
                                start=(j == 0), stop=(j == NF - 1))
            inc(mm)

        def SI(i, eng, inc):
            sl = 1024 * (i % 2)
            so = 512 * (i % 2)
            eng.activation(out=ssil[:, so:so + 512], in_=pBig[:, sl:sl + 512],
                           func=AF.Silu)
            inc(eng.activation(out=sup[:, so:so + 512],
                               in_=pBig[:, sl + 512:sl + 1024], func=AF.Copy), 2)

        def MU(i, eng, inc):
            m, h = i // 2, i % 2
            so = 512 * (i % 2)
            inc(eng.tensor_mul(out=sH1[:, TPC * m + 512 * h:TPC * m + 512 * (h + 1)],
                               in0=ssil[:, so:so + 512], in1=sup[:, so:so + 512]))

        pipe(ph, 2 * NI1, [("sync", 0, L, lambda i: 16 * 8 if i % 2 == 0 else 0),
                           ("gpsimd", 0, LU, lambda i: 16 * 8 if i % 2 == 0 else 0),
                           ("tensor", 1, M, 2), ("scalar", 2, SI, 2),
                           ("vector", 3, MU, 1)])

        NG = 11
        for tp in range(2):
            for cP in range(4):
                def L2(g, eng, inc, cP=cP):
                    lo = 4 * g
                    for q in range(min(4, NI1 - lo)):
                        mj = lo + q
                        inc(eng.dma_start(
                            out=swd[:, 2048 * (g % 2) + 512 * q:
                                     2048 * (g % 2) + 512 * (q + 1)],
                            in_=wdg[128 * mj:128 * (mj + 1),
                                    512 * cP:512 * (cP + 1)]), 16)

                def M2(g, eng, inc, tp=tp):
                    lo = 4 * g
                    for q in range(min(4, NI1 - lo)):
                        mj = lo + q
                        for t4 in range(4):
                            t = 4 * tp + t4
                            mm = eng.matmul(pBig[:, 512 * t4:512 * (t4 + 1)],
                                            sH1[:, TPC * mj + 128 * t:
                                                 TPC * mj + 128 * (t + 1)],
                                            swd[:, 2048 * (g % 2) + 512 * q:
                                                 2048 * (g % 2) + 512 * (q + 1)],
                                            start=(mj == 0), stop=(mj == NI1 - 1))
                    inc(mm)

                pipe(ph, NG, [("sync", 0, L2, lambda g: 16 * min(4, NI1 - 4 * g)),
                              ("tensor", 1, M2, 1)])

                def EV(t4, eng, inc, tp=tp, cP=cP):
                    t = 4 * tp + t4
                    inc(eng.dma_start(out=sxs[:, 512 * (t4 % 2):512 * (t4 % 2) + 512],
                                      in_=r2[128 * t:128 * (t + 1),
                                             512 * cP:512 * (cP + 1)]), 16)

                def AD(t4, eng, inc):
                    inc(eng.tensor_add(out=sout[:, 512 * (t4 % 2):512 * (t4 % 2) + 512],
                                       in0=sxs[:, 512 * (t4 % 2):512 * (t4 % 2) + 512],
                                       in1=pBig[:, 512 * t4:512 * (t4 + 1)]))

                def ST(t4, eng, inc, tp=tp, cP=cP):
                    t = 4 * tp + t4
                    inc(eng.dma_start(out=outd[128 * t:128 * (t + 1),
                                               512 * cP:512 * (cP + 1)],
                                      in_=sout[:, 512 * (t4 % 2):512 * (t4 % 2) + 512]), 16)

                pipe(ph, 4, [("sync", 0, EV, 16), ("vector", 1, AD, 1),
                             ("gpsimd", 2, ST, 16)])



def prog_fused():
    nc = bass.Bass("TRN2", target_bir_lowering=False)
    EI, EO, IN = "ExternalInput", "ExternalOutput", "Internal"

    def dr(name, shape, dt, kind):
        return nc.dram_tensor(name, shape, dt, kind=kind).ap()

    x = dr("x", [TPC, D], F32, EI)
    wq0 = dr("wq0", [NF, D, 128], F32, EI)
    wk0 = dr("wk0", [NF, D, 128], F32, EI)
    wv0 = dr("wv0", [NF, D, 128], F32, EI)
    cosT = dr("cosT", [TPC, TPC], F32, EI)
    sinT = dr("sinT", [TPC, TPC], F32, EI)
    wfc_sh = dr("wfc_sh", [256, FFI], F32, EI)
    wproj_sh = dr("wproj_sh", [1024, D], F32, EI)
    woT0 = dr("woT0", [D, D], F32, EI)
    wq1 = dr("wq1", [NF, D, 128], F32, EI)
    wk1 = dr("wk1", [NF, D, 128], F32, EI)
    wv1 = dr("wv1", [NF, D, 128], F32, EI)
    woT1 = dr("woT1", [D, D], F32, EI)
    bo_row = dr("bo_row", [1, D], F32, EI)
    gate_row = dr("gate_row", [1, 16], F32, EI)
    sel_row = dr("sel_row", [1, 2], F32, EI)
    lut_row = dr("lut_row", [1, 16], F32, EI)
    gidx = dr("gidx", [256, INTER1], I32, EI)
    uidx = dr("uidx", [256, INTER1], I32, EI)
    didx = dr("didx", [768, D], I32, EI)
    identB = dr("identB", [128, 128], BF16, EI)
    identF = dr("identF", [128, 128], F32, EI)
    outd = dr("outd", [TPC, D], F32, EO)

    xn = dr("xn", [TPC, D], F32, IN)
    maxes = dr("maxes", [1, 3], F32, IN)
    gmax = dr("gmax", [1, 3], F32, IN)
    qf = dr("qf", [D, TPC], BF16, IN)
    kf = dr("kf", [D, TPC], BF16, IN)
    vf = dr("vf", [D, TPC], BF16, IN)
    qg = dr("qg", [2 * D, TPC], BF16, IN)
    kg = dr("kg", [2 * D, TPC], BF16, IN)
    vg = dr("vg", [2 * D, TPC], BF16, IN)
    x8f = dr("x8f", [D, TPC], BF16, IN)
    wfc8t_sh = dr("wfc8t_sh", [NFI * 256, 128], BF16, IN)
    wfc8g = dr("wfc8g", [8 * NFI * 256, 128], BF16, IN)
    wproj8_sh = dr("wproj8_sh", [1024, D], BF16, IN)
    wproj8g = dr("wproj8g", [FFI, D], BF16, IN)
    combh0 = dr("combh0", [TPC, TPC], BF16, IN)
    combh1 = dr("combh1", [TPC, TPC], BF16, IN)
    cgA = dr("cgA", [D, TPC], BF16, IN)
    cgB = dr("cgB", [D, TPC], BF16, IN)
    hbuf = dr("hbuf", [FFI, TPC], F32, IN)
    hmax = dr("hmax", [1, 1], F32, IN)
    ghmax = dr("ghmax", [1, 1], F32, IN)
    x1 = dr("x1", [TPC, D], F32, IN)
    xn1 = dr("xn1", [TPC, D], F32, IN)
    q1 = dr("q1", [D, TPC], BF16, IN)
    k1 = dr("k1", [D, TPC], BF16, IN)
    v1 = dr("v1", [D, TPC], BF16, IN)
    k1g = dr("k1g", [2 * D, TPC], BF16, IN)
    v1g = dr("v1g", [2 * D, TPC], BF16, IN)
    v1t = dr("v1t", [SEQ, D], BF16, IN)
    r2 = dr("r2", [TPC, D], F32, IN)
    wgt_sh = dr("wgt_sh", [NI1 * 256, 128], BF16, IN)
    wut_sh = dr("wut_sh", [NI1 * 256, 128], BF16, IN)
    wgg = dr("wgg", [8 * NI1 * 256, 128], BF16, IN)
    wug = dr("wug", [8 * NI1 * 256, 128], BF16, IN)
    wd_sh = dr("wd_sh", [768, D], BF16, IN)
    wdg = dr("wdg", [6144, D], BF16, IN)

    with contextlib.ExitStack() as ctx:
        ph = Phaser(nc, ctx)

        csem = ctx.enter_context(nc.semaphore("csem"))
        csw = ctx.enter_context(nc.semaphore("csw"))
        csq = ctx.enter_context(nc.semaphore("csq"))
        csh = ctx.enter_context(nc.semaphore("csh"))
        csc = ctx.enter_context(nc.semaphore("csc"))
        csk = ctx.enter_context(nc.semaphore("csk"))
        csl = ctx.enter_context(nc.semaphore("csl"))
        stage_ln(nc, ph, {"x": x, "xn": xn, "identF": identF,
                          "wfc_sh": wfc_sh, "wproj_sh": wproj_sh,
                          "maxes": maxes}, with_maxes=True, tag="l0")
        coll_phase(ph, "AllReduce", ALU.max, GROUP8, [(maxes, gmax)])
        stage_prep0(nc, ph, {
            "xn": xn, "gmax": gmax, "wq": wq0, "wk": wk0, "wv": wv0,
            "cosT": cosT, "sinT": sinT, "wfc_sh": wfc_sh, "wproj_sh": wproj_sh,
            "identB": identB, "qf": qf, "kf": kf, "vf": vf, "x8f": x8f,
            "wfc8t_sh": wfc8t_sh, "wproj8_sh": wproj8_sh,
            "wcoll": lambda: coll_phase(ph, "AllGather", ALU.bypass, GROUP8,
                                        [(wfc8t_sh, wfc8g), (wproj8_sh, wproj8g)],
                                        sem=csw)})
        coll_phase(ph, "AllGather", ALU.bypass, PAIRS,
                   [(qf, qg), (kf, kg), (vf, vg)], sem=csq)
        coll_await(ph, csw, 2)
        stage_fc0(nc, ph, {"x8f": x8f, "wfc8g": wfc8g, "gmax": gmax,
                           "identF": identF, "hbuf": hbuf, "hmax": hmax})
        coll_phase(ph, "AllReduce", ALU.max, GROUP8, [(hmax, ghmax)], sem=csh)
        coll_await(ph, csq, 3)
        stage_attn0(nc, ph, {
            "qg": qg, "kg": kg, "vg": vg, "gate_row": gate_row,
            "sel_row": sel_row, "identB": identB, "identF": identF,
            "combh0": combh0, "combh1": combh1})
        coll_phase(ph, "AllGather", ALU.bypass, PAIRS,
                   [(combh0, cgA), (combh1, cgB)], sem=csc)
        coll_await(ph, csh, 1)
        stage_proj0(nc, ph, {
            "x": x, "combA": cgA, "combB": cgB, "woT": woT0, "hbuf": hbuf,
            "wproj8g": wproj8g, "gmax": gmax, "ghmax": ghmax,
            "sel_row": sel_row, "x1": x1,
            "comb_await": lambda: coll_await(ph, csc, 2)})
        stage_ln(nc, ph, {"x": x1, "xn": xn1, "identF": identF},
                 with_maxes=False, tag="l1")
        stage_qkv1(nc, ph, {"xn1": xn1, "identB": identB, "wq": wq1,
                            "wk": wk1, "wv": wv1, "q1": q1, "k1": k1, "v1": v1})
        coll_phase(ph, "AllGather", ALU.bypass, PAIRS, [(k1, k1g), (v1, v1g)],
                   sem=csk)
        stage_lut(nc, ph, {"lut_row": lut_row, "gidx": gidx, "uidx": uidx,
                           "didx": didx, "wgt_sh": wgt_sh, "wut_sh": wut_sh,
                           "wd_sh": wd_sh})
        coll_await(ph, csk, 2)
        stage_v1t(nc, ph, {"v1g": v1g, "identB": identB, "v1t": v1t})
        stage_attn1(nc, ph, {
            "q1": q1, "k1g": k1g, "v1t": v1t, "woT": woT1, "bo_row": bo_row,
            "x1": x1, "identB": identB, "identF": identF, "r2": r2})
        coll_phase(ph, "AllGather", ALU.bypass, GROUP8,
                   [(wgt_sh, wgg), (wut_sh, wug), (wd_sh, wdg)], sem=csl)
        stage_ffn1(nc, ph, {"xn1": xn1, "identB": identB, "wgg": wgg,
                            "wug": wug, "wdg": wdg, "r2": r2, "outd": outd,
                            "w_await": lambda: coll_await(ph, csl, 3)})
        ph.finish()
    return nc



def host_constants():
    c = {}
    c["identB"] = np.eye(128, dtype=np.float32).astype(ml_dtypes.bfloat16)
    c["identF"] = np.eye(128, dtype=np.float32)
    H = np.int32(D)
    inv_freq = (np.float32(1.0) /
                np.power(np.float32(10000.0),
                         np.arange(0, D, 2, dtype=np.float32) / np.float32(H)))
    t = np.arange(SEQ, dtype=np.float32)
    freqs = t[:, None] * inv_freq[None, :]
    cos = np.cos(freqs).astype(np.float32)
    sin = np.sin(freqs).astype(np.float32)
    c["cosT"] = [np.ascontiguousarray(cos[h * TPC:(h + 1) * TPC, :].T) for h in range(2)]
    c["sinT"] = [np.ascontiguousarray(sin[h * TPC:(h + 1) * TPC, :].T) for h in range(2)]
    return c


def _tile_wT(w):
    o, i = w.shape
    return np.ascontiguousarray(w.T.reshape(i, o // 128, 128).transpose(1, 0, 2))


def host_shards(inputs):
    d = {}
    x = np.ascontiguousarray(inputs["L0_attn_hidden_states"]).reshape(8192, D)
    d["x_slab"] = [np.ascontiguousarray(x[TPC * c:TPC * (c + 1)]) for c in range(NCORES)]
    wfcT = np.ascontiguousarray(inputs["L0_ffn_w_fc"].T)
    wprojT = np.ascontiguousarray(inputs["L0_ffn_w_proj"].T)
    d["wfcT_sh"] = [np.ascontiguousarray(wfcT[256 * c:256 * (c + 1)]) for c in range(NCORES)]
    d["wprojT_sh"] = [np.ascontiguousarray(wprojT[1024 * c:1024 * (c + 1)]) for c in range(NCORES)]
    for nm, key in (("wq0", "L0_attn_Wq"), ("wk0", "L0_attn_Wk"), ("wv0", "L0_attn_Wv"),
                    ("wq1", "L1_attn_Wq"), ("wk1", "L1_attn_Wk"), ("wv1", "L1_attn_Wv")):
        d[nm] = _tile_wT(np.ascontiguousarray(inputs[key]))
    d["woT0"] = np.ascontiguousarray(inputs["L0_attn_Wo"].T)
    d["woT1"] = np.ascontiguousarray(inputs["L1_attn_Wo"].T)
    d["bo_row"] = np.ascontiguousarray(inputs["L1_attn_bo"]).reshape(1, D)
    d["gate_row"] = np.ascontiguousarray(inputs["L0_attn_gate"]).reshape(1, 16)
    d["lut_row"] = np.ascontiguousarray(inputs["L1_ffn_lut"]).reshape(1, 16)
    gT = np.ascontiguousarray(inputs["L1_ffn_gate_idx"].T)
    uT = np.ascontiguousarray(inputs["L1_ffn_up_idx"].T)
    dT = np.ascontiguousarray(inputs["L1_ffn_down_idx"].T)
    d["gidxT_sh"] = [np.ascontiguousarray(gT[256 * c:256 * (c + 1)]) for c in range(NCORES)]
    d["uidxT_sh"] = [np.ascontiguousarray(uT[256 * c:256 * (c + 1)]) for c in range(NCORES)]
    dTp = np.concatenate([dT, np.zeros((6144 - INTER1, D), np.int32)], axis=0)
    d["didxT_sh"] = [np.ascontiguousarray(dTp[768 * c:768 * (c + 1)]) for c in range(NCORES)]
    return d


_PROG_CACHE = {}


def _cached(name, fn, *args):
    key = (name,) + args
    if key not in _PROG_CACHE:
        _PROG_CACHE[key] = fn(*args)
    return _PROG_CACHE[key]


def kernel(**inputs):
    consts = host_constants()
    shards = host_shards(inputs)
    nc = _cached("fused", prog_fused)
    in_maps = []
    for c in range(NCORES):
        p = c % 2
        gr = shards["gate_row"]
        if p == 1:
            gr = np.ascontiguousarray(np.concatenate([gr[:, 8:], gr[:, :8]], axis=1))
        in_maps.append({
            "x": shards["x_slab"][c],
            "wq0": shards["wq0"], "wk0": shards["wk0"], "wv0": shards["wv0"],
            "cosT": consts["cosT"][p], "sinT": consts["sinT"][p],
            "wfc_sh": shards["wfcT_sh"][c], "wproj_sh": shards["wprojT_sh"][c],
            "woT0": shards["woT0"],
            "wq1": shards["wq1"], "wk1": shards["wk1"], "wv1": shards["wv1"],
            "woT1": shards["woT1"], "bo_row": shards["bo_row"],
            "gate_row": gr,
            "sel_row": np.array([[1.0 - p, float(p)]], np.float32),
            "lut_row": shards["lut_row"],
            "gidx": shards["gidxT_sh"][c], "uidx": shards["uidxT_sh"][c],
            "didx": shards["didxT_sh"][c],
            "identB": consts["identB"], "identF": consts["identF"],
        })
    res = run_bass_kernel_spmd(nc, in_maps, core_ids=list(range(NCORES)))
    out = np.concatenate([res.results[c]["outd"] for c in range(NCORES)], axis=0)
    return out.reshape(4, SEQ, D)
